# version 04
# baseline (speedup 1.0000x reference)
import contextlib
import math
import numpy as np
import ml_dtypes

import concourse.bass as bass
import concourse.mybir as mybir
from concourse.bass_utils import run_bass_kernel_spmd

BFD = ml_dtypes.bfloat16
F32 = mybir.dt.float32
BF16 = mybir.dt.bfloat16
AF = mybir.ActivationFunctionType
OP = mybir.AluOpType
AX = mybir.AxisListType

D = 2048
L = 4096
TOWN = 2048
NF = 2176
NFT = 17
PAD = 512
DI = 8192

_DEBUG_TAPS = ()
_UPTO = 99


class Ctr:
    def __init__(self, sem):
        self.sem = sem
        self.n = 0

    def inc(self, inst, k=1):
        inst.then_inc(self.sem, k)
        self.n += k
        return inst

    def dma(self, inst):
        return self.inc(inst, 16)


def _hadamard(n):
    H = np.array([[1.0]])
    while H.shape[0] < n:
        H = np.block([[H, H], [H, -H]])
    return H


def _bf(x):
    return np.ascontiguousarray(np.asarray(x, dtype=np.float32).astype(BFD))


def host_constants():
    C = {}
    C["h128"] = _bf(_hadamard(128))
    C["w2a"] = _bf(np.kron(_hadamard(16), np.eye(8)))
    C["w2b"] = _bf(np.kron(_hadamard(64), np.eye(2)))
    C["identb"] = _bf(np.eye(128))
    C["identf"] = np.ascontiguousarray(np.eye(128, dtype=np.float32))
    t = np.arange(2048, dtype=np.float64)[:, None]
    f = np.arange(NF, dtype=np.float64)[None, :]
    ang = 2.0 * np.pi * t * f / 4096.0
    C["fwdr"] = _bf(np.cos(ang))
    C["fwdi"] = _bf(-np.sin(ang))
    k = np.arange(NF, dtype=np.float64)[:, None]
    tt = np.arange(2048, dtype=np.float64)[None, :]
    wk = np.where(k == 0, 1.0, 2.0)
    wk = np.where(k == 2048, 1.0, wk)
    wk = np.where(k > 2048, 0.0, wk) / 4096.0
    angi = 2.0 * np.pi * tt * k / 4096.0
    C["invr"] = _bf(wk * np.cos(angi))
    C["invi"] = _bf(-wk * np.sin(angi))
    C["sgn"] = np.ascontiguousarray(((-1.0) ** np.arange(128)).astype(np.float32)[:, None])
    return C


def _perm_fwht(n_outer, n_inner, offcnt):
    gcnt = n_inner // offcnt
    perm = np.empty(n_outer * n_inner, dtype=np.int64)
    for g in range(gcnt):
        for ho in range(n_outer):
            for off in range(offcnt):
                perm[g * 128 + ho * offcnt + off] = ho * n_inner + g * offcnt + off
    return perm


PERM1 = _perm_fwht(16, 128, 8)
PERM2 = _perm_fwht(64, 128, 2)



def _bc(t, pitch, col, n):
    return bass.AP(t, col, [[pitch, 128], [0, n]])

def build_program(upto=None):
    _scope_state = []
    def _PH(name):
        if _scope_state:
            pn, sid = _scope_state.pop()
            nc.leave_named_scope(pn, sid, False)
        if name:
            sid, _ = nc.enter_named_scope(name, False)
            _scope_state.append((name, sid))

    if upto is None:
        upto = _UPTO
    nc = bass.Bass("TRN2", target_bir_lowering=False)

    def scratch(name, shape, dt):
        if name in _DEBUG_TAPS:
            return nc.dram_tensor(name, shape, dt, kind="ExternalOutput")
        return nc.dram_tensor(name, shape, dt)

    x_own = nc.dram_tensor("x_own", [TOWN, D], F32, kind="ExternalInput")
    x_prev = nc.dram_tensor("x_prev", [TOWN, D], F32, kind="ExternalInput")
    alpha_d = nc.dram_tensor("alpha", [128, 1], F32, kind="ExternalInput")
    wqT = nc.dram_tensor("wqT", [D, D], F32, kind="ExternalInput")
    wkT = nc.dram_tensor("wkT", [D, D], F32, kind="ExternalInput")
    wvT = nc.dram_tensor("wvT", [D, D], F32, kind="ExternalInput")
    woT = nc.dram_tensor("woT", [D, D], F32, kind="ExternalInput")
    chvec = nc.dram_tensor("chvec", [D, 8], F32, kind="ExternalInput")
    cwvec = nc.dram_tensor("cwvec", [D, 8], F32, kind="ExternalInput")
    ssmk = nc.dram_tensor("ssmk", [D, D], F32, kind="ExternalInput")
    wgT = nc.dram_tensor("wgT", [D, DI], F32, kind="ExternalInput")
    wuT = nc.dram_tensor("wuT", [D, DI], F32, kind="ExternalInput")
    wdT = nc.dram_tensor("wdT", [DI, D], F32, kind="ExternalInput")
    cst = {}
    for nm, shape, dt in [("h128", [128, 128], BF16), ("w2a", [128, 128], BF16),
                          ("w2b", [128, 128], BF16), ("identb", [128, 128], BF16),
                          ("identf", [128, 128], F32),
                          ("fwdr", [2048, NF], BF16), ("fwdi", [2048, NF], BF16),
                          ("invr", [NF, 2048], BF16), ("invi", [NF, 2048], BF16),
                          ("sgn", [128, 1], F32)]:
        cst[nm] = nc.dram_tensor(nm, shape, dt, kind="ExternalInput")
    out_d = nc.dram_tensor("out", [TOWN, D], F32, kind="ExternalOutput")

    xn_h = scratch("xn_h", [PAD + L, D], BF16)
    k_h = scratch("k_h", [D, L], BF16)
    v_h = scratch("v_h", [D, L], BF16)
    q_h = scratch("q_h", [D, 2560], BF16)
    kv_h = scratch("kv_h", [D, L], BF16)
    qc_h = scratch("qc_h", [D, TOWN], BF16)
    ssmb_h = scratch("ssmb_h", [D, D], BF16)
    y_h = scratch("y_h", [TOWN, D], BF16)
    attn_h = scratch("attn_h", [D, TOWN], BF16)
    s1a_h = scratch("s1a_h", [D, TOWN], BF16)
    xq1t_h = scratch("xq1t_h", [TOWN, D], BF16)
    z_h = scratch("z_h", [DI, TOWN], BF16)
    s2a_h = scratch("s2a_h", [DI, TOWN], BF16)
    xq2t_h = scratch("xq2t_h", [TOWN, DI], BF16)
    wdq_h = scratch("wdq_h", [DI, D], BF16)
    scr_h = scratch("scr_h", [1, 16], F32)

    es = contextlib.ExitStack()
    sb = lambda nm, sh, dt: es.enter_context(nc.sbuf_tensor(nm, sh, dt))
    psm = lambda nm, sh, dt: es.enter_context(nc.psum_tensor(nm, sh, dt))
    mksem = lambda nm: Ctr(es.enter_context(nc.semaphore(nm)))

    cA = sb("cA", [128, 8192], BF16)
    cB = sb("cB", [128, 8192], BF16)
    cE = sb("cE", [128, 8192], BF16)
    cF = sb("cF", [128, 8192], BF16)
    junk = sb("junkb", [128, 2048], BF16)
    zpad = sb("zpad", [128, 2048], BF16)
    hm = {nm: sb("s_" + nm, [128, 128], BF16)
          for nm in ("h128", "w2a", "w2b", "identb")}
    identf_s = sb("s_identf", [128, 128], F32)
    chv_s = sb("chv_s", [128, 128], F32)
    cw_s = sb("cw_s", [128, 128], F32)
    alpha_s = sb("alpha_s", [128, 1], F32)
    sgn_s = sb("sgn_s", [128, 1], F32)
    asg_s = sb("asg_s", [128, 1], F32)
    st_s = sb("st_s", [128, 128], F32)
    wsc = sb("wsc_s", [128, 16], F32)

    pf = [psm(f"pf{i}", [128, 512], F32) for i in range(6)]
    pb = psm("pb", [128, 1024], BF16)
    pb2 = psm("pb2", [128, 1024], BF16)

    s_ld = mksem("s_ld")
    s_l0 = mksem("s_l0")
    s_l1 = mksem("s_l1")
    s_lx = mksem("s_lx")
    s_gw0 = mksem("s_gw0")
    s_gw1 = mksem("s_gw1")
    s_gs = mksem("s_gs")
    s_gd = mksem("s_gd")
    s_mm = mksem("s_mm")
    s_ev = mksem("s_ev")
    s_ac = mksem("s_ac")
    s_fr = mksem("s_fr")
    s_vh = mksem("s_vh")
    s_pc = mksem("s_pc")

    def store(g, dst_ap, src_ap):
        s_gs.dma(g.dma_start(dst_ap, src_ap))
        g.wait_ge(s_gs.sem, s_gs.n)

    def vhs(v, inst):
        s_vh.inc(v.drain())
        v.wait_ge(s_vh.sem, s_vh.n)

    with nc.Block() as blk:
        @blk.sync
        def _(sync):
            for nm in ("h128", "w2a", "w2b", "identb"):
                s_lx.dma(sync.dma_start(hm[nm][:], cst[nm].ap()))
            s_lx.dma(sync.dma_start(identf_s[:], cst["identf"].ap()))
            s_lx.dma(sync.dma_start(alpha_s[:], alpha_d.ap()))
            s_lx.dma(sync.dma_start(sgn_s[:], cst["sgn"].ap()))
            s_lx.dma(sync.dma_start(
                chv_s[:], bass.AP(chvec, 0, [[8, 128], [1024, 16], [1, 8]])))
            s_lx.dma(sync.dma_start(
                cw_s[:], bass.AP(cwvec, 0, [[8, 128], [1024, 16], [1, 8]])))

        @blk.vector
        def _(v):
            v.memset(st_s[:], 0.0)
            v.memset(wsc[:], 0.0)
            v.drain()
            v.wait_ge(s_lx.sem, s_lx.n)
            v.tensor_mul(asg_s[:], alpha_s[:], sgn_s[:])

        @blk.gpsimd
        def _(g):
            g.memset(junk[:], 0.0)
            g.memset(zpad[:], 0.0)

    esA = contextlib.ExitStack()
    sbA = lambda nm, sh, dt: esA.enter_context(nc.sbuf_tensor(nm, sh, dt))
    cC = sbA("cCa", [128, 8192], F32)
    cD = sbA("cDa", [128, 8192], F32)
    bigR = sbA("bigRa", [128, 32768], BF16)

    _PH("P1")
    if upto >= 1:
        with nc.Block() as blk:
            ev0, ac0, fr0 = s_ev.n, s_ac.n, s_fr.n
            l_base = [s_l0.n, s_l1.n]
            gs0 = s_gs.n

            @blk.sync
            def _(sync):
                for r in range(PAD // 128):
                    s_lx.dma(sync.dma_start(
                        bass.AP(xn_h, r * 128 * D, [[D, 128], [1, D]]), zpad[:]))
                for i in range(32):
                    src = x_prev if i < 16 else x_own
                    if i >= 2:
                        sync.wait_ge(s_fr.sem, fr0 + i - 1)
                    sem = s_l0 if i % 2 == 0 else s_l1
                    sem.dma(sync.dma_start(
                        cC[:, (i % 2) * 2048:(i % 2) * 2048 + 2048],
                        bass.AP(src, (i % 16) * 128 * D, [[D, 128], [1, D]])))
                sync.wait_ge(s_lx.sem, s_lx.n)

            @blk.vector
            def _(v):
                for i in range(32):
                    xt = cC[:, (i % 2) * 2048:(i % 2) * 2048 + 2048]
                    dt_ = cD[:, (i % 2) * 2048:(i % 2) * 2048 + 2048]
                    sem = s_l0 if i % 2 == 0 else s_l1
                    v.wait_ge(sem.sem, l_base[i % 2] + 16 * (i // 2 + 1))
                    vhs(v, v.reduce_sum(st_s[:, 0:1], xt, axis=AX.X))
                    vhs(v, v.tensor_single_scalar(st_s[:, 1:2], st_s[:, 0:1],
                                                  1.0 / 2048.0, OP.mult))
                    v.tensor_tensor(dt_, xt, _bc(st_s, 128, 1, 2048), OP.subtract)
                    v.drain()
                    v.tensor_mul(xt, dt_, dt_)
                    v.drain()
                    vhs(v, v.reduce_sum(st_s[:, 2:3], xt, axis=AX.X))
                    vhs(v, v.tensor_scalar(st_s[:, 3:4], st_s[:, 2:3], 1.0 / 2048.0,
                                           1e-5, OP.mult, OP.add))
                    vhs(v, v.reciprocal(st_s[:, 5:6], st_s[:, 3:4]))
                    for _ in range(4):
                        vhs(v, v.tensor_mul(st_s[:, 6:7], st_s[:, 5:6], st_s[:, 5:6]))
                        vhs(v, v.tensor_mul(st_s[:, 7:8], st_s[:, 3:4], st_s[:, 6:7]))
                        vhs(v, v.tensor_scalar(st_s[:, 8:9], st_s[:, 7:8], -0.5, 1.5,
                                               OP.mult, OP.add))
                        vhs(v, v.tensor_mul(st_s[:, 5:6], st_s[:, 5:6], st_s[:, 8:9]))
                    xo = cE[:, (i % 2) * 2048:(i % 2) * 2048 + 2048]
                    if i >= 2:
                        v.wait_ge(s_gs.sem, gs0 + 16 * (i - 1))
                    v.tensor_tensor(xo, dt_, _bc(st_s, 128, 5, 2048), OP.mult)
                    s_ev.inc(v.drain())
                    s_fr.inc(v.drain())

            @blk.gpsimd
            def _(g):
                for i in range(32):
                    g.wait_ge(s_ev.sem, ev0 + i + 1)
                    store(g, bass.AP(xn_h, (PAD + i * 128) * D, [[D, 128], [1, D]]),
                          cE[:, (i % 2) * 2048:(i % 2) * 2048 + 2048])

    _PH("P1b")
    if upto >= 2:
        with nc.Block() as blk:
            ac0 = s_ac.n
            l_base = [s_l0.n, s_l1.n]
            chunks = [(mi, m, ci) for mi, m in enumerate((wgT, wuT, wdT))
                      for ci in range(64)]

            @blk.sync
            def _(sync):
                for idx, (mi, m, ci) in enumerate(chunks):
                    if idx >= 2:
                        sync.wait_ge(s_ac.sem, ac0 + idx - 1)
                    sem = s_l0 if idx % 2 == 0 else s_l1
                    sem.dma(sync.dma_start(
                        cD[:, (idx % 2) * 2048:(idx % 2) * 2048 + 2048],
                        bass.AP(m, ci * 128 * 2048, [[2048, 128], [1, 2048]])))

            @blk.scalar
            def _(s):
                for idx, (mi, m, ci) in enumerate(chunks):
                    sem = s_l0 if idx % 2 == 0 else s_l1
                    s.wait_ge(sem.sem, l_base[idx % 2] + 16 * (idx // 2 + 1))
                    s.activation(junk[:, 0:2048],
                                 cD[:, (idx % 2) * 2048:(idx % 2) * 2048 + 2048],
                                 AF.Abs, accum_out=cC[:, idx:idx + 1])
                    s_ac.inc(s.drain())

            @blk.vector
            def _(v):
                v.wait_ge(s_ac.sem, ac0 + len(chunks))
                for mi in range(3):
                    v.reduce_sum(st_s[:, mi * 32:mi * 32 + 1], cC[:, mi * 64:mi * 64 + 64],
                                 axis=AX.X)

        with nc.Block() as blk:
            @blk.tensor
            def _(te):
                te.transpose(bass.AP(pf[0], 0, [[512, 128], [1, 128]]),
                             st_s[:, 0:128], identf_s[:])

        with nc.Block() as blk:
            ev0 = s_ev.n

            @blk.vector
            def _(v):
                v.tensor_copy(cC[:, 0:128], pf[0][:, 0:128])
                v.drain()
                vhs(v, v.reduce_sum(st_s[:, 13:14], cC[:, 0:128], axis=AX.X))
                s_ev.inc(v.drain())

            @blk.gpsimd
            def _(g):
                g.wait_ge(s_ev.sem, ev0 + 1)
                for mi in range(3):
                    store(g, bass.AP(scr_h, mi, [[1, 1], [1, 1]]),
                          st_s[mi * 32:mi * 32 + 1, 13:14])

        with nc.Block() as blk:
            @blk.sync
            def _(sync):
                for mi in range(3):
                    s_lx.dma(sync.dma_start(wsc[:, mi:mi + 1],
                                            bass.AP(scr_h, mi, [[0, 128], [1, 1]])))

            @blk.vector
            def _(v):
                v.wait_ge(s_lx.sem, s_lx.n)
                for mi in range(3):
                    vhs(v, v.tensor_scalar(wsc[:, 3 + mi:4 + mi], wsc[:, mi:mi + 1],
                                           1.0 / (D * DI), 1e-5, OP.mult, OP.max))
                    vhs(v, v.tensor_single_scalar(wsc[:, 6 + mi:7 + mi],
                                                  wsc[:, 3 + mi:4 + mi], 0.5, OP.mult))
                    vhs(v, v.tensor_single_scalar(wsc[:, 9 + mi:10 + mi],
                                                  wsc[:, 6 + mi:7 + mi], -1.0, OP.mult))

    def proj_pass(wT, dst_h, nchunk, row_base, bias_col):
        with nc.Block() as blk:
            mm0, ev0 = s_mm.n, s_ev.n
            l_base = [s_l0.n, s_l1.n]
            gw0 = s_gw0.n
            gs0 = s_gs.n

            @blk.gpsimd
            def _(g):
                for o in range(16):
                    s_gw0.dma(g.dma_start(
                        bass.AP(bigR, o * 2048, [[32768, 128], [128, 16], [1, 128]]),
                        bass.AP(wT, o * 128, [[D, 128], [128 * D, 16], [1, 128]])))
                for ci in range(nchunk):
                    for o in range(16):
                        idx = ci * 16 + o
                        g.wait_ge(s_ev.sem, ev0 + idx + 1)
                        store(g, bass.AP(dst_h, o * 128 * (nchunk * 512) + ci * 512,
                                         [[nchunk * 512, 128], [1, 512]]),
                              cE[:, (idx % 8) * 512:(idx % 8) * 512 + 512])

            @blk.sync
            def _(sync):
                for ci in range(nchunk):
                    slot = cB if ci % 2 == 0 else cF
                    if ci >= 2:
                        sync.wait_ge(s_mm.sem, mm0 + (ci - 1) * 16)
                    sem = s_l0 if ci % 2 == 0 else s_l1
                    for ht in range(16):
                        sem.dma(sync.dma_start_transpose(
                            slot[:, ht * 512:ht * 512 + 512],
                            bass.AP(xn_h, (row_base + ci * 512) * D + ht * 128,
                                    [[D, 512], [1, 128]])))

            @blk.tensor
            def _(te):
                te.wait_ge(s_gw0.sem, s_gw0.n)
                for ci in range(nchunk):
                    slot = cB if ci % 2 == 0 else cF
                    sem = s_l0 if ci % 2 == 0 else s_l1
                    te.wait_ge(sem.sem, l_base[ci % 2] + 256 * (ci // 2 + 1))
                    for o in range(16):
                        idx = ci * 16 + o
                        if idx >= 4:
                            te.wait_ge(s_ev.sem, ev0 + idx - 3)
                        bank = pf[o % 4]
                        for ht in range(16):
                            te.matmul(bank[:, 0:512],
                                      bigR[:, (o * 16 + ht) * 128:(o * 16 + ht) * 128 + 128],
                                      slot[:, ht * 512:ht * 512 + 512],
                                      start=(ht == 0), stop=(ht == 15))
                        s_mm.inc(te.drain())

            @blk.vector
            def _(v):
                for ci in range(nchunk):
                    for o in range(16):
                        idx = ci * 16 + o
                        v.wait_ge(s_mm.sem, mm0 + idx + 1)
                        if idx >= 8:
                            v.wait_ge(s_gs.sem, gs0 + 16 * (idx - 7))
                        v.tensor_tensor(
                            cE[:, (idx % 8) * 512:(idx % 8) * 512 + 512],
                            pf[o % 4][:, 0:512],
                            _bc(chv_s, 128, o * 8 + bias_col, 512), OP.add)
                        s_ev.inc(v.drain())

    _PH("P2k")
    if upto >= 3:
        proj_pass(wkT, k_h, 8, PAD, 1)
    _PH("P2v")
    if upto >= 4:
        proj_pass(wvT, v_h, 8, PAD, 2)
    _PH("P2q")
    if upto >= 5:
        proj_pass(wqT, q_h, 5, PAD + 1536, 0)

    _PH("P2kv")
    if upto >= 6:
        with nc.Block() as blk:
            ev0 = s_ev.n
            l_base = [s_l0.n, s_l1.n]
            gs0 = s_gs.n

            @blk.sync
            def _(sync):
                for o in range(16):
                    if o >= 2:
                        sync.wait_ge(s_ev.sem, ev0 + o - 1)
                    sem = s_l0 if o % 2 == 0 else s_l1
                    sem.dma(sync.dma_start(
                        cA[:, (o % 2) * 4096:(o % 2) * 4096 + 4096],
                        bass.AP(k_h, o * 128 * L, [[L, 128], [1, L]])))
                    sem.dma(sync.dma_start(
                        cB[:, (o % 2) * 4096:(o % 2) * 4096 + 4096],
                        bass.AP(v_h, o * 128 * L, [[L, 128], [1, L]])))

            @blk.vector
            def _(v):
                for o in range(16):
                    sem = s_l0 if o % 2 == 0 else s_l1
                    v.wait_ge(sem.sem, l_base[o % 2] + 32 * (o // 2 + 1))
                    ks = cA[:, (o % 2) * 4096:(o % 2) * 4096 + 4096]
                    vs = cB[:, (o % 2) * 4096:(o % 2) * 4096 + 4096]
                    km = cA[:, (o % 2) * 4096 + 2046:(o % 2) * 4096 + 2048]
                    v.tensor_tensor(km, km, _bc(alpha_s, 1, 0, 2), OP.mult)
                    v.drain()
                    v.memset(cC[:, 0:2], 0.0)
                    v.tensor_tensor(cC[:, 2:4098], ks, _bc(cw_s, 128, o * 8 + 5, 4096), OP.mult)
                    v.drain()
                    v.tensor_tensor(cC[:, 2:4098], cC[:, 2:4098],
                                    _bc(chv_s, 128, o * 8 + 5, 4096), OP.add)
                    v.drain()
                    v.tensor_tensor(cD[:, 0:4095], ks[:, 0:4095],
                                    _bc(cw_s, 128, o * 8 + 4, 4095), OP.mult)
                    v.drain()
                    v.tensor_tensor(cC[:, 3:4098], cC[:, 3:4098], cD[:, 0:4095], OP.add)
                    v.drain()
                    v.tensor_tensor(cD[:, 0:4094], ks[:, 0:4094],
                                    _bc(cw_s, 128, o * 8 + 3, 4094), OP.mult)
                    v.drain()
                    v.tensor_tensor(cC[:, 4:4098], cC[:, 4:4098], cD[:, 0:4094], OP.add)
                    v.drain()
                    if o >= 2:
                        v.wait_ge(s_gs.sem, gs0 + 16 * (o - 1))
                    v.tensor_mul(cE[:, (o % 2) * 4096:(o % 2) * 4096 + 4096],
                                 cC[:, 2:4098], vs)
                    s_ev.inc(v.drain())

            @blk.gpsimd
            def _(g):
                for o in range(16):
                    g.wait_ge(s_ev.sem, ev0 + o + 1)
                    store(g, bass.AP(kv_h, o * 128 * L, [[L, 128], [1, L]]),
                          cE[:, (o % 2) * 4096:(o % 2) * 4096 + 4096])

    _PH("P2qc")
    if upto >= 7:
        with nc.Block() as blk:
            ev0 = s_ev.n
            l_base = [s_l0.n, s_l1.n]
            gs0 = s_gs.n

            @blk.sync
            def _(sync):
                for o in range(16):
                    if o >= 2:
                        sync.wait_ge(s_ev.sem, ev0 + o - 1)
                    sem = s_l0 if o % 2 == 0 else s_l1
                    sem.dma(sync.dma_start(
                        cA[:, (o % 2) * 2560:(o % 2) * 2560 + 2560],
                        bass.AP(q_h, o * 128 * 2560, [[2560, 128], [1, 2560]])))

            @blk.vector
            def _(v):
                for o in range(16):
                    sem = s_l0 if o % 2 == 0 else s_l1
                    v.wait_ge(sem.sem, l_base[o % 2] + 16 * (o // 2 + 1))
                    qs = cA[:, (o % 2) * 2560:(o % 2) * 2560 + 2560]
                    qm = cA[:, (o % 2) * 2560 + 510:(o % 2) * 2560 + 512]
                    v.tensor_tensor(qm, qm, _bc(alpha_s, 1, 0, 2), OP.mult)
                    v.drain()
                    v.tensor_tensor(cC[:, 0:2048], qs[:, 512:2560],
                                    _bc(cw_s, 128, o * 8 + 2, 2048), OP.mult)
                    v.drain()
                    v.tensor_tensor(cC[:, 0:2048], cC[:, 0:2048],
                                    _bc(chv_s, 128, o * 8 + 4, 2048), OP.add)
                    v.drain()
                    v.tensor_tensor(cD[:, 0:2048], qs[:, 511:2559],
                                    _bc(cw_s, 128, o * 8 + 1, 2048), OP.mult)
                    v.drain()
                    v.tensor_tensor(cC[:, 0:2048], cC[:, 0:2048], cD[:, 0:2048], OP.add)
                    v.drain()
                    v.tensor_tensor(cD[:, 0:2048], qs[:, 510:2558],
                                    _bc(cw_s, 128, o * 8 + 0, 2048), OP.mult)
                    v.drain()
                    v.tensor_tensor(cC[:, 0:2048], cC[:, 0:2048], cD[:, 0:2048], OP.add)
                    v.drain()
                    if o >= 2:
                        v.wait_ge(s_gs.sem, gs0 + 16 * (o - 1))
                    v.tensor_copy(cE[:, (o % 2) * 2048:(o % 2) * 2048 + 2048],
                                  cC[:, 0:2048])
                    s_ev.inc(v.drain())

            @blk.gpsimd
            def _(g):
                for o in range(16):
                    g.wait_ge(s_ev.sem, ev0 + o + 1)
                    store(g, bass.AP(qc_h, o * 128 * TOWN, [[TOWN, 128], [1, TOWN]]),
                          cE[:, (o % 2) * 2048:(o % 2) * 2048 + 2048])

    _PH("P2ssm")
    if upto >= 8:
        with nc.Block() as blk:
            @blk.gpsimd
            def _(g):
                for o in range(16):
                    s_gw0.dma(g.dma_start(
                        cF[:, (o % 2) * 2048:(o % 2) * 2048 + 2048],
                        bass.AP(ssmk, o * 128 * D, [[D, 128], [1, D]])))
                    g.wait_ge(s_gw0.sem, s_gw0.n)
                    store(g, bass.AP(ssmb_h, o * 128 * D, [[D, 128], [1, D]]),
                          cF[:, (o % 2) * 2048:(o % 2) * 2048 + 2048])

    esA.close()

    _PH("P3conv")
    if upto >= 9:
        esB = contextlib.ExitStack()
        sbB = lambda nm, sh, dt: esB.enter_context(nc.sbuf_tensor(nm, sh, dt))
        vcv = sbB("vcv", [128, 24576], BF16)
        ybf = sbB("ybf", [128, 17408], BF16)
        cmA = sbB("cmA", [128, 4352], BF16)
        cmB = sbB("cmB", [128, 4352], BF16)
        w32 = sbB("w32", [128, 2048], F32)

        for cc in range(4):
            with nc.Block() as blk:
                mm0, ev0 = s_mm.n, s_ev.n
                l_base = [s_l0.n, s_l1.n]
                lx_mark = s_lx.n + 48 * 16

                @blk.sync
                def _(sync):
                    for seg in range(3):
                        src = [kv_h, kv_h, ssmb_h][seg]
                        roww = L if seg < 2 else D
                        tbase = [2048, 0, 0][seg]
                        for tt in range(16):
                            s_lx.dma(sync.dma_start_transpose(
                                vcv[:, (seg * 16 + tt) * 512:(seg * 16 + tt) * 512 + 512],
                                bass.AP(src, cc * 512 * roww + tbase + tt * 128,
                                        [[roww, 512], [1, 128]])))
                    for ft in range(NFT):
                        cm = cmA if ft % 2 == 0 else cmB
                        sem = s_l0 if ft % 2 == 0 else s_l1
                        if ft >= 2:
                            sync.wait_ge(s_mm.sem, mm0 + ft - 1)
                        sem.dma(sync.dma_start(
                            cm[:, 0:2048],
                            bass.AP(cst["fwdr"], ft * 128,
                                    [[NF, 128], [128 * NF, 16], [1, 128]])))
                        sem.dma(sync.dma_start(
                            cm[:, 2048:4096],
                            bass.AP(cst["fwdi"], ft * 128,
                                    [[NF, 128], [128 * NF, 16], [1, 128]])))

                @blk.tensor
                def _(te):
                    te.wait_ge(s_lx.sem, lx_mark)
                    for ft in range(NFT):
                        cm = cmA if ft % 2 == 0 else cmB
                        sem = s_l0 if ft % 2 == 0 else s_l1
                        te.wait_ge(sem.sem, l_base[ft % 2] + 32 * (ft // 2 + 1))
                        if ft >= 1:
                            te.wait_ge(s_ev.sem, ev0 + ft)
                        for tt in range(16):
                            for mat in range(2):
                                for seg in range(3):
                                    te.matmul(
                                        pf[seg * 2 + mat][:, 0:512],
                                        cm[:, mat * 2048 + tt * 128:mat * 2048 + tt * 128 + 128],
                                        vcv[:, (seg * 16 + tt) * 512:(seg * 16 + tt) * 512 + 512],
                                        start=(tt == 0), stop=(tt == 15))
                        s_mm.inc(te.drain())

                @blk.vector
                def _(v):
                    for ft in range(NFT):
                        v.wait_ge(s_mm.sem, mm0 + ft + 1)
                        v.tensor_tensor(w32[:, 0:512], pf[2][:, 0:512],
                                        _bc(asg_s, 1, 0, 512), OP.mult)
                        v.drain()
                        v.tensor_tensor(w32[:, 0:512], w32[:, 0:512],
                                        pf[0][:, 0:512], OP.add)
                        v.tensor_tensor(w32[:, 512:1024], pf[3][:, 0:512],
                                        _bc(asg_s, 1, 0, 512), OP.mult)
                        v.drain()
                        v.tensor_tensor(w32[:, 512:1024], w32[:, 512:1024],
                                        pf[1][:, 0:512], OP.add)
                        v.drain()
                        v.tensor_mul(w32[:, 1024:1536], w32[:, 0:512], pf[4][:, 0:512])
                        v.tensor_mul(w32[:, 1536:2048], w32[:, 512:1024], pf[5][:, 0:512])
                        v.drain()
                        v.tensor_sub(ybf[:, ft * 1024:ft * 1024 + 512],
                                     w32[:, 1024:1536], w32[:, 1536:2048])
                        v.drain()
                        v.tensor_mul(w32[:, 1024:1536], w32[:, 0:512], pf[5][:, 0:512])
                        v.tensor_mul(w32[:, 1536:2048], w32[:, 512:1024], pf[4][:, 0:512])
                        v.drain()
                        v.tensor_add(ybf[:, ft * 1024 + 512:ft * 1024 + 1024],
                                     w32[:, 1024:1536], w32[:, 1536:2048])
                        s_ev.inc(v.drain())

            with nc.Block() as blk:
                mm0, ev0 = s_mm.n, s_ev.n
                l_base = [s_l0.n, s_l1.n]
                gs0 = s_gs.n

                @blk.sync
                def _(sync):
                    for tt in range(16):
                        cm = cmA if tt % 2 == 0 else cmB
                        sem = s_l0 if tt % 2 == 0 else s_l1
                        if tt >= 2:
                            sync.wait_ge(s_mm.sem, mm0 + tt - 1)
                        sem.dma(sync.dma_start(
                            cm[:, 0:2176],
                            bass.AP(cst["invr"], tt * 128,
                                    [[2048, 128], [128 * 2048, 17], [1, 128]])))
                        sem.dma(sync.dma_start(
                            cm[:, 2176:4352],
                            bass.AP(cst["invi"], tt * 128,
                                    [[2048, 128], [128 * 2048, 17], [1, 128]])))

                @blk.tensor
                def _(te):
                    for tt in range(16):
                        cm = cmA if tt % 2 == 0 else cmB
                        sem = s_l0 if tt % 2 == 0 else s_l1
                        te.wait_ge(sem.sem, l_base[tt % 2] + 32 * (tt // 2 + 1))
                        if tt >= 2:
                            te.wait_ge(s_ev.sem, ev0 + tt - 1)
                        bank = pf[tt % 2]
                        for ft in range(NFT):
                            te.matmul(bank[:, 0:512], cm[:, ft * 128:ft * 128 + 128],
                                      ybf[:, ft * 1024:ft * 1024 + 512],
                                      start=(ft == 0), stop=False)
                            te.matmul(bank[:, 0:512],
                                      cm[:, 2176 + ft * 128:2176 + ft * 128 + 128],
                                      ybf[:, ft * 1024 + 512:ft * 1024 + 1024],
                                      start=False, stop=(ft == NFT - 1))
                        s_mm.inc(te.drain())

                @blk.vector
                def _(v):
                    for tt in range(16):
                        v.wait_ge(s_mm.sem, mm0 + tt + 1)
                        if tt >= 2:
                            v.wait_ge(s_gs.sem, gs0 + 16 * (tt - 1))
                        v.tensor_copy(cE[:, (tt % 2) * 512:(tt % 2) * 512 + 512],
                                      pf[tt % 2][:, 0:512])
                        s_ev.inc(v.drain())

                @blk.gpsimd
                def _(g):
                    for tt in range(16):
                        g.wait_ge(s_ev.sem, ev0 + tt + 1)
                        store(g, bass.AP(y_h, tt * 128 * D + cc * 512,
                                         [[D, 128], [1, 512]]),
                              cE[:, (tt % 2) * 512:(tt % 2) * 512 + 512])

        esB.close()

    esC = contextlib.ExitStack()
    sbC = lambda nm, sh, dt: esC.enter_context(nc.sbuf_tensor(nm, sh, dt))
    cC = sbC("cCc", [128, 8192], F32)
    cD = sbC("cDc", [128, 8192], F32)
    bigR = sbC("bigRc", [128, 32768], BF16)

    _PH("P4")
    if upto >= 10:
        with nc.Block() as blk:
            ev0 = s_ev.n
            l_base = [s_l0.n, s_l1.n]

            @blk.sync
            def _(sync):
                for o in range(16):
                    if o >= 2:
                        sync.wait_ge(s_ev.sem, ev0 + o - 1)
                    sem = s_l0 if o % 2 == 0 else s_l1
                    sem.dma(sync.dma_start_transpose(
                        cA[:, (o % 2) * 2048:(o % 2) * 2048 + 2048],
                        bass.AP(y_h, o * 128, [[D, 2048], [1, 128]])))
                    sem.dma(sync.dma_start(
                        cB[:, (o % 2) * 2048:(o % 2) * 2048 + 2048],
                        bass.AP(kv_h, o * 128 * L + 2048, [[L, 128], [1, 2048]])))
                    sem.dma(sync.dma_start(
                        cF[:, (o % 2) * 2048:(o % 2) * 2048 + 2048],
                        bass.AP(qc_h, o * 128 * TOWN, [[TOWN, 128], [1, TOWN]])))

            @blk.vector
            def _(v):
                for o in range(16):
                    sem = s_l0 if o % 2 == 0 else s_l1
                    v.wait_ge(sem.sem, l_base[o % 2] + 48 * (o // 2 + 1))
                    ys = cA[:, (o % 2) * 2048:(o % 2) * 2048 + 2048]
                    kvs = cB[:, (o % 2) * 2048:(o % 2) * 2048 + 2048]
                    qcs = cF[:, (o % 2) * 2048:(o % 2) * 2048 + 2048]
                    v.tensor_tensor(cC[:, 0:2048], kvs,
                                    _bc(chv_s, 128, o * 8 + 6, 2048), OP.mult)
                    v.drain()
                    v.tensor_tensor(cC[:, 0:2048], cC[:, 0:2048], ys, OP.add)
                    v.drain()
                    v.tensor_mul(bigR[:, o * 2048:o * 2048 + 2048], cC[:, 0:2048], qcs)
                    s_ev.inc(v.drain())

        with nc.Block() as blk:
            mm0, ev0 = s_mm.n, s_ev.n
            gw_base = [s_gw0.n, s_gw1.n]
            gs0 = s_gs.n

            @blk.gpsimd
            def _(g):
                for o in range(16):
                    if o >= 2:
                        g.wait_ge(s_mm.sem, mm0 + (o - 1) * 4)
                    gsem = s_gw0 if o % 2 == 0 else s_gw1
                    gsem.dma(g.dma_start(
                        bass.AP(cA, (o % 2) * 2048, [[8192, 128], [128, 16], [1, 128]]),
                        bass.AP(woT, o * 128, [[D, 128], [128 * D, 16], [1, 128]])))
                    if o >= 1:
                        for ci in range(4):
                            idx = (o - 1) * 4 + ci
                            g.wait_ge(s_ev.sem, ev0 + idx + 1)
                            store(g, bass.AP(attn_h, (o - 1) * 128 * TOWN + ci * 512,
                                             [[TOWN, 128], [1, 512]]),
                                  cE[:, (idx % 8) * 512:(idx % 8) * 512 + 512])
                for ci in range(4):
                    idx = 15 * 4 + ci
                    g.wait_ge(s_ev.sem, ev0 + idx + 1)
                    store(g, bass.AP(attn_h, 15 * 128 * TOWN + ci * 512,
                                     [[TOWN, 128], [1, 512]]),
                          cE[:, (idx % 8) * 512:(idx % 8) * 512 + 512])

            @blk.tensor
            def _(te):
                for o in range(16):
                    gsem = s_gw0 if o % 2 == 0 else s_gw1
                    te.wait_ge(gsem.sem, gw_base[o % 2] + 16 * (o // 2 + 1))
                    for ci in range(4):
                        idx = o * 4 + ci
                        if idx >= 4:
                            te.wait_ge(s_ev.sem, ev0 + idx - 3)
                        bank = pf[ci % 4]
                        for ht in range(16):
                            te.matmul(
                                bank[:, 0:512],
                                cA[:, (o % 2) * 2048 + ht * 128:(o % 2) * 2048 + ht * 128 + 128],
                                bigR[:, ht * 2048 + ci * 512:ht * 2048 + ci * 512 + 512],
                                start=(ht == 0), stop=(ht == 15))
                        s_mm.inc(te.drain())

            @blk.vector
            def _(v):
                for o in range(16):
                    for ci in range(4):
                        idx = o * 4 + ci
                        v.wait_ge(s_mm.sem, mm0 + idx + 1)
                        if idx >= 8:
                            v.wait_ge(s_gs.sem, gs0 + 16 * (idx - 7))
                        v.tensor_tensor(
                            cE[:, (idx % 8) * 512:(idx % 8) * 512 + 512],
                            pf[ci % 4][:, 0:512],
                            _bc(chv_s, 128, o * 8 + 3, 512), OP.add)
                        s_ev.inc(v.drain())

    def stage2_quant(src_h, dst_h, ncols, n_outer, offcnt, w2_name, norm):
        gcnt = ncols // 128
        post = norm / 127.0
        for ti in range(16):
            with nc.Block() as blk:
                mm0, ev0 = s_mm.n, s_ev.n

                @blk.sync
                def _(sync):
                    s_lx.dma(sync.dma_start_transpose(
                        cB[:, 0:ncols],
                        bass.AP(src_h, ti * 128, [[TOWN, ncols], [1, 128]])))

                @blk.vector
                def _(v):
                    v.wait_ge(s_lx.sem, s_lx.n)
                    for g in range(gcnt):
                        v.tensor_copy(
                            bass.AP(cF, g * 128, [[8192, 128], [offcnt, n_outer], [1, offcnt]]),
                            bass.AP(cB, g * offcnt, [[8192, 128], [128, n_outer], [1, offcnt]]))
                        s_ev.inc(v.drain())
                    for g in range(gcnt):
                        v.wait_ge(s_mm.sem, mm0 + 2 * g + 1)
                        pbank = pb if g % 2 == 0 else pb2
                        v.tensor_copy(cA[:, g * 128:g * 128 + 128], pbank[:, 0:128])
                        s_ev.inc(v.drain())
                        v.wait_ge(s_mm.sem, mm0 + 2 * g + 2)
                        dst = cC if g < 32 else cD
                        gg = g if g < 32 else g - 32
                        v.tensor_copy(dst[:, gg * 128:gg * 128 + 128], pf[g % 3][:, 0:128])
                        s_ev.inc(v.drain())

                @blk.tensor
                def _(te):
                    for g in range(gcnt):
                        te.wait_ge(s_ev.sem, ev0 + g + 1)
                        if g >= 2:
                            te.wait_ge(s_ev.sem, ev0 + gcnt + 2 * (g - 2) + 1)
                        pbank = pb if g % 2 == 0 else pb2
                        te.transpose(pbank[:, 0:128],
                                     cF[:, g * 128:g * 128 + 128], hm["identb"][:])
                        s_mm.inc(te.drain())
                        te.wait_ge(s_ev.sem, ev0 + gcnt + 2 * g + 1)
                        if g >= 3:
                            te.wait_ge(s_ev.sem, ev0 + gcnt + 2 * (g - 3) + 2)
                        te.matmul(pf[g % 3][:, 0:128], cA[:, g * 128:g * 128 + 128],
                                  hm[w2_name][:], start=True, stop=True)
                        s_mm.inc(te.drain())

            with nc.Block() as blk:
                ev0, ac0, pc0 = s_ev.n, s_ac.n, s_pc.n
                half = ncols > 4096
                w1 = min(ncols, 4096)
                if half:
                    pA = (cC, 0, 4096, 4096, cF[:, 0:4096],
                          cE[:, 4096:8192], cE[:, 0:4096])
                    pB = (cD, 0, 4096, 4096, cF[:, 4096:8192],
                          bigR[:, 0:4096], bigR[:, 4096:8192])
                    stw = 4096
                else:
                    wA = w1 // 2
                    pA = (cC, 0, wA, 4096, cF[:, 0:wA],
                          cE[:, 4096:4096 + wA], cE[:, 0:wA])
                    pB = (cC, wA, w1 - wA, 4096 + wA, cF[:, wA:w1],
                          bigR[:, 0:w1 - wA], bigR[:, 4096:4096 + w1 - wA])
                    stw = wA
                marker = {}

                def round_part(e, part):
                    buf, c0, w, r0, sgn_sl, btmp, xqd = part
                    e.tensor_mul(buf[:, r0:r0 + w], buf[:, c0:c0 + w], sgn_sl)
                    e.drain()
                    e.tensor_copy(buf[:, c0:c0 + w], buf[:, r0:r0 + w])
                    e.drain()
                    for kv_ in (64.0, 32.0, 16.0, 8.0, 4.0, 2.0, 1.0):
                        e.tensor_scalar(btmp, buf[:, c0:c0 + w], kv_ - 0.5, kv_,
                                        OP.is_ge, OP.mult)
                        e.drain()
                        e.tensor_sub(buf[:, c0:c0 + w], buf[:, c0:c0 + w], btmp)
                        e.drain()
                    e.tensor_sub(buf[:, c0:c0 + w], buf[:, r0:r0 + w], buf[:, c0:c0 + w])
                    e.drain()
                    e.tensor_tensor(btmp, buf[:, c0:c0 + w], sgn_sl, OP.mult)
                    e.drain()
                    e.tensor_tensor(xqd, btmp, _bc(st_s, 128, 24, w), OP.mult)
                    e.drain()

                @blk.vector
                def _(v):
                    vhs(v, v.tensor_reduce(st_s[:, 20:21], cC[:, 0:w1], AX.X, OP.max,
                                           apply_absolute_value=True))
                    if half:
                        vhs(v, v.tensor_reduce(st_s[:, 25:26], cD[:, 0:ncols - 4096],
                                               AX.X, OP.max, apply_absolute_value=True))
                        vhs(v, v.tensor_max(st_s[:, 20:21], st_s[:, 20:21],
                                            st_s[:, 25:26]))
                    vhs(v, v.tensor_single_scalar(st_s[:, 21:22], st_s[:, 20:21],
                                                  1e-5, OP.max))
                    vhs(v, v.reciprocal(st_s[:, 22:23], st_s[:, 21:22]))
                    vhs(v, v.tensor_single_scalar(st_s[:, 23:24], st_s[:, 22:23],
                                                  127.0, OP.mult))
                    vhs(v, v.tensor_single_scalar(st_s[:, 24:25], st_s[:, 21:22],
                                                  post, OP.mult))
                    marker["stats"] = s_vh.n
                    bufA, cA0, wwA = pA[0], pA[1], pA[2]
                    v.tensor_tensor(bufA[:, cA0:cA0 + wwA], bufA[:, cA0:cA0 + wwA],
                                    _bc(st_s, 128, 23, wwA), OP.mult)
                    s_ev.inc(v.drain())
                    v.wait_ge(s_ac.sem, ac0 + 1)
                    round_part(v, pA)
                    s_ev.inc(v.drain())

                @blk.scalar
                def _(s):
                    s.wait_ge(s_ev.sem, ev0 + 1)
                    s.wait_ge(s_pc.sem, pc0 + 1)
                    s.activation(pA[4], pA[0][:, pA[1]:pA[1] + pA[2]], AF.Sign)
                    s.activation(pB[4], pB[0][:, pB[1]:pB[1] + pB[2]], AF.Sign)
                    s_ac.inc(s.drain())

                @blk.gpsimd
                def _(g):
                    g.wait_ge(s_vh.sem, marker["stats"])
                    bufB, cB0, wwB = pB[0], pB[1], pB[2]
                    g.tensor_tensor(bufB[:, cB0:cB0 + wwB], bufB[:, cB0:cB0 + wwB],
                                    _bc(st_s, 128, 23, wwB), OP.mult)
                    s_pc.inc(g.drain())
                    g.wait_ge(s_ac.sem, ac0 + 1)
                    round_part(g, pB)
                    g.wait_ge(s_ev.sem, ev0 + 2)
                    store(g, bass.AP(dst_h, ti * 128 * ncols, [[ncols, 128], [1, stw]]),
                          cE[:, 0:stw])
                    store(g, bass.AP(dst_h, ti * 128 * ncols + stw,
                                     [[ncols, 128], [1, pB[2]]]),
                          bigR[:, 4096:4096 + pB[2]])

    _PH("P5a")
    if upto >= 11:
        with nc.Block() as blk:
            mm0, ev0 = s_mm.n, s_ev.n
            l_base = [s_l0.n, s_l1.n]
            gs0 = s_gs.n

            @blk.sync
            def _(sync):
                for ho in range(16):
                    for ci in range(4):
                        idx = ho * 4 + ci
                        if idx >= 2:
                            sync.wait_ge(s_mm.sem, mm0 + idx - 1)
                        sem = s_l0 if idx % 2 == 0 else s_l1
                        sem.dma(sync.dma_start_transpose(
                            cB[:, (idx % 8) * 512:(idx % 8) * 512 + 512],
                            bass.AP(xn_h, (PAD + 2048 + ci * 512) * D + ho * 128,
                                    [[D, 512], [1, 128]])))

            @blk.tensor
            def _(te):
                for ho in range(16):
                    for ci in range(4):
                        idx = ho * 4 + ci
                        sem = s_l0 if idx % 2 == 0 else s_l1
                        te.wait_ge(sem.sem, l_base[idx % 2] + 16 * (idx // 2 + 1))
                        if idx >= 4:
                            te.wait_ge(s_ev.sem, ev0 + idx - 3)
                        te.matmul(pf[ci % 4][:, 0:512], hm["h128"][:],
                                  cB[:, (idx % 8) * 512:(idx % 8) * 512 + 512],
                                  start=True, stop=True)
                        s_mm.inc(te.drain())

            @blk.vector
            def _(v):
                for ho in range(16):
                    for ci in range(4):
                        idx = ho * 4 + ci
                        v.wait_ge(s_mm.sem, mm0 + idx + 1)
                        if ho >= 2:
                            v.wait_ge(s_gs.sem, gs0 + 16 * (ho - 1))
                        v.tensor_copy(cE[:, (idx % 8) * 512:(idx % 8) * 512 + 512],
                                      pf[ci % 4][:, 0:512])
                        s_ev.inc(v.drain())

            @blk.gpsimd
            def _(g):
                for ho in range(16):
                    g.wait_ge(s_ev.sem, ev0 + ho * 4 + 4)
                    store(g, bass.AP(s1a_h, ho * 128 * TOWN, [[TOWN, 128], [1, TOWN]]),
                          cE[:, (ho % 2) * 2048:(ho % 2) * 2048 + 2048])

        stage2_quant(s1a_h, xq1t_h, 2048, 16, 8, "w2a", 1.0 / math.sqrt(2048.0))

    _PH("P5b")
    if upto >= 12:
        with nc.Block() as blk:
            @blk.sync
            def _(sync):
                for ht in range(16):
                    for ci in range(4):
                        s_lx.dma(sync.dma_start_transpose(
                            bigR[:, (ht * 4 + ci) * 512:(ht * 4 + ci) * 512 + 512],
                            bass.AP(xq1t_h, ci * 512 * D + ht * 128, [[D, 512], [1, 128]])))
                sync.wait_ge(s_lx.sem, s_lx.n)

        with nc.Block() as blk:
            mm0, ev0, ac0, pc0 = s_mm.n, s_ev.n, s_ac.n, s_pc.n
            l_base = [s_l0.n, s_l1.n]
            gs0 = s_gs.n

            def _wsp(col):
                return bass.AP(wsc, col, [[16, 128], [0, 1]])

            def ternarize(e, o, half, mi, t0, pool=False):
                wsrc = cD[:, half + (o % 2) * 2048:half + (o % 2) * 2048 + 2048]
                wdst = cF[:, half + (o % 2) * 2048:half + (o % 2) * 2048 + 2048]
                if pool:
                    e.tensor_single_scalar(cC[:, t0:t0 + 2048], wsrc,
                                           _wsp(6 + mi), OP.is_gt)
                    e.tensor_single_scalar(cC[:, t0 + 2048:t0 + 4096], wsrc,
                                           _wsp(9 + mi), OP.is_lt)
                else:
                    e.tensor_tensor(cC[:, t0:t0 + 2048], wsrc,
                                    _bc(wsc, 16, 6 + mi, 2048), OP.is_gt)
                    e.tensor_tensor(cC[:, t0 + 2048:t0 + 4096], wsrc,
                                    _bc(wsc, 16, 9 + mi, 2048), OP.is_lt)
                e.drain()
                e.tensor_sub(cC[:, t0:t0 + 2048], cC[:, t0:t0 + 2048],
                             cC[:, t0 + 2048:t0 + 4096])
                e.drain()
                e.tensor_tensor(wdst, cC[:, t0:t0 + 2048],
                                _bc(wsc, 16, 3 + mi, 2048), OP.mult)
                e.drain()

            @blk.sync
            def _(sync):
                for o in range(64):
                    if o >= 2:
                        sync.wait_ge(s_ev.sem, ev0 + (o - 2) * 3 + 1)
                        sync.wait_ge(s_pc.sem, pc0 + o - 1)
                    sem = s_l0 if o % 2 == 0 else s_l1
                    sem.dma(sync.dma_start(
                        bass.AP(cD, (o % 2) * 2048, [[8192, 128], [128, 16], [1, 128]]),
                        bass.AP(wgT, o * 128, [[DI, 128], [128 * DI, 16], [1, 128]])))
                    sem.dma(sync.dma_start(
                        bass.AP(cD, 4096 + (o % 2) * 2048, [[8192, 128], [128, 16], [1, 128]]),
                        bass.AP(wuT, o * 128, [[DI, 128], [128 * DI, 16], [1, 128]])))

            @blk.gpsimd
            def _(g):
                for o in range(64):
                    if o >= 1:
                        g.wait_ge(s_ev.sem, ev0 + (o - 1) * 3 + 3)
                        store(g, bass.AP(z_h, (o - 1) * 128 * TOWN, [[TOWN, 128], [1, TOWN]]),
                              cE[:, 2048 + ((o - 1) % 2) * 2048:
                                   2048 + ((o - 1) % 2) * 2048 + 2048])
                    sem = s_l0 if o % 2 == 0 else s_l1
                    g.wait_ge(sem.sem, l_base[o % 2] + 32 * (o // 2 + 1))
                    if o >= 2:
                        g.wait_ge(s_mm.sem, mm0 + (o - 1) * 2)
                    ternarize(g, o, 4096, 1, 4096, pool=True)
                    s_pc.inc(g.drain())
                g.wait_ge(s_ev.sem, ev0 + 63 * 3 + 3)
                store(g, bass.AP(z_h, 63 * 128 * TOWN, [[TOWN, 128], [1, TOWN]]),
                      cE[:, 2048 + (63 % 2) * 2048:2048 + (63 % 2) * 2048 + 2048])

            @blk.vector
            def _(v):
                for o in range(64):
                    sem = s_l0 if o % 2 == 0 else s_l1
                    v.wait_ge(sem.sem, l_base[o % 2] + 32 * (o // 2 + 1))
                    if o >= 2:
                        v.wait_ge(s_mm.sem, mm0 + (o - 1) * 2)
                    ternarize(v, o, 0, 0, 0)
                    s_ev.inc(v.drain())
                    v.wait_ge(s_mm.sem, mm0 + o * 2 + 2)
                    v.wait_ge(s_ac.sem, ac0 + (o + 1) * 4)
                    if o >= 2:
                        v.wait_ge(s_gs.sem, gs0 + 16 * (o - 1))
                    for ci in range(4):
                        v.tensor_mul(cE[:, 2048 + (o % 2) * 2048 + ci * 512:
                                        2048 + (o % 2) * 2048 + ci * 512 + 512],
                                     cE[:, ci * 512:ci * 512 + 512], pf[ci][:, 0:512])
                    s_ev.inc(v.drain())
                    s_ev.inc(v.drain())

            @blk.tensor
            def _(te):
                for o in range(64):
                    te.wait_ge(s_ev.sem, ev0 + o * 3 + 1)
                    if o >= 1:
                        te.wait_ge(s_ev.sem, ev0 + (o - 1) * 3 + 2)
                    for ht in range(16):
                        lhs = cF[:, (o % 2) * 2048 + ht * 128:(o % 2) * 2048 + ht * 128 + 128]
                        for ci in range(4):
                            te.matmul(pf[ci][:, 0:512], lhs,
                                      bigR[:, (ht * 4 + ci) * 512:(ht * 4 + ci) * 512 + 512],
                                      start=(ht == 0), stop=(ht == 15))
                    s_mm.inc(te.drain())
                    te.wait_ge(s_ac.sem, ac0 + (o + 1) * 4)
                    te.wait_ge(s_pc.sem, pc0 + o + 1)
                    for ht in range(16):
                        lhs = cF[:, 4096 + (o % 2) * 2048 + ht * 128:
                                 4096 + (o % 2) * 2048 + ht * 128 + 128]
                        for ci in range(4):
                            te.matmul(pf[ci][:, 0:512], lhs,
                                      bigR[:, (ht * 4 + ci) * 512:(ht * 4 + ci) * 512 + 512],
                                      start=(ht == 0), stop=(ht == 15))
                    s_mm.inc(te.drain())

            @blk.scalar
            def _(s):
                for o in range(64):
                    s.wait_ge(s_mm.sem, mm0 + o * 2 + 1)
                    if o >= 1:
                        s.wait_ge(s_ev.sem, ev0 + (o - 1) * 3 + 2)
                    for ci in range(4):
                        s.activation(cE[:, ci * 512:ci * 512 + 512], pf[ci][:, 0:512],
                                     AF.Silu)
                    s_ac.inc(s.drain(), 4)

    _PH("P5c")
    if upto >= 13:
        with nc.Block() as blk:
            mm0, ev0 = s_mm.n, s_ev.n
            l_base = [s_l0.n, s_l1.n]
            gs0 = s_gs.n

            @blk.sync
            def _(sync):
                for ho in range(64):
                    if ho >= 2:
                        sync.wait_ge(s_mm.sem, mm0 + (ho - 1) * 4)
                    sem = s_l0 if ho % 2 == 0 else s_l1
                    sem.dma(sync.dma_start(
                        cB[:, (ho % 2) * 2048:(ho % 2) * 2048 + 2048],
                        bass.AP(z_h, ho * 128 * TOWN, [[TOWN, 128], [1, TOWN]])))

            @blk.tensor
            def _(te):
                for ho in range(64):
                    sem = s_l0 if ho % 2 == 0 else s_l1
                    te.wait_ge(sem.sem, l_base[ho % 2] + 16 * (ho // 2 + 1))
                    for ci in range(4):
                        idx = ho * 4 + ci
                        if idx >= 4:
                            te.wait_ge(s_ev.sem, ev0 + idx - 3)
                        te.matmul(pf[ci % 4][:, 0:512], hm["h128"][:],
                                  cB[:, (ho % 2) * 2048 + ci * 512:(ho % 2) * 2048 + ci * 512 + 512],
                                  start=True, stop=True)
                        s_mm.inc(te.drain())

            @blk.vector
            def _(v):
                for ho in range(64):
                    for ci in range(4):
                        idx = ho * 4 + ci
                        v.wait_ge(s_mm.sem, mm0 + idx + 1)
                        if ho >= 2:
                            v.wait_ge(s_gs.sem, gs0 + 16 * (ho - 1))
                        v.tensor_copy(cE[:, (ho % 2) * 2048 + ci * 512:
                                         (ho % 2) * 2048 + ci * 512 + 512],
                                      pf[ci % 4][:, 0:512])
                        s_ev.inc(v.drain())

            @blk.gpsimd
            def _(g):
                for ho in range(64):
                    g.wait_ge(s_ev.sem, ev0 + ho * 4 + 4)
                    store(g, bass.AP(s2a_h, ho * 128 * TOWN, [[TOWN, 128], [1, TOWN]]),
                          cE[:, (ho % 2) * 2048:(ho % 2) * 2048 + 2048])

        stage2_quant(s2a_h, xq2t_h, 8192, 64, 2, "w2b", 1.0 / math.sqrt(8192.0))

    _PH("P5e")
    if upto >= 14:
        with nc.Block() as blk:
            ev0 = s_ev.n
            l_base = [s_l0.n, s_l1.n]
            gs0 = s_gs.n

            @blk.sync
            def _(sync):
                for j in range(64):
                    if j >= 2:
                        sync.wait_ge(s_ev.sem, ev0 + j - 1)
                    sem = s_l0 if j % 2 == 0 else s_l1
                    sem.dma(sync.dma_start(
                        cC[:, (j % 2) * 2048:(j % 2) * 2048 + 2048],
                        bass.AP(wdT, j * 128 * D, [[D, 128], [1, D]])))

            @blk.vector
            def _(v):
                for j in range(64):
                    sem = s_l0 if j % 2 == 0 else s_l1
                    v.wait_ge(sem.sem, l_base[j % 2] + 16 * (j // 2 + 1))
                    wsrc = cC[:, (j % 2) * 2048:(j % 2) * 2048 + 2048]
                    v.tensor_tensor(cD[:, 0:2048], wsrc, _bc(wsc, 16, 8, 2048), OP.is_gt)
                    v.tensor_tensor(cD[:, 2048:4096], wsrc, _bc(wsc, 16, 11, 2048), OP.is_lt)
                    v.drain()
                    v.tensor_sub(cD[:, 0:2048], cD[:, 0:2048], cD[:, 2048:4096])
                    v.drain()
                    if j >= 2:
                        v.wait_ge(s_gs.sem, gs0 + 16 * (j - 1))
                    v.tensor_tensor(cE[:, (j % 2) * 2048:(j % 2) * 2048 + 2048],
                                    cD[:, 0:2048], _bc(wsc, 16, 5, 2048), OP.mult)
                    s_ev.inc(v.drain())

            @blk.gpsimd
            def _(g):
                for j in range(64):
                    g.wait_ge(s_ev.sem, ev0 + j + 1)
                    store(g, bass.AP(wdq_h, j * 128 * D, [[D, 128], [1, D]]),
                          cE[:, (j % 2) * 2048:(j % 2) * 2048 + 2048])

    _PH("P6")
    if upto >= 15:
        for c in range(4):
            with nc.Block() as blk:
                mm0, ev0 = s_mm.n, s_ev.n
                l_base = [s_l0.n, s_l1.n]
                gw_base = [s_gw0.n, s_gw1.n]
                gs0 = s_gs.n
                lx_mark = s_lx.n + 64 * 16

                @blk.sync
                def _(sync):
                    for g in range(64):
                        s_lx.dma(sync.dma_start_transpose(
                            bigR[:, g * 512:g * 512 + 512],
                            bass.AP(xq2t_h, c * 512 * DI + g * 128, [[DI, 512], [1, 128]])))
                    for d in range(16):
                        if d >= 2:
                            sync.wait_ge(s_ev.sem, ev0 + (d - 1) * 6)
                        sem = s_l0 if d % 2 == 0 else s_l1
                        sem.dma(sync.dma_start(
                            cB[:, (d % 2) * 512:(d % 2) * 512 + 512],
                            bass.AP(attn_h, d * 128 * TOWN + c * 512,
                                    [[TOWN, 128], [1, 512]])))
                        for tt in range(4):
                            sem.dma(sync.dma_start(
                                cD[:, 4096 + (d % 2) * 1024 + tt * 128:
                                     4096 + (d % 2) * 1024 + tt * 128 + 128],
                                bass.AP(x_own, (c * 512 + tt * 128) * D + d * 128,
                                        [[D, 128], [1, 128]])))

                @blk.gpsimd
                def _(g):
                    for d in range(16):
                        if d >= 2:
                            g.wait_ge(s_mm.sem, mm0 + (d - 1) * 5)
                        dst = cA if d % 2 == 0 else cE
                        gsem = s_gw0 if d % 2 == 0 else s_gw1
                        gsem.dma(g.dma_start(
                            bass.AP(dst, 0, [[8192, 128], [128, 64], [1, 128]]),
                            bass.AP(wdq_h, d * 128, [[D, 128], [128 * D, 64], [1, 128]])))
                        if d >= 1:
                            for tt in range(4):
                                g.wait_ge(s_ev.sem, ev0 + (d - 1) * 6 + 2 + tt + 1)
                                store(g, bass.AP(out_d,
                                                 (c * 512 + tt * 128) * D + (d - 1) * 128,
                                                 [[D, 128], [1, 128]]),
                                      cC[:, (((d - 1) * 4 + tt) % 8) * 128:
                                           (((d - 1) * 4 + tt) % 8) * 128 + 128])
                    for tt in range(4):
                        g.wait_ge(s_ev.sem, ev0 + 15 * 6 + 2 + tt + 1)
                        store(g, bass.AP(out_d, (c * 512 + tt * 128) * D + 15 * 128,
                                         [[D, 128], [1, 128]]),
                              cC[:, ((15 * 4 + tt) % 8) * 128:
                                   ((15 * 4 + tt) % 8) * 128 + 128])

                @blk.tensor
                def _(te):
                    te.wait_ge(s_lx.sem, lx_mark)
                    for d in range(16):
                        gsem = s_gw0 if d % 2 == 0 else s_gw1
                        te.wait_ge(gsem.sem, gw_base[d % 2] + 16 * (d // 2 + 1))
                        bank = pf[d % 2]
                        if d >= 2:
                            te.wait_ge(s_ev.sem, ev0 + (d - 1) * 6 + 2)
                        src = cA if d % 2 == 0 else cE
                        for g in range(64):
                            te.matmul(bank[:, 0:512], src[:, g * 128:g * 128 + 128],
                                      bigR[:, g * 512:g * 512 + 512],
                                      start=(g == 0), stop=(g == 63))
                        s_mm.inc(te.drain())
                        te.wait_ge(s_ev.sem, ev0 + d * 6 + 2)
                        for tt in range(4):
                            if d * 6 + 2 + tt - 2 > 0 and tt >= 2:
                                te.wait_ge(s_ev.sem, ev0 + d * 6 + 2 + tt - 1)
                            te.transpose(
                                bass.AP(pf[2 + tt % 2], 0, [[512, 128], [1, 128]]),
                                cF[:, (d % 2) * 512 + tt * 128:(d % 2) * 512 + tt * 128 + 128]
                                if False else
                                cC[:, 4096 + (d % 2) * 512 + tt * 128:
                                     4096 + (d % 2) * 512 + tt * 128 + 128],
                                identf_s[:])
                            s_mm.inc(te.drain())

                @blk.vector
                def _(v):
                    for d in range(16):
                        v.wait_ge(s_mm.sem, mm0 + d * 5 + 1)
                        sem = s_l0 if d % 2 == 0 else s_l1
                        v.wait_ge(sem.sem, l_base[d % 2] + 80 * (d // 2 + 1))
                        v.tensor_add(cC[:, 4096 + (d % 2) * 512:4096 + (d % 2) * 512 + 512],
                                     pf[d % 2][:, 0:512],
                                     cB[:, (d % 2) * 512:(d % 2) * 512 + 512])
                        s_ev.inc(v.drain())
                        s_ev.inc(v.drain())
                        for tt in range(4):
                            v.wait_ge(s_mm.sem, mm0 + d * 5 + 1 + tt + 1)
                            slot = (d * 4 + tt) % 8
                            if d * 4 + tt >= 8:
                                v.wait_ge(s_gs.sem, gs0 + 16 * (d * 4 + tt - 7))
                            v.tensor_add(
                                cC[:, slot * 128:slot * 128 + 128],
                                pf[2 + tt % 2][:, 0:128],
                                cD[:, 4096 + (d % 2) * 1024 + tt * 128:
                                     4096 + (d % 2) * 1024 + tt * 128 + 128])
                            s_ev.inc(v.drain())

        with nc.Block() as blk:
            @blk.gpsimd
            def _(g):
                g.wait_ge(s_gs.sem, s_gs.n)
                g.wait_ge(s_gw0.sem, s_gw0.n)
                g.wait_ge(s_gw1.sem, s_gw1.n)

            @blk.sync
            def _(sync):
                sync.wait_ge(s_lx.sem, s_lx.n)
                sync.wait_ge(s_l0.sem, s_l0.n)
                sync.wait_ge(s_l1.sem, s_l1.n)

    _PH(None)
    esC.close()
    es.close()
    return nc


def kernel(L0_attn_u, L0_attn_Wq, L0_attn_bq, L0_attn_Wk, L0_attn_bk,
           L0_attn_Wv, L0_attn_bv, L0_attn_convq_w, L0_attn_convq_b,
           L0_attn_convk_w, L0_attn_convk_b, L0_attn_ssm_kernel,
           L0_attn_D_skip, L0_attn_Wo, L0_attn_bo,
           L0_ffn_w_gate, L0_ffn_w_up, L0_ffn_w_down):
    x = np.ascontiguousarray(np.asarray(L0_attn_u, dtype=np.float32))
    f32 = lambda a: np.ascontiguousarray(np.asarray(a, dtype=np.float32))
    C = host_constants()

    chvec = np.zeros((D, 8), dtype=np.float32)
    chvec[:, 0] = f32(L0_attn_bq)
    chvec[:, 1] = f32(L0_attn_bk)
    chvec[:, 2] = f32(L0_attn_bv)
    chvec[:, 3] = f32(L0_attn_bo)
    chvec[:, 4] = f32(L0_attn_convq_b)
    chvec[:, 5] = f32(L0_attn_convk_b)
    chvec[:, 6] = f32(L0_attn_D_skip)
    cwvec = np.zeros((D, 8), dtype=np.float32)
    cwvec[:, 0:3] = f32(L0_attn_convq_w)[:, 0, :]
    cwvec[:, 3:6] = f32(L0_attn_convk_w)[:, 0, :]

    shared = {
        "wqT": np.ascontiguousarray(f32(L0_attn_Wq).T),
        "wkT": np.ascontiguousarray(f32(L0_attn_Wk).T),
        "wvT": np.ascontiguousarray(f32(L0_attn_Wv).T),
        "woT": np.ascontiguousarray(f32(L0_attn_Wo).T),
        "chvec": chvec, "cwvec": cwvec,
        "ssmk": f32(L0_attn_ssm_kernel),
        "wgT": np.ascontiguousarray(f32(L0_ffn_w_gate).T[:, :]),
        "wuT": np.ascontiguousarray(f32(L0_ffn_w_up).T[:, :]),
        "wdT": np.ascontiguousarray(f32(L0_ffn_w_down).T[PERM2, :]),
    }
    shared["wgT"] = np.ascontiguousarray(shared["wgT"][PERM1, :])
    shared["wuT"] = np.ascontiguousarray(shared["wuT"][PERM1, :])
    shared.update(C)

    nc = build_program()
    in_maps = []
    for core in range(8):
        b, h = core // 2, core % 2
        m = dict(shared)
        m["x_own"] = np.ascontiguousarray(x[b, h * 2048:(h + 1) * 2048])
        m["x_prev"] = np.ascontiguousarray(x[b, (1 - h) * 2048:(2 - h) * 2048])
        m["alpha"] = np.full((128, 1), float(h), dtype=np.float32)
        in_maps.append(m)

    res = run_bass_kernel_spmd(nc, in_maps, core_ids=list(range(8)))
    out = np.empty((4, L, D), dtype=np.float32)
    for core in range(8):
        b, h = core // 2, core % 2
        out[b, h * 2048:(h + 1) * 2048] = res.results[core]["out"]
    return out


# version 05
# speedup vs baseline: 1.6040x; 1.6040x over previous
import contextlib
import math
import numpy as np
import ml_dtypes

import concourse.bass as bass
import concourse.mybir as mybir
from concourse.bass_utils import run_bass_kernel_spmd

BFD = ml_dtypes.bfloat16
F32 = mybir.dt.float32
BF16 = mybir.dt.bfloat16
AF = mybir.ActivationFunctionType
OP = mybir.AluOpType
AX = mybir.AxisListType

D = 2048
L = 4096
TOWN = 2048
NF = 2176
NFT = 17
PAD = 512
DI = 8192

_DEBUG_TAPS = ()
_UPTO = 99


class Ctr:
    def __init__(self, sem):
        self.sem = sem
        self.n = 0

    def inc(self, inst, k=1):
        inst.then_inc(self.sem, k)
        self.n += k
        return inst

    def dma(self, inst):
        return self.inc(inst, 16)


def _hadamard(n):
    H = np.array([[1.0]])
    while H.shape[0] < n:
        H = np.block([[H, H], [H, -H]])
    return H


def _bf(x):
    return np.ascontiguousarray(np.asarray(x, dtype=np.float32).astype(BFD))


def host_constants():
    C = {}
    C["h128"] = _bf(_hadamard(128))
    C["w2a"] = _bf(np.kron(_hadamard(16), np.eye(8)))
    C["w2b"] = _bf(np.kron(_hadamard(64), np.eye(2)))
    C["identb"] = _bf(np.eye(128))
    C["identf"] = np.ascontiguousarray(np.eye(128, dtype=np.float32))
    t = np.arange(2048, dtype=np.float64)[:, None]
    f = np.arange(NF, dtype=np.float64)[None, :]
    ang = 2.0 * np.pi * t * f / 4096.0
    C["fwdr"] = _bf(np.cos(ang))
    C["fwdi"] = _bf(-np.sin(ang))
    k = np.arange(NF, dtype=np.float64)[:, None]
    tt = np.arange(2048, dtype=np.float64)[None, :]
    wk = np.where(k == 0, 1.0, 2.0)
    wk = np.where(k == 2048, 1.0, wk)
    wk = np.where(k > 2048, 0.0, wk) / 4096.0
    angi = 2.0 * np.pi * tt * k / 4096.0
    C["invr"] = _bf(wk * np.cos(angi))
    C["invi"] = _bf(-wk * np.sin(angi))
    C["sgn"] = np.ascontiguousarray(((-1.0) ** np.arange(128)).astype(np.float32)[:, None])
    return C


def _perm_fwht(n_outer, n_inner, offcnt):
    gcnt = n_inner // offcnt
    perm = np.empty(n_outer * n_inner, dtype=np.int64)
    for g in range(gcnt):
        for ho in range(n_outer):
            for off in range(offcnt):
                perm[g * 128 + ho * offcnt + off] = ho * n_inner + g * offcnt + off
    return perm


PERM1 = _perm_fwht(16, 128, 8)
PERM2 = _perm_fwht(64, 128, 2)


def _tile16(wT):
    D_, N = wT.shape
    A = wT.reshape(16, 128, N // 128, 128)
    return np.ascontiguousarray(A.transpose(2, 1, 0, 3).reshape(D_, N))



def _bc(t, pitch, col, n):
    return bass.AP(t, col, [[pitch, 128], [0, n]])

def build_program(upto=None):
    _scope_state = []
    def _PH(name):
        if _scope_state:
            pn, sid = _scope_state.pop()
            nc.leave_named_scope(pn, sid, False)
        if name:
            sid, _ = nc.enter_named_scope(name, False)
            _scope_state.append((name, sid))

    if upto is None:
        upto = _UPTO
    nc = bass.Bass("TRN2", target_bir_lowering=False)

    def scratch(name, shape, dt):
        if name in _DEBUG_TAPS:
            return nc.dram_tensor(name, shape, dt, kind="ExternalOutput")
        return nc.dram_tensor(name, shape, dt)

    x_own = nc.dram_tensor("x_own", [TOWN, D], F32, kind="ExternalInput")
    x_prev = nc.dram_tensor("x_prev", [TOWN, D], F32, kind="ExternalInput")
    alpha_d = nc.dram_tensor("alpha", [128, 1], F32, kind="ExternalInput")
    wqT = nc.dram_tensor("wqT", [D, D], F32, kind="ExternalInput")
    wkT = nc.dram_tensor("wkT", [D, D], F32, kind="ExternalInput")
    wvT = nc.dram_tensor("wvT", [D, D], F32, kind="ExternalInput")
    woT = nc.dram_tensor("woT", [D, D], F32, kind="ExternalInput")
    chvec = nc.dram_tensor("chvec", [D, 8], F32, kind="ExternalInput")
    cwvec = nc.dram_tensor("cwvec", [D, 8], F32, kind="ExternalInput")
    ssmk = nc.dram_tensor("ssmk", [D, D], F32, kind="ExternalInput")
    wgT = nc.dram_tensor("wgT", [D, DI], F32, kind="ExternalInput")
    wuT = nc.dram_tensor("wuT", [D, DI], F32, kind="ExternalInput")
    wdT = nc.dram_tensor("wdT", [DI, D], F32, kind="ExternalInput")
    cst = {}
    for nm, shape, dt in [("h128", [128, 128], BF16), ("w2a", [128, 128], BF16),
                          ("w2b", [128, 128], BF16), ("identb", [128, 128], BF16),
                          ("identf", [128, 128], F32),
                          ("fwdr", [2048, NF], BF16), ("fwdi", [2048, NF], BF16),
                          ("invr", [NF, 2048], BF16), ("invi", [NF, 2048], BF16),
                          ("sgn", [128, 1], F32)]:
        cst[nm] = nc.dram_tensor(nm, shape, dt, kind="ExternalInput")
    out_d = nc.dram_tensor("out", [TOWN, D], F32, kind="ExternalOutput")

    xn_h = scratch("xn_h", [PAD + L, D], BF16)
    k_h = scratch("k_h", [D, L], BF16)
    v_h = scratch("v_h", [D, L], BF16)
    q_h = scratch("q_h", [D, 2560], BF16)
    kv_h = scratch("kv_h", [D, L], BF16)
    qc_h = scratch("qc_h", [D, TOWN], BF16)
    ssmb_h = scratch("ssmb_h", [D, D], BF16)
    y_h = scratch("y_h", [TOWN, D], BF16)
    attn_h = scratch("attn_h", [D, TOWN], BF16)
    s1a_h = scratch("s1a_h", [D, TOWN], BF16)
    xq1t_h = scratch("xq1t_h", [TOWN, D], BF16)
    z_h = scratch("z_h", [DI, TOWN], BF16)
    s2a_h = scratch("s2a_h", [DI, TOWN], BF16)
    xq2t_h = scratch("xq2t_h", [TOWN, DI], BF16)
    wdq_h = scratch("wdq_h", [DI, D], BF16)
    scr_h = scratch("scr_h", [1, 16], F32)

    es = contextlib.ExitStack()
    sb = lambda nm, sh, dt: es.enter_context(nc.sbuf_tensor(nm, sh, dt))
    psm = lambda nm, sh, dt: es.enter_context(nc.psum_tensor(nm, sh, dt))
    mksem = lambda nm: Ctr(es.enter_context(nc.semaphore(nm)))

    cA = sb("cA", [128, 8192], BF16)
    cB = sb("cB", [128, 8192], BF16)
    cE = sb("cE", [128, 8192], BF16)
    cF = sb("cF", [128, 8192], BF16)
    junk = sb("junkb", [128, 2048], BF16)
    zpad = sb("zpad", [128, 2048], BF16)
    hm = {nm: sb("s_" + nm, [128, 128], BF16)
          for nm in ("h128", "w2a", "w2b", "identb")}
    identf_s = sb("s_identf", [128, 128], F32)
    chv_s = sb("chv_s", [128, 128], F32)
    cw_s = sb("cw_s", [128, 128], F32)
    alpha_s = sb("alpha_s", [128, 1], F32)
    sgn_s = sb("sgn_s", [128, 1], F32)
    asg_s = sb("asg_s", [128, 1], F32)
    st_s = sb("st_s", [128, 128], F32)
    wsc = sb("wsc_s", [128, 16], F32)

    pf = [psm(f"pf{i}", [128, 512], F32) for i in range(6)]
    pb = psm("pb", [128, 1024], BF16)
    pb2 = psm("pb2", [128, 1024], BF16)

    s_ld = mksem("s_ld")
    s_l0 = mksem("s_l0")
    s_l1 = mksem("s_l1")
    s_lx = mksem("s_lx")
    s_gw0 = mksem("s_gw0")
    s_gw1 = mksem("s_gw1")
    s_gs = mksem("s_gs")
    s_gd = mksem("s_gd")
    s_mm = mksem("s_mm")
    s_ev = mksem("s_ev")
    s_ac = mksem("s_ac")
    s_fr = mksem("s_fr")
    s_vh = mksem("s_vh")
    s_pc = mksem("s_pc")

    def store(g, dst_ap, src_ap):
        s_gs.dma(g.dma_start(dst_ap, src_ap))
        g.wait_ge(s_gs.sem, s_gs.n)

    def vhs(v, inst):
        s_vh.inc(v.drain())
        v.wait_ge(s_vh.sem, s_vh.n)

    with nc.Block() as blk:
        @blk.sync
        def _(sync):
            for nm in ("h128", "w2a", "w2b", "identb"):
                s_lx.dma(sync.dma_start(hm[nm][:], cst[nm].ap()))
            s_lx.dma(sync.dma_start(identf_s[:], cst["identf"].ap()))
            s_lx.dma(sync.dma_start(alpha_s[:], alpha_d.ap()))
            s_lx.dma(sync.dma_start(sgn_s[:], cst["sgn"].ap()))
            s_lx.dma(sync.dma_start(
                chv_s[:], bass.AP(chvec, 0, [[8, 128], [1024, 16], [1, 8]])))
            s_lx.dma(sync.dma_start(
                cw_s[:], bass.AP(cwvec, 0, [[8, 128], [1024, 16], [1, 8]])))

        @blk.vector
        def _(v):
            v.memset(st_s[:], 0.0)
            v.memset(wsc[:], 0.0)
            v.drain()
            v.wait_ge(s_lx.sem, s_lx.n)
            v.tensor_mul(asg_s[:], alpha_s[:], sgn_s[:])

        @blk.gpsimd
        def _(g):
            g.memset(junk[:], 0.0)
            g.memset(zpad[:], 0.0)

    esA = contextlib.ExitStack()
    sbA = lambda nm, sh, dt: esA.enter_context(nc.sbuf_tensor(nm, sh, dt))
    cC = sbA("cCa", [128, 8192], F32)
    cD = sbA("cDa", [128, 8192], F32)
    bigR = sbA("bigRa", [128, 32768], BF16)

    _PH("P1")
    if upto >= 1:
        with nc.Block() as blk:
            ev0, ac0, fr0 = s_ev.n, s_ac.n, s_fr.n
            l_base = [s_l0.n, s_l1.n]
            gs0 = s_gs.n

            @blk.sync
            def _(sync):
                for r in range(PAD // 128):
                    s_lx.dma(sync.dma_start(
                        bass.AP(xn_h, r * 128 * D, [[D, 128], [1, D]]), zpad[:]))
                for i in range(32):
                    src = x_prev if i < 16 else x_own
                    if i >= 2:
                        sync.wait_ge(s_fr.sem, fr0 + i - 1)
                    sem = s_l0 if i % 2 == 0 else s_l1
                    sem.dma(sync.dma_start(
                        cC[:, (i % 2) * 2048:(i % 2) * 2048 + 2048],
                        bass.AP(src, (i % 16) * 128 * D, [[D, 128], [1, D]])))
                sync.wait_ge(s_lx.sem, s_lx.n)

            @blk.vector
            def _(v):
                for i in range(32):
                    xt = cC[:, (i % 2) * 2048:(i % 2) * 2048 + 2048]
                    dt_ = cD[:, (i % 2) * 2048:(i % 2) * 2048 + 2048]
                    sem = s_l0 if i % 2 == 0 else s_l1
                    v.wait_ge(sem.sem, l_base[i % 2] + 16 * (i // 2 + 1))
                    vhs(v, v.reduce_sum(st_s[:, 0:1], xt, axis=AX.X))
                    vhs(v, v.tensor_single_scalar(st_s[:, 1:2], st_s[:, 0:1],
                                                  1.0 / 2048.0, OP.mult))
                    v.tensor_tensor(dt_, xt, _bc(st_s, 128, 1, 2048), OP.subtract)
                    v.drain()
                    v.tensor_mul(xt, dt_, dt_)
                    v.drain()
                    vhs(v, v.reduce_sum(st_s[:, 2:3], xt, axis=AX.X))
                    vhs(v, v.tensor_scalar(st_s[:, 3:4], st_s[:, 2:3], 1.0 / 2048.0,
                                           1e-5, OP.mult, OP.add))
                    vhs(v, v.reciprocal(st_s[:, 5:6], st_s[:, 3:4]))
                    for _ in range(4):
                        vhs(v, v.tensor_mul(st_s[:, 6:7], st_s[:, 5:6], st_s[:, 5:6]))
                        vhs(v, v.tensor_mul(st_s[:, 7:8], st_s[:, 3:4], st_s[:, 6:7]))
                        vhs(v, v.tensor_scalar(st_s[:, 8:9], st_s[:, 7:8], -0.5, 1.5,
                                               OP.mult, OP.add))
                        vhs(v, v.tensor_mul(st_s[:, 5:6], st_s[:, 5:6], st_s[:, 8:9]))
                    xo = cE[:, (i % 2) * 2048:(i % 2) * 2048 + 2048]
                    if i >= 2:
                        v.wait_ge(s_gs.sem, gs0 + 16 * (i - 1))
                    v.tensor_tensor(xo, dt_, _bc(st_s, 128, 5, 2048), OP.mult)
                    s_ev.inc(v.drain())
                    s_fr.inc(v.drain())

            @blk.gpsimd
            def _(g):
                for i in range(32):
                    g.wait_ge(s_ev.sem, ev0 + i + 1)
                    store(g, bass.AP(xn_h, (PAD + i * 128) * D, [[D, 128], [1, D]]),
                          cE[:, (i % 2) * 2048:(i % 2) * 2048 + 2048])

    _PH("P1b")
    if upto >= 2:
        with nc.Block() as blk:
            ac0 = s_ac.n
            l_base = [s_l0.n, s_l1.n]
            chunks = [(mi, m, ci) for mi, m in enumerate((wgT, wuT, wdT))
                      for ci in range(64)]

            @blk.sync
            def _(sync):
                for idx, (mi, m, ci) in enumerate(chunks):
                    if idx >= 2:
                        sync.wait_ge(s_ac.sem, ac0 + idx - 1)
                    sem = s_l0 if idx % 2 == 0 else s_l1
                    sem.dma(sync.dma_start(
                        cD[:, (idx % 2) * 2048:(idx % 2) * 2048 + 2048],
                        bass.AP(m, ci * 128 * 2048, [[2048, 128], [1, 2048]])))

            @blk.scalar
            def _(s):
                for idx, (mi, m, ci) in enumerate(chunks):
                    sem = s_l0 if idx % 2 == 0 else s_l1
                    s.wait_ge(sem.sem, l_base[idx % 2] + 16 * (idx // 2 + 1))
                    s.activation(junk[:, 0:2048],
                                 cD[:, (idx % 2) * 2048:(idx % 2) * 2048 + 2048],
                                 AF.Abs, accum_out=cC[:, idx:idx + 1])
                    s_ac.inc(s.drain())

            @blk.vector
            def _(v):
                v.wait_ge(s_ac.sem, ac0 + len(chunks))
                for mi in range(3):
                    v.reduce_sum(st_s[:, mi * 32:mi * 32 + 1], cC[:, mi * 64:mi * 64 + 64],
                                 axis=AX.X)

        with nc.Block() as blk:
            @blk.tensor
            def _(te):
                te.transpose(bass.AP(pf[0], 0, [[512, 128], [1, 128]]),
                             st_s[:, 0:128], identf_s[:])

        with nc.Block() as blk:
            ev0 = s_ev.n

            @blk.vector
            def _(v):
                v.tensor_copy(cC[:, 0:128], pf[0][:, 0:128])
                v.drain()
                vhs(v, v.reduce_sum(st_s[:, 13:14], cC[:, 0:128], axis=AX.X))
                s_ev.inc(v.drain())

            @blk.gpsimd
            def _(g):
                g.wait_ge(s_ev.sem, ev0 + 1)
                for mi in range(3):
                    store(g, bass.AP(scr_h, mi, [[1, 1], [1, 1]]),
                          st_s[mi * 32:mi * 32 + 1, 13:14])

        with nc.Block() as blk:
            @blk.sync
            def _(sync):
                for mi in range(3):
                    s_lx.dma(sync.dma_start(wsc[:, mi:mi + 1],
                                            bass.AP(scr_h, mi, [[0, 128], [1, 1]])))

            @blk.vector
            def _(v):
                v.wait_ge(s_lx.sem, s_lx.n)
                for mi in range(3):
                    vhs(v, v.tensor_scalar(wsc[:, 3 + mi:4 + mi], wsc[:, mi:mi + 1],
                                           1.0 / (D * DI), 1e-5, OP.mult, OP.max))
                    vhs(v, v.tensor_single_scalar(wsc[:, 6 + mi:7 + mi],
                                                  wsc[:, 3 + mi:4 + mi], 0.5, OP.mult))
                    vhs(v, v.tensor_single_scalar(wsc[:, 9 + mi:10 + mi],
                                                  wsc[:, 6 + mi:7 + mi], -1.0, OP.mult))

    def proj_pass(wT, dst_h, nchunk, row_base, bias_col):
        with nc.Block() as blk:
            mm0, ev0 = s_mm.n, s_ev.n
            l_base = [s_l0.n, s_l1.n]
            gw0 = s_gw0.n
            gs0 = s_gs.n

            @blk.gpsimd
            def _(g):
                for o in range(16):
                    s_gw0.dma(g.dma_start(
                        bass.AP(bigR, o * 2048, [[32768, 128], [1, 2048]]),
                        bass.AP(wT, o * 128 * 2048, [[2048, 128], [1, 2048]])))
                for ci in range(nchunk):
                    for o in range(16):
                        idx = ci * 16 + o
                        g.wait_ge(s_ev.sem, ev0 + idx + 1)
                        store(g, bass.AP(dst_h, o * 128 * (nchunk * 512) + ci * 512,
                                         [[nchunk * 512, 128], [1, 512]]),
                              cE[:, (idx % 8) * 512:(idx % 8) * 512 + 512])

            @blk.sync
            def _(sync):
                for ci in range(nchunk):
                    slot = cB if ci % 2 == 0 else cF
                    if ci >= 2:
                        sync.wait_ge(s_mm.sem, mm0 + (ci - 1) * 16)
                    sem = s_l0 if ci % 2 == 0 else s_l1
                    for ht in range(16):
                        sem.dma(sync.dma_start_transpose(
                            slot[:, ht * 512:ht * 512 + 512],
                            bass.AP(xn_h, (row_base + ci * 512) * D + ht * 128,
                                    [[D, 512], [1, 128]])))

            @blk.tensor
            def _(te):
                te.wait_ge(s_gw0.sem, s_gw0.n)
                for ci in range(nchunk):
                    slot = cB if ci % 2 == 0 else cF
                    sem = s_l0 if ci % 2 == 0 else s_l1
                    te.wait_ge(sem.sem, l_base[ci % 2] + 256 * (ci // 2 + 1))
                    for o in range(16):
                        idx = ci * 16 + o
                        if idx >= 4:
                            te.wait_ge(s_ev.sem, ev0 + idx - 3)
                        bank = pf[o % 4]
                        for ht in range(16):
                            te.matmul(bank[:, 0:512],
                                      bigR[:, (o * 16 + ht) * 128:(o * 16 + ht) * 128 + 128],
                                      slot[:, ht * 512:ht * 512 + 512],
                                      start=(ht == 0), stop=(ht == 15))
                        s_mm.inc(te.drain())

            @blk.vector
            def _(v):
                for ci in range(nchunk):
                    for o in range(16):
                        idx = ci * 16 + o
                        v.wait_ge(s_mm.sem, mm0 + idx + 1)
                        if idx >= 8:
                            v.wait_ge(s_gs.sem, gs0 + 16 * (idx - 7))
                        v.tensor_tensor(
                            cE[:, (idx % 8) * 512:(idx % 8) * 512 + 512],
                            pf[o % 4][:, 0:512],
                            _bc(chv_s, 128, o * 8 + bias_col, 512), OP.add)
                        s_ev.inc(v.drain())

    _PH("P2k")
    if upto >= 3:
        proj_pass(wkT, k_h, 8, PAD, 1)
    _PH("P2v")
    if upto >= 4:
        proj_pass(wvT, v_h, 8, PAD, 2)
    _PH("P2q")
    if upto >= 5:
        proj_pass(wqT, q_h, 5, PAD + 1536, 0)

    _PH("P2kv")
    if upto >= 6:
        with nc.Block() as blk:
            ev0 = s_ev.n
            l_base = [s_l0.n, s_l1.n]
            gs0 = s_gs.n

            @blk.sync
            def _(sync):
                for o in range(16):
                    if o >= 2:
                        sync.wait_ge(s_ev.sem, ev0 + o - 1)
                    sem = s_l0 if o % 2 == 0 else s_l1
                    sem.dma(sync.dma_start(
                        cA[:, (o % 2) * 4096:(o % 2) * 4096 + 4096],
                        bass.AP(k_h, o * 128 * L, [[L, 128], [1, L]])))
                    sem.dma(sync.dma_start(
                        cB[:, (o % 2) * 4096:(o % 2) * 4096 + 4096],
                        bass.AP(v_h, o * 128 * L, [[L, 128], [1, L]])))

            @blk.vector
            def _(v):
                for o in range(16):
                    sem = s_l0 if o % 2 == 0 else s_l1
                    v.wait_ge(sem.sem, l_base[o % 2] + 32 * (o // 2 + 1))
                    ks = cA[:, (o % 2) * 4096:(o % 2) * 4096 + 4096]
                    vs = cB[:, (o % 2) * 4096:(o % 2) * 4096 + 4096]
                    km = cA[:, (o % 2) * 4096 + 2046:(o % 2) * 4096 + 2048]
                    v.tensor_tensor(km, km, _bc(alpha_s, 1, 0, 2), OP.mult)
                    v.drain()
                    v.memset(cC[:, 0:2], 0.0)
                    v.tensor_tensor(cC[:, 2:4098], ks, _bc(cw_s, 128, o * 8 + 5, 4096), OP.mult)
                    v.drain()
                    v.tensor_tensor(cC[:, 2:4098], cC[:, 2:4098],
                                    _bc(chv_s, 128, o * 8 + 5, 4096), OP.add)
                    v.drain()
                    v.tensor_tensor(cD[:, 0:4095], ks[:, 0:4095],
                                    _bc(cw_s, 128, o * 8 + 4, 4095), OP.mult)
                    v.drain()
                    v.tensor_tensor(cC[:, 3:4098], cC[:, 3:4098], cD[:, 0:4095], OP.add)
                    v.drain()
                    v.tensor_tensor(cD[:, 0:4094], ks[:, 0:4094],
                                    _bc(cw_s, 128, o * 8 + 3, 4094), OP.mult)
                    v.drain()
                    v.tensor_tensor(cC[:, 4:4098], cC[:, 4:4098], cD[:, 0:4094], OP.add)
                    v.drain()
                    if o >= 2:
                        v.wait_ge(s_gs.sem, gs0 + 16 * (o - 1))
                    v.tensor_mul(cE[:, (o % 2) * 4096:(o % 2) * 4096 + 4096],
                                 cC[:, 2:4098], vs)
                    s_ev.inc(v.drain())

            @blk.gpsimd
            def _(g):
                for o in range(16):
                    g.wait_ge(s_ev.sem, ev0 + o + 1)
                    store(g, bass.AP(kv_h, o * 128 * L, [[L, 128], [1, L]]),
                          cE[:, (o % 2) * 4096:(o % 2) * 4096 + 4096])

    _PH("P2qc")
    if upto >= 7:
        with nc.Block() as blk:
            ev0 = s_ev.n
            l_base = [s_l0.n, s_l1.n]
            gs0 = s_gs.n

            @blk.sync
            def _(sync):
                for o in range(16):
                    if o >= 2:
                        sync.wait_ge(s_ev.sem, ev0 + o - 1)
                    sem = s_l0 if o % 2 == 0 else s_l1
                    sem.dma(sync.dma_start(
                        cA[:, (o % 2) * 2560:(o % 2) * 2560 + 2560],
                        bass.AP(q_h, o * 128 * 2560, [[2560, 128], [1, 2560]])))

            @blk.vector
            def _(v):
                for o in range(16):
                    sem = s_l0 if o % 2 == 0 else s_l1
                    v.wait_ge(sem.sem, l_base[o % 2] + 16 * (o // 2 + 1))
                    qs = cA[:, (o % 2) * 2560:(o % 2) * 2560 + 2560]
                    qm = cA[:, (o % 2) * 2560 + 510:(o % 2) * 2560 + 512]
                    v.tensor_tensor(qm, qm, _bc(alpha_s, 1, 0, 2), OP.mult)
                    v.drain()
                    v.tensor_tensor(cC[:, 0:2048], qs[:, 512:2560],
                                    _bc(cw_s, 128, o * 8 + 2, 2048), OP.mult)
                    v.drain()
                    v.tensor_tensor(cC[:, 0:2048], cC[:, 0:2048],
                                    _bc(chv_s, 128, o * 8 + 4, 2048), OP.add)
                    v.drain()
                    v.tensor_tensor(cD[:, 0:2048], qs[:, 511:2559],
                                    _bc(cw_s, 128, o * 8 + 1, 2048), OP.mult)
                    v.drain()
                    v.tensor_tensor(cC[:, 0:2048], cC[:, 0:2048], cD[:, 0:2048], OP.add)
                    v.drain()
                    v.tensor_tensor(cD[:, 0:2048], qs[:, 510:2558],
                                    _bc(cw_s, 128, o * 8 + 0, 2048), OP.mult)
                    v.drain()
                    v.tensor_tensor(cC[:, 0:2048], cC[:, 0:2048], cD[:, 0:2048], OP.add)
                    v.drain()
                    if o >= 2:
                        v.wait_ge(s_gs.sem, gs0 + 16 * (o - 1))
                    v.tensor_copy(cE[:, (o % 2) * 2048:(o % 2) * 2048 + 2048],
                                  cC[:, 0:2048])
                    s_ev.inc(v.drain())

            @blk.gpsimd
            def _(g):
                for o in range(16):
                    g.wait_ge(s_ev.sem, ev0 + o + 1)
                    store(g, bass.AP(qc_h, o * 128 * TOWN, [[TOWN, 128], [1, TOWN]]),
                          cE[:, (o % 2) * 2048:(o % 2) * 2048 + 2048])

    _PH("P2ssm")
    if upto >= 8:
        with nc.Block() as blk:
            @blk.gpsimd
            def _(g):
                for o in range(16):
                    s_gw0.dma(g.dma_start(
                        cF[:, (o % 2) * 2048:(o % 2) * 2048 + 2048],
                        bass.AP(ssmk, o * 128 * D, [[D, 128], [1, D]])))
                    g.wait_ge(s_gw0.sem, s_gw0.n)
                    store(g, bass.AP(ssmb_h, o * 128 * D, [[D, 128], [1, D]]),
                          cF[:, (o % 2) * 2048:(o % 2) * 2048 + 2048])

    esA.close()

    _PH("P3conv")
    if upto >= 9:
        esB = contextlib.ExitStack()
        sbB = lambda nm, sh, dt: esB.enter_context(nc.sbuf_tensor(nm, sh, dt))
        vcv = sbB("vcv", [128, 24576], BF16)
        ybf = sbB("ybf", [128, 17408], BF16)
        cmA = sbB("cmA", [128, 4352], BF16)
        cmB = sbB("cmB", [128, 4352], BF16)
        w32 = sbB("w32", [128, 2048], F32)

        for cc in range(4):
            with nc.Block() as blk:
                mm0, ev0 = s_mm.n, s_ev.n
                l_base = [s_l0.n, s_l1.n]
                lx_mark = s_lx.n + 48 * 16

                @blk.sync
                def _(sync):
                    for seg in range(3):
                        src = [kv_h, kv_h, ssmb_h][seg]
                        roww = L if seg < 2 else D
                        tbase = [2048, 0, 0][seg]
                        for tt in range(16):
                            s_lx.dma(sync.dma_start_transpose(
                                vcv[:, (seg * 16 + tt) * 512:(seg * 16 + tt) * 512 + 512],
                                bass.AP(src, cc * 512 * roww + tbase + tt * 128,
                                        [[roww, 512], [1, 128]])))
                    for ft in range(NFT):
                        cm = cmA if ft % 2 == 0 else cmB
                        sem = s_l0 if ft % 2 == 0 else s_l1
                        if ft >= 2:
                            sync.wait_ge(s_mm.sem, mm0 + ft - 1)
                        sem.dma(sync.dma_start(
                            cm[:, 0:2048],
                            bass.AP(cst["fwdr"], ft * 128,
                                    [[NF, 128], [128 * NF, 16], [1, 128]])))
                        sem.dma(sync.dma_start(
                            cm[:, 2048:4096],
                            bass.AP(cst["fwdi"], ft * 128,
                                    [[NF, 128], [128 * NF, 16], [1, 128]])))

                @blk.tensor
                def _(te):
                    te.wait_ge(s_lx.sem, lx_mark)
                    for ft in range(NFT):
                        cm = cmA if ft % 2 == 0 else cmB
                        sem = s_l0 if ft % 2 == 0 else s_l1
                        te.wait_ge(sem.sem, l_base[ft % 2] + 32 * (ft // 2 + 1))
                        if ft >= 1:
                            te.wait_ge(s_ev.sem, ev0 + ft)
                        for tt in range(16):
                            for mat in range(2):
                                for seg in range(3):
                                    te.matmul(
                                        pf[seg * 2 + mat][:, 0:512],
                                        cm[:, mat * 2048 + tt * 128:mat * 2048 + tt * 128 + 128],
                                        vcv[:, (seg * 16 + tt) * 512:(seg * 16 + tt) * 512 + 512],
                                        start=(tt == 0), stop=(tt == 15))
                        s_mm.inc(te.drain())

                @blk.vector
                def _(v):
                    for ft in range(NFT):
                        v.wait_ge(s_mm.sem, mm0 + ft + 1)
                        v.tensor_tensor(w32[:, 0:512], pf[2][:, 0:512],
                                        _bc(asg_s, 1, 0, 512), OP.mult)
                        v.drain()
                        v.tensor_tensor(w32[:, 0:512], w32[:, 0:512],
                                        pf[0][:, 0:512], OP.add)
                        v.tensor_tensor(w32[:, 512:1024], pf[3][:, 0:512],
                                        _bc(asg_s, 1, 0, 512), OP.mult)
                        v.drain()
                        v.tensor_tensor(w32[:, 512:1024], w32[:, 512:1024],
                                        pf[1][:, 0:512], OP.add)
                        v.drain()
                        v.tensor_mul(w32[:, 1024:1536], w32[:, 0:512], pf[4][:, 0:512])
                        v.tensor_mul(w32[:, 1536:2048], w32[:, 512:1024], pf[5][:, 0:512])
                        v.drain()
                        v.tensor_sub(ybf[:, ft * 1024:ft * 1024 + 512],
                                     w32[:, 1024:1536], w32[:, 1536:2048])
                        v.drain()
                        v.tensor_mul(w32[:, 1024:1536], w32[:, 0:512], pf[5][:, 0:512])
                        v.tensor_mul(w32[:, 1536:2048], w32[:, 512:1024], pf[4][:, 0:512])
                        v.drain()
                        v.tensor_add(ybf[:, ft * 1024 + 512:ft * 1024 + 1024],
                                     w32[:, 1024:1536], w32[:, 1536:2048])
                        s_ev.inc(v.drain())

            with nc.Block() as blk:
                mm0, ev0 = s_mm.n, s_ev.n
                l_base = [s_l0.n, s_l1.n]
                gs0 = s_gs.n

                @blk.sync
                def _(sync):
                    for tt in range(16):
                        cm = cmA if tt % 2 == 0 else cmB
                        sem = s_l0 if tt % 2 == 0 else s_l1
                        if tt >= 2:
                            sync.wait_ge(s_mm.sem, mm0 + tt - 1)
                        sem.dma(sync.dma_start(
                            cm[:, 0:2176],
                            bass.AP(cst["invr"], tt * 128,
                                    [[2048, 128], [128 * 2048, 17], [1, 128]])))
                        sem.dma(sync.dma_start(
                            cm[:, 2176:4352],
                            bass.AP(cst["invi"], tt * 128,
                                    [[2048, 128], [128 * 2048, 17], [1, 128]])))

                @blk.tensor
                def _(te):
                    for tt in range(16):
                        cm = cmA if tt % 2 == 0 else cmB
                        sem = s_l0 if tt % 2 == 0 else s_l1
                        te.wait_ge(sem.sem, l_base[tt % 2] + 32 * (tt // 2 + 1))
                        if tt >= 2:
                            te.wait_ge(s_ev.sem, ev0 + tt - 1)
                        bank = pf[tt % 2]
                        for ft in range(NFT):
                            te.matmul(bank[:, 0:512], cm[:, ft * 128:ft * 128 + 128],
                                      ybf[:, ft * 1024:ft * 1024 + 512],
                                      start=(ft == 0), stop=False)
                            te.matmul(bank[:, 0:512],
                                      cm[:, 2176 + ft * 128:2176 + ft * 128 + 128],
                                      ybf[:, ft * 1024 + 512:ft * 1024 + 1024],
                                      start=False, stop=(ft == NFT - 1))
                        s_mm.inc(te.drain())

                @blk.vector
                def _(v):
                    for tt in range(16):
                        v.wait_ge(s_mm.sem, mm0 + tt + 1)
                        if tt >= 2:
                            v.wait_ge(s_gs.sem, gs0 + 16 * (tt - 1))
                        v.tensor_copy(cE[:, (tt % 2) * 512:(tt % 2) * 512 + 512],
                                      pf[tt % 2][:, 0:512])
                        s_ev.inc(v.drain())

                @blk.gpsimd
                def _(g):
                    for tt in range(16):
                        g.wait_ge(s_ev.sem, ev0 + tt + 1)
                        store(g, bass.AP(y_h, tt * 128 * D + cc * 512,
                                         [[D, 128], [1, 512]]),
                              cE[:, (tt % 2) * 512:(tt % 2) * 512 + 512])

        esB.close()

    esC = contextlib.ExitStack()
    sbC = lambda nm, sh, dt: esC.enter_context(nc.sbuf_tensor(nm, sh, dt))
    cC = sbC("cCc", [128, 8192], F32)
    cD = sbC("cDc", [128, 8192], F32)
    bigR = sbC("bigRc", [128, 32768], BF16)

    _PH("P4")
    if upto >= 10:
        with nc.Block() as blk:
            ev0 = s_ev.n
            l_base = [s_l0.n, s_l1.n]

            @blk.sync
            def _(sync):
                for o in range(16):
                    if o >= 2:
                        sync.wait_ge(s_ev.sem, ev0 + o - 1)
                    sem = s_l0 if o % 2 == 0 else s_l1
                    sem.dma(sync.dma_start_transpose(
                        cA[:, (o % 2) * 2048:(o % 2) * 2048 + 2048],
                        bass.AP(y_h, o * 128, [[D, 2048], [1, 128]])))
                    sem.dma(sync.dma_start(
                        cB[:, (o % 2) * 2048:(o % 2) * 2048 + 2048],
                        bass.AP(kv_h, o * 128 * L + 2048, [[L, 128], [1, 2048]])))
                    sem.dma(sync.dma_start(
                        cF[:, (o % 2) * 2048:(o % 2) * 2048 + 2048],
                        bass.AP(qc_h, o * 128 * TOWN, [[TOWN, 128], [1, TOWN]])))

            @blk.vector
            def _(v):
                for o in range(16):
                    sem = s_l0 if o % 2 == 0 else s_l1
                    v.wait_ge(sem.sem, l_base[o % 2] + 48 * (o // 2 + 1))
                    ys = cA[:, (o % 2) * 2048:(o % 2) * 2048 + 2048]
                    kvs = cB[:, (o % 2) * 2048:(o % 2) * 2048 + 2048]
                    qcs = cF[:, (o % 2) * 2048:(o % 2) * 2048 + 2048]
                    v.tensor_tensor(cC[:, 0:2048], kvs,
                                    _bc(chv_s, 128, o * 8 + 6, 2048), OP.mult)
                    v.drain()
                    v.tensor_tensor(cC[:, 0:2048], cC[:, 0:2048], ys, OP.add)
                    v.drain()
                    v.tensor_mul(bigR[:, o * 2048:o * 2048 + 2048], cC[:, 0:2048], qcs)
                    s_ev.inc(v.drain())

        with nc.Block() as blk:
            mm0, ev0 = s_mm.n, s_ev.n
            gw_base = [s_gw0.n, s_gw1.n]
            gs0 = s_gs.n

            @blk.gpsimd
            def _(g):
                for o in range(16):
                    if o >= 2:
                        g.wait_ge(s_mm.sem, mm0 + (o - 1) * 4)
                    gsem = s_gw0 if o % 2 == 0 else s_gw1
                    gsem.dma(g.dma_start(
                        bass.AP(cA, (o % 2) * 2048, [[8192, 128], [1, 2048]]),
                        bass.AP(woT, o * 128 * 2048, [[2048, 128], [1, 2048]])))
                    if o >= 1:
                        for ci in range(4):
                            idx = (o - 1) * 4 + ci
                            g.wait_ge(s_ev.sem, ev0 + idx + 1)
                            store(g, bass.AP(attn_h, (o - 1) * 128 * TOWN + ci * 512,
                                             [[TOWN, 128], [1, 512]]),
                                  cE[:, (idx % 8) * 512:(idx % 8) * 512 + 512])
                for ci in range(4):
                    idx = 15 * 4 + ci
                    g.wait_ge(s_ev.sem, ev0 + idx + 1)
                    store(g, bass.AP(attn_h, 15 * 128 * TOWN + ci * 512,
                                     [[TOWN, 128], [1, 512]]),
                          cE[:, (idx % 8) * 512:(idx % 8) * 512 + 512])

            @blk.tensor
            def _(te):
                for o in range(16):
                    gsem = s_gw0 if o % 2 == 0 else s_gw1
                    te.wait_ge(gsem.sem, gw_base[o % 2] + 16 * (o // 2 + 1))
                    for ci in range(4):
                        idx = o * 4 + ci
                        if idx >= 4:
                            te.wait_ge(s_ev.sem, ev0 + idx - 3)
                        bank = pf[ci % 4]
                        for ht in range(16):
                            te.matmul(
                                bank[:, 0:512],
                                cA[:, (o % 2) * 2048 + ht * 128:(o % 2) * 2048 + ht * 128 + 128],
                                bigR[:, ht * 2048 + ci * 512:ht * 2048 + ci * 512 + 512],
                                start=(ht == 0), stop=(ht == 15))
                        s_mm.inc(te.drain())

            @blk.vector
            def _(v):
                for o in range(16):
                    for ci in range(4):
                        idx = o * 4 + ci
                        v.wait_ge(s_mm.sem, mm0 + idx + 1)
                        if idx >= 8:
                            v.wait_ge(s_gs.sem, gs0 + 16 * (idx - 7))
                        v.tensor_tensor(
                            cE[:, (idx % 8) * 512:(idx % 8) * 512 + 512],
                            pf[ci % 4][:, 0:512],
                            _bc(chv_s, 128, o * 8 + 3, 512), OP.add)
                        s_ev.inc(v.drain())

    def stage2_quant(src_h, dst_h, ncols, n_outer, offcnt, w2_name, norm):
        gcnt = ncols // 128
        post = norm / 127.0
        for ti in range(16):
            with nc.Block() as blk:
                mm0, ev0 = s_mm.n, s_ev.n

                @blk.sync
                def _(sync):
                    s_lx.dma(sync.dma_start_transpose(
                        cB[:, 0:ncols],
                        bass.AP(src_h, ti * 128, [[TOWN, ncols], [1, 128]])))

                @blk.vector
                def _(v):
                    v.wait_ge(s_lx.sem, s_lx.n)
                    for g in range(gcnt):
                        v.tensor_copy(
                            bass.AP(cF, g * 128, [[8192, 128], [offcnt, n_outer], [1, offcnt]]),
                            bass.AP(cB, g * offcnt, [[8192, 128], [128, n_outer], [1, offcnt]]))
                        s_ev.inc(v.drain())
                    for g in range(gcnt):
                        v.wait_ge(s_mm.sem, mm0 + 2 * g + 1)
                        pbank = pb if g % 2 == 0 else pb2
                        v.tensor_copy(cA[:, g * 128:g * 128 + 128], pbank[:, 0:128])
                        s_ev.inc(v.drain())
                        v.wait_ge(s_mm.sem, mm0 + 2 * g + 2)
                        dst = cC if g < 32 else cD
                        gg = g if g < 32 else g - 32
                        v.tensor_copy(dst[:, gg * 128:gg * 128 + 128], pf[g % 3][:, 0:128])
                        s_ev.inc(v.drain())

                @blk.tensor
                def _(te):
                    for g in range(gcnt):
                        te.wait_ge(s_ev.sem, ev0 + g + 1)
                        if g >= 2:
                            te.wait_ge(s_ev.sem, ev0 + gcnt + 2 * (g - 2) + 1)
                        pbank = pb if g % 2 == 0 else pb2
                        te.transpose(pbank[:, 0:128],
                                     cF[:, g * 128:g * 128 + 128], hm["identb"][:])
                        s_mm.inc(te.drain())
                        te.wait_ge(s_ev.sem, ev0 + gcnt + 2 * g + 1)
                        if g >= 3:
                            te.wait_ge(s_ev.sem, ev0 + gcnt + 2 * (g - 3) + 2)
                        te.matmul(pf[g % 3][:, 0:128], cA[:, g * 128:g * 128 + 128],
                                  hm[w2_name][:], start=True, stop=True)
                        s_mm.inc(te.drain())

            with nc.Block() as blk:
                ev0, ac0 = s_ev.n, s_ac.n
                half = ncols > 4096
                w1 = min(ncols, 4096)

                @blk.vector
                def _(v):
                    vhs(v, v.tensor_reduce(st_s[:, 20:21], cC[:, 0:w1], AX.X, OP.max,
                                           apply_absolute_value=True))
                    if half:
                        vhs(v, v.tensor_reduce(st_s[:, 25:26], cD[:, 0:ncols - 4096],
                                               AX.X, OP.max, apply_absolute_value=True))
                        vhs(v, v.tensor_max(st_s[:, 20:21], st_s[:, 20:21],
                                            st_s[:, 25:26]))
                    vhs(v, v.tensor_single_scalar(st_s[:, 21:22], st_s[:, 20:21],
                                                  1e-5, OP.max))
                    vhs(v, v.reciprocal(st_s[:, 22:23], st_s[:, 21:22]))
                    vhs(v, v.tensor_single_scalar(st_s[:, 23:24], st_s[:, 22:23],
                                                  127.0, OP.mult))
                    vhs(v, v.tensor_single_scalar(st_s[:, 24:25], st_s[:, 21:22],
                                                  post, OP.mult))
                    v.tensor_tensor(cC[:, 0:w1], cC[:, 0:w1],
                                    _bc(st_s, 128, 23, w1), OP.mult)
                    if half:
                        v.tensor_tensor(cD[:, 0:4096], cD[:, 0:4096],
                                        _bc(st_s, 128, 23, 4096), OP.mult)
                    s_ev.inc(v.drain())
                    v.wait_ge(s_ac.sem, ac0 + 1)
                    v.tensor_mul(cC[:, 4096:4096 + w1], cC[:, 0:w1], cF[:, 0:w1])
                    if half:
                        v.tensor_mul(cD[:, 4096:8192], cD[:, 0:4096], cF[:, 4096:8192])
                    v.drain()
                    v.tensor_copy(cC[:, 0:w1], cC[:, 4096:4096 + w1])
                    if half:
                        v.tensor_copy(cD[:, 0:4096], cD[:, 4096:8192])
                    v.drain()
                    parts = [(cC, 0, w1, cE[:, 4096:4096 + w1] if not half else cE[:, 4096:8192],
                              cF[:, 0:w1])]
                    if half:
                        parts = [(cC, 0, 4096, cE[:, 4096:8192], cF[:, 0:4096]),
                                 (cD, 0, 4096, cE[:, 4096:8192], cF[:, 4096:8192])]
                    for pi, (buf, base, w, btmp, sgn_sl) in enumerate(parts):
                        for kv_ in (64.0, 32.0, 16.0, 8.0, 4.0, 2.0, 1.0):
                            v.tensor_scalar(btmp, buf[:, 0:w], kv_ - 0.5, kv_,
                                            OP.is_ge, OP.mult)
                            v.drain()
                            v.tensor_sub(buf[:, 0:w], buf[:, 0:w], btmp)
                            v.drain()
                        v.tensor_sub(buf[:, 0:w], buf[:, 4096:4096 + w], buf[:, 0:w])
                        v.drain()
                        v.tensor_tensor(btmp, buf[:, 0:w], sgn_sl, OP.mult)
                        v.drain()
                        v.tensor_tensor(cE[:, pi * 4096:pi * 4096 + w], btmp,
                                        _bc(st_s, 128, 24, w), OP.mult)
                        v.drain()
                    s_ev.inc(v.drain())

                @blk.scalar
                def _(s):
                    s.wait_ge(s_ev.sem, ev0 + 1)
                    s.activation(cF[:, 0:w1], cC[:, 0:w1], AF.Sign)
                    if half:
                        s.activation(cF[:, 4096:8192], cD[:, 0:4096], AF.Sign)
                    s_ac.inc(s.drain())

                @blk.gpsimd
                def _(g):
                    g.wait_ge(s_ev.sem, ev0 + 2)
                    store(g, bass.AP(dst_h, ti * 128 * ncols, [[ncols, 128], [1, ncols]]),
                          cE[:, 0:ncols])

    _PH("P5a")
    if upto >= 11:
        with nc.Block() as blk:
            mm0, ev0 = s_mm.n, s_ev.n
            l_base = [s_l0.n, s_l1.n]
            gs0 = s_gs.n

            @blk.sync
            def _(sync):
                for ho in range(16):
                    for ci in range(4):
                        idx = ho * 4 + ci
                        if idx >= 2:
                            sync.wait_ge(s_mm.sem, mm0 + idx - 1)
                        sem = s_l0 if idx % 2 == 0 else s_l1
                        sem.dma(sync.dma_start_transpose(
                            cB[:, (idx % 8) * 512:(idx % 8) * 512 + 512],
                            bass.AP(xn_h, (PAD + 2048 + ci * 512) * D + ho * 128,
                                    [[D, 512], [1, 128]])))

            @blk.tensor
            def _(te):
                for ho in range(16):
                    for ci in range(4):
                        idx = ho * 4 + ci
                        sem = s_l0 if idx % 2 == 0 else s_l1
                        te.wait_ge(sem.sem, l_base[idx % 2] + 16 * (idx // 2 + 1))
                        if idx >= 4:
                            te.wait_ge(s_ev.sem, ev0 + idx - 3)
                        te.matmul(pf[ci % 4][:, 0:512], hm["h128"][:],
                                  cB[:, (idx % 8) * 512:(idx % 8) * 512 + 512],
                                  start=True, stop=True)
                        s_mm.inc(te.drain())

            @blk.vector
            def _(v):
                for ho in range(16):
                    for ci in range(4):
                        idx = ho * 4 + ci
                        v.wait_ge(s_mm.sem, mm0 + idx + 1)
                        if ho >= 2:
                            v.wait_ge(s_gs.sem, gs0 + 16 * (ho - 1))
                        v.tensor_copy(cE[:, (idx % 8) * 512:(idx % 8) * 512 + 512],
                                      pf[ci % 4][:, 0:512])
                        s_ev.inc(v.drain())

            @blk.gpsimd
            def _(g):
                for ho in range(16):
                    g.wait_ge(s_ev.sem, ev0 + ho * 4 + 4)
                    store(g, bass.AP(s1a_h, ho * 128 * TOWN, [[TOWN, 128], [1, TOWN]]),
                          cE[:, (ho % 2) * 2048:(ho % 2) * 2048 + 2048])

        stage2_quant(s1a_h, xq1t_h, 2048, 16, 8, "w2a", 1.0 / math.sqrt(2048.0))

    _PH("P5b")
    if upto >= 12:
        with nc.Block() as blk:
            @blk.sync
            def _(sync):
                for ht in range(16):
                    for ci in range(4):
                        s_lx.dma(sync.dma_start_transpose(
                            bigR[:, (ht * 4 + ci) * 512:(ht * 4 + ci) * 512 + 512],
                            bass.AP(xq1t_h, ci * 512 * D + ht * 128, [[D, 512], [1, 128]])))
                sync.wait_ge(s_lx.sem, s_lx.n)

        with nc.Block() as blk:
            mm0, ev0, ac0 = s_mm.n, s_ev.n, s_ac.n
            l_base = [s_l0.n, s_l1.n]
            gs0 = s_gs.n

            def _wsp(col):
                return bass.AP(wsc, col, [[16, 128], [0, 1]])

            def ternarize(e, o, half, mi, t0, pool=False):
                wsrc = cD[:, half + (o % 2) * 2048:half + (o % 2) * 2048 + 2048]
                wdst = cF[:, half + (o % 2) * 2048:half + (o % 2) * 2048 + 2048]
                if pool:
                    e.tensor_single_scalar(cC[:, t0:t0 + 2048], wsrc,
                                           _wsp(6 + mi), OP.is_gt)
                    e.tensor_single_scalar(cC[:, t0 + 2048:t0 + 4096], wsrc,
                                           _wsp(9 + mi), OP.is_lt)
                else:
                    e.tensor_tensor(cC[:, t0:t0 + 2048], wsrc,
                                    _bc(wsc, 16, 6 + mi, 2048), OP.is_gt)
                    e.tensor_tensor(cC[:, t0 + 2048:t0 + 4096], wsrc,
                                    _bc(wsc, 16, 9 + mi, 2048), OP.is_lt)
                e.drain()
                e.tensor_sub(cC[:, t0:t0 + 2048], cC[:, t0:t0 + 2048],
                             cC[:, t0 + 2048:t0 + 4096])
                e.drain()
                e.tensor_tensor(wdst, cC[:, t0:t0 + 2048],
                                _bc(wsc, 16, 3 + mi, 2048), OP.mult)
                e.drain()

            @blk.sync
            def _(sync):
                for o in range(64):
                    if o >= 2:
                        sync.wait_ge(s_ev.sem, ev0 + (o - 2) * 3 + 1)
                    sem = s_l0 if o % 2 == 0 else s_l1
                    sem.dma(sync.dma_start(
                        bass.AP(cD, (o % 2) * 2048, [[8192, 128], [1, 2048]]),
                        bass.AP(wgT, o * 128 * 2048, [[2048, 128], [1, 2048]])))
                    sem.dma(sync.dma_start(
                        bass.AP(cD, 4096 + (o % 2) * 2048, [[8192, 128], [1, 2048]]),
                        bass.AP(wuT, o * 128 * 2048, [[2048, 128], [1, 2048]])))

            @blk.gpsimd
            def _(g):
                for o in range(64):
                    g.wait_ge(s_ev.sem, ev0 + o * 3 + 3)
                    store(g, bass.AP(z_h, o * 128 * TOWN, [[TOWN, 128], [1, TOWN]]),
                          cE[:, 2048 + (o % 2) * 2048:2048 + (o % 2) * 2048 + 2048])

            @blk.vector
            def _(v):
                for o in range(64):
                    sem = s_l0 if o % 2 == 0 else s_l1
                    v.wait_ge(sem.sem, l_base[o % 2] + 32 * (o // 2 + 1))
                    if o >= 2:
                        v.wait_ge(s_mm.sem, mm0 + (o - 1) * 2)
                    ternarize(v, o, 0, 0, 0)
                    ternarize(v, o, 4096, 1, 4096)
                    s_ev.inc(v.drain())
                    v.wait_ge(s_mm.sem, mm0 + o * 2 + 2)
                    v.wait_ge(s_ac.sem, ac0 + (o + 1) * 4)
                    if o >= 2:
                        v.wait_ge(s_gs.sem, gs0 + 16 * (o - 1))
                    for ci in range(4):
                        v.tensor_mul(cE[:, 2048 + (o % 2) * 2048 + ci * 512:
                                        2048 + (o % 2) * 2048 + ci * 512 + 512],
                                     cE[:, ci * 512:ci * 512 + 512], pf[ci][:, 0:512])
                    s_ev.inc(v.drain())
                    s_ev.inc(v.drain())

            @blk.tensor
            def _(te):
                for o in range(64):
                    te.wait_ge(s_ev.sem, ev0 + o * 3 + 1)
                    if o >= 1:
                        te.wait_ge(s_ev.sem, ev0 + (o - 1) * 3 + 2)
                    for ht in range(16):
                        lhs = cF[:, (o % 2) * 2048 + ht * 128:(o % 2) * 2048 + ht * 128 + 128]
                        for ci in range(4):
                            te.matmul(pf[ci][:, 0:512], lhs,
                                      bigR[:, (ht * 4 + ci) * 512:(ht * 4 + ci) * 512 + 512],
                                      start=(ht == 0), stop=(ht == 15))
                    s_mm.inc(te.drain())
                    te.wait_ge(s_ac.sem, ac0 + (o + 1) * 4)
                    for ht in range(16):
                        lhs = cF[:, 4096 + (o % 2) * 2048 + ht * 128:
                                 4096 + (o % 2) * 2048 + ht * 128 + 128]
                        for ci in range(4):
                            te.matmul(pf[ci][:, 0:512], lhs,
                                      bigR[:, (ht * 4 + ci) * 512:(ht * 4 + ci) * 512 + 512],
                                      start=(ht == 0), stop=(ht == 15))
                    s_mm.inc(te.drain())

            @blk.scalar
            def _(s):
                for o in range(64):
                    s.wait_ge(s_mm.sem, mm0 + o * 2 + 1)
                    if o >= 1:
                        s.wait_ge(s_ev.sem, ev0 + (o - 1) * 3 + 2)
                    for ci in range(4):
                        s.activation(cE[:, ci * 512:ci * 512 + 512], pf[ci][:, 0:512],
                                     AF.Silu)
                    s_ac.inc(s.drain(), 4)

    _PH("P5c")
    if upto >= 13:
        with nc.Block() as blk:
            mm0, ev0 = s_mm.n, s_ev.n
            l_base = [s_l0.n, s_l1.n]
            gs0 = s_gs.n

            @blk.sync
            def _(sync):
                for ho in range(64):
                    if ho >= 2:
                        sync.wait_ge(s_mm.sem, mm0 + (ho - 1) * 4)
                    sem = s_l0 if ho % 2 == 0 else s_l1
                    sem.dma(sync.dma_start(
                        cB[:, (ho % 2) * 2048:(ho % 2) * 2048 + 2048],
                        bass.AP(z_h, ho * 128 * TOWN, [[TOWN, 128], [1, TOWN]])))

            @blk.tensor
            def _(te):
                for ho in range(64):
                    sem = s_l0 if ho % 2 == 0 else s_l1
                    te.wait_ge(sem.sem, l_base[ho % 2] + 16 * (ho // 2 + 1))
                    for ci in range(4):
                        idx = ho * 4 + ci
                        if idx >= 4:
                            te.wait_ge(s_ev.sem, ev0 + idx - 3)
                        te.matmul(pf[ci % 4][:, 0:512], hm["h128"][:],
                                  cB[:, (ho % 2) * 2048 + ci * 512:(ho % 2) * 2048 + ci * 512 + 512],
                                  start=True, stop=True)
                        s_mm.inc(te.drain())

            @blk.vector
            def _(v):
                for ho in range(64):
                    for ci in range(4):
                        idx = ho * 4 + ci
                        v.wait_ge(s_mm.sem, mm0 + idx + 1)
                        if ho >= 2:
                            v.wait_ge(s_gs.sem, gs0 + 16 * (ho - 1))
                        v.tensor_copy(cE[:, (ho % 2) * 2048 + ci * 512:
                                         (ho % 2) * 2048 + ci * 512 + 512],
                                      pf[ci % 4][:, 0:512])
                        s_ev.inc(v.drain())

            @blk.gpsimd
            def _(g):
                for ho in range(64):
                    g.wait_ge(s_ev.sem, ev0 + ho * 4 + 4)
                    store(g, bass.AP(s2a_h, ho * 128 * TOWN, [[TOWN, 128], [1, TOWN]]),
                          cE[:, (ho % 2) * 2048:(ho % 2) * 2048 + 2048])

        stage2_quant(s2a_h, xq2t_h, 8192, 64, 2, "w2b", 1.0 / math.sqrt(8192.0))

    _PH("P5e")
    if upto >= 14:
        with nc.Block() as blk:
            ev0 = s_ev.n
            l_base = [s_l0.n, s_l1.n]
            gs0 = s_gs.n

            @blk.sync
            def _(sync):
                for j in range(64):
                    if j >= 2:
                        sync.wait_ge(s_ev.sem, ev0 + j - 1)
                    sem = s_l0 if j % 2 == 0 else s_l1
                    sem.dma(sync.dma_start(
                        cC[:, (j % 2) * 2048:(j % 2) * 2048 + 2048],
                        bass.AP(wdT, j * 128 * D, [[D, 128], [1, D]])))

            @blk.vector
            def _(v):
                for j in range(64):
                    sem = s_l0 if j % 2 == 0 else s_l1
                    v.wait_ge(sem.sem, l_base[j % 2] + 16 * (j // 2 + 1))
                    wsrc = cC[:, (j % 2) * 2048:(j % 2) * 2048 + 2048]
                    v.tensor_tensor(cD[:, 0:2048], wsrc, _bc(wsc, 16, 8, 2048), OP.is_gt)
                    v.tensor_tensor(cD[:, 2048:4096], wsrc, _bc(wsc, 16, 11, 2048), OP.is_lt)
                    v.drain()
                    v.tensor_sub(cD[:, 0:2048], cD[:, 0:2048], cD[:, 2048:4096])
                    v.drain()
                    if j >= 2:
                        v.wait_ge(s_gs.sem, gs0 + 16 * (j - 1))
                    v.tensor_tensor(cE[:, (j % 2) * 2048:(j % 2) * 2048 + 2048],
                                    cD[:, 0:2048], _bc(wsc, 16, 5, 2048), OP.mult)
                    s_ev.inc(v.drain())

            @blk.gpsimd
            def _(g):
                for j in range(64):
                    g.wait_ge(s_ev.sem, ev0 + j + 1)
                    store(g, bass.AP(wdq_h, j * 128 * D, [[D, 128], [1, D]]),
                          cE[:, (j % 2) * 2048:(j % 2) * 2048 + 2048])

    _PH("P6")
    if upto >= 15:
        for c in range(4):
            with nc.Block() as blk:
                mm0, ev0 = s_mm.n, s_ev.n
                l_base = [s_l0.n, s_l1.n]
                gw_base = [s_gw0.n, s_gw1.n]
                gs0 = s_gs.n
                lx_mark = s_lx.n + 64 * 16

                @blk.sync
                def _(sync):
                    for g in range(64):
                        s_lx.dma(sync.dma_start_transpose(
                            bigR[:, g * 512:g * 512 + 512],
                            bass.AP(xq2t_h, c * 512 * DI + g * 128, [[DI, 512], [1, 128]])))
                    for d in range(16):
                        if d >= 2:
                            sync.wait_ge(s_ev.sem, ev0 + (d - 1) * 6)
                        sem = s_l0 if d % 2 == 0 else s_l1
                        sem.dma(sync.dma_start(
                            cB[:, (d % 2) * 512:(d % 2) * 512 + 512],
                            bass.AP(attn_h, d * 128 * TOWN + c * 512,
                                    [[TOWN, 128], [1, 512]])))
                        for tt in range(4):
                            sem.dma(sync.dma_start(
                                cD[:, 4096 + (d % 2) * 1024 + tt * 128:
                                     4096 + (d % 2) * 1024 + tt * 128 + 128],
                                bass.AP(x_own, (c * 512 + tt * 128) * D + d * 128,
                                        [[D, 128], [1, 128]])))

                @blk.gpsimd
                def _(g):
                    for d in range(16):
                        if d >= 2:
                            g.wait_ge(s_mm.sem, mm0 + (d - 1) * 5)
                        dst = cA if d % 2 == 0 else cE
                        gsem = s_gw0 if d % 2 == 0 else s_gw1
                        gsem.dma(g.dma_start(
                            bass.AP(dst, 0, [[8192, 128], [128, 64], [1, 128]]),
                            bass.AP(wdq_h, d * 128, [[D, 128], [128 * D, 64], [1, 128]])))
                        if d >= 1:
                            for tt in range(4):
                                g.wait_ge(s_ev.sem, ev0 + (d - 1) * 6 + 2 + tt + 1)
                                store(g, bass.AP(out_d,
                                                 (c * 512 + tt * 128) * D + (d - 1) * 128,
                                                 [[D, 128], [1, 128]]),
                                      cC[:, (((d - 1) * 4 + tt) % 8) * 128:
                                           (((d - 1) * 4 + tt) % 8) * 128 + 128])
                    for tt in range(4):
                        g.wait_ge(s_ev.sem, ev0 + 15 * 6 + 2 + tt + 1)
                        store(g, bass.AP(out_d, (c * 512 + tt * 128) * D + 15 * 128,
                                         [[D, 128], [1, 128]]),
                              cC[:, ((15 * 4 + tt) % 8) * 128:
                                   ((15 * 4 + tt) % 8) * 128 + 128])

                @blk.tensor
                def _(te):
                    te.wait_ge(s_lx.sem, lx_mark)
                    for d in range(16):
                        gsem = s_gw0 if d % 2 == 0 else s_gw1
                        te.wait_ge(gsem.sem, gw_base[d % 2] + 16 * (d // 2 + 1))
                        bank = pf[d % 2]
                        if d >= 2:
                            te.wait_ge(s_ev.sem, ev0 + (d - 1) * 6 + 2)
                        src = cA if d % 2 == 0 else cE
                        for g in range(64):
                            te.matmul(bank[:, 0:512], src[:, g * 128:g * 128 + 128],
                                      bigR[:, g * 512:g * 512 + 512],
                                      start=(g == 0), stop=(g == 63))
                        s_mm.inc(te.drain())
                        te.wait_ge(s_ev.sem, ev0 + d * 6 + 2)
                        for tt in range(4):
                            if d * 6 + 2 + tt - 2 > 0 and tt >= 2:
                                te.wait_ge(s_ev.sem, ev0 + d * 6 + 2 + tt - 1)
                            te.transpose(
                                bass.AP(pf[2 + tt % 2], 0, [[512, 128], [1, 128]]),
                                cF[:, (d % 2) * 512 + tt * 128:(d % 2) * 512 + tt * 128 + 128]
                                if False else
                                cC[:, 4096 + (d % 2) * 512 + tt * 128:
                                     4096 + (d % 2) * 512 + tt * 128 + 128],
                                identf_s[:])
                            s_mm.inc(te.drain())

                @blk.vector
                def _(v):
                    for d in range(16):
                        v.wait_ge(s_mm.sem, mm0 + d * 5 + 1)
                        sem = s_l0 if d % 2 == 0 else s_l1
                        v.wait_ge(sem.sem, l_base[d % 2] + 80 * (d // 2 + 1))
                        v.tensor_add(cC[:, 4096 + (d % 2) * 512:4096 + (d % 2) * 512 + 512],
                                     pf[d % 2][:, 0:512],
                                     cB[:, (d % 2) * 512:(d % 2) * 512 + 512])
                        s_ev.inc(v.drain())
                        s_ev.inc(v.drain())
                        for tt in range(4):
                            v.wait_ge(s_mm.sem, mm0 + d * 5 + 1 + tt + 1)
                            slot = (d * 4 + tt) % 8
                            if d * 4 + tt >= 8:
                                v.wait_ge(s_gs.sem, gs0 + 16 * (d * 4 + tt - 7))
                            v.tensor_add(
                                cC[:, slot * 128:slot * 128 + 128],
                                pf[2 + tt % 2][:, 0:128],
                                cD[:, 4096 + (d % 2) * 1024 + tt * 128:
                                     4096 + (d % 2) * 1024 + tt * 128 + 128])
                            s_ev.inc(v.drain())

        with nc.Block() as blk:
            @blk.gpsimd
            def _(g):
                g.wait_ge(s_gs.sem, s_gs.n)
                g.wait_ge(s_gw0.sem, s_gw0.n)
                g.wait_ge(s_gw1.sem, s_gw1.n)

            @blk.sync
            def _(sync):
                sync.wait_ge(s_lx.sem, s_lx.n)
                sync.wait_ge(s_l0.sem, s_l0.n)
                sync.wait_ge(s_l1.sem, s_l1.n)

    _PH(None)
    esC.close()
    es.close()
    return nc


def kernel(L0_attn_u, L0_attn_Wq, L0_attn_bq, L0_attn_Wk, L0_attn_bk,
           L0_attn_Wv, L0_attn_bv, L0_attn_convq_w, L0_attn_convq_b,
           L0_attn_convk_w, L0_attn_convk_b, L0_attn_ssm_kernel,
           L0_attn_D_skip, L0_attn_Wo, L0_attn_bo,
           L0_ffn_w_gate, L0_ffn_w_up, L0_ffn_w_down):
    x = np.ascontiguousarray(np.asarray(L0_attn_u, dtype=np.float32))
    f32 = lambda a: np.ascontiguousarray(np.asarray(a, dtype=np.float32))
    C = host_constants()

    chvec = np.zeros((D, 8), dtype=np.float32)
    chvec[:, 0] = f32(L0_attn_bq)
    chvec[:, 1] = f32(L0_attn_bk)
    chvec[:, 2] = f32(L0_attn_bv)
    chvec[:, 3] = f32(L0_attn_bo)
    chvec[:, 4] = f32(L0_attn_convq_b)
    chvec[:, 5] = f32(L0_attn_convk_b)
    chvec[:, 6] = f32(L0_attn_D_skip)
    cwvec = np.zeros((D, 8), dtype=np.float32)
    cwvec[:, 0:3] = f32(L0_attn_convq_w)[:, 0, :]
    cwvec[:, 3:6] = f32(L0_attn_convk_w)[:, 0, :]

    shared = {
        "wqT": _tile16(f32(L0_attn_Wq).T),
        "wkT": _tile16(f32(L0_attn_Wk).T),
        "wvT": _tile16(f32(L0_attn_Wv).T),
        "woT": _tile16(f32(L0_attn_Wo).T),
        "chvec": chvec, "cwvec": cwvec,
        "ssmk": f32(L0_attn_ssm_kernel),
        "wdT": np.ascontiguousarray(f32(L0_ffn_w_down).T[PERM2, :]),
    }
    shared["wgT"] = _tile16(np.ascontiguousarray(f32(L0_ffn_w_gate).T[PERM1, :]))
    shared["wuT"] = _tile16(np.ascontiguousarray(f32(L0_ffn_w_up).T[PERM1, :]))
    shared.update(C)

    nc = build_program()
    in_maps = []
    for core in range(8):
        b, h = core // 2, core % 2
        m = dict(shared)
        m["x_own"] = np.ascontiguousarray(x[b, h * 2048:(h + 1) * 2048])
        m["x_prev"] = np.ascontiguousarray(x[b, (1 - h) * 2048:(2 - h) * 2048])
        m["alpha"] = np.full((128, 1), float(h), dtype=np.float32)
        in_maps.append(m)

    res = run_bass_kernel_spmd(nc, in_maps, core_ids=list(range(8)))
    out = np.empty((4, L, D), dtype=np.float32)
    for core in range(8):
        b, h = core // 2, core % 2
        out[b, h * 2048:(h + 1) * 2048] = res.results[core]["out"]
    return out


# version 06
# speedup vs baseline: 1.6284x; 1.0152x over previous
import contextlib
import math
import numpy as np
import ml_dtypes

import concourse.bass as bass
import concourse.mybir as mybir
from concourse.bass_utils import run_bass_kernel_spmd

BFD = ml_dtypes.bfloat16
F32 = mybir.dt.float32
BF16 = mybir.dt.bfloat16
AF = mybir.ActivationFunctionType
OP = mybir.AluOpType
AX = mybir.AxisListType

D = 2048
L = 4096
TOWN = 2048
NF = 2176
NFT = 17
PAD = 512
DI = 8192

_DEBUG_TAPS = ()
_UPTO = 99


class Ctr:
    def __init__(self, sem):
        self.sem = sem
        self.n = 0

    def inc(self, inst, k=1):
        inst.then_inc(self.sem, k)
        self.n += k
        return inst

    def dma(self, inst):
        return self.inc(inst, 16)


def _hadamard(n):
    H = np.array([[1.0]])
    while H.shape[0] < n:
        H = np.block([[H, H], [H, -H]])
    return H


def _bf(x):
    return np.ascontiguousarray(np.asarray(x, dtype=np.float32).astype(BFD))


def host_constants():
    C = {}
    C["h128"] = _bf(_hadamard(128))
    C["w2a"] = _bf(np.kron(_hadamard(16), np.eye(8)))
    C["w2b"] = _bf(np.kron(_hadamard(64), np.eye(2)))
    C["identb"] = _bf(np.eye(128))
    C["identf"] = np.ascontiguousarray(np.eye(128, dtype=np.float32))
    t = np.arange(2048, dtype=np.float64)[:, None]
    f = np.arange(NF, dtype=np.float64)[None, :]
    ang = 2.0 * np.pi * t * f / 4096.0
    C["fwdr"] = _bf(np.cos(ang))
    C["fwdi"] = _bf(-np.sin(ang))
    k = np.arange(NF, dtype=np.float64)[:, None]
    tt = np.arange(2048, dtype=np.float64)[None, :]
    wk = np.where(k == 0, 1.0, 2.0)
    wk = np.where(k == 2048, 1.0, wk)
    wk = np.where(k > 2048, 0.0, wk) / 4096.0
    angi = 2.0 * np.pi * tt * k / 4096.0
    C["invr"] = _bf(wk * np.cos(angi))
    C["invi"] = _bf(-wk * np.sin(angi))
    C["sgn"] = np.ascontiguousarray(((-1.0) ** np.arange(128)).astype(np.float32)[:, None])
    return C


def _perm_fwht(n_outer, n_inner, offcnt):
    gcnt = n_inner // offcnt
    perm = np.empty(n_outer * n_inner, dtype=np.int64)
    for g in range(gcnt):
        for ho in range(n_outer):
            for off in range(offcnt):
                perm[g * 128 + ho * offcnt + off] = ho * n_inner + g * offcnt + off
    return perm


PERM1 = _perm_fwht(16, 128, 8)
PERM2 = _perm_fwht(64, 128, 2)


def _tile16(wT):
    D_, N = wT.shape
    A = wT.reshape(16, 128, N // 128, 128)
    return np.ascontiguousarray(A.transpose(2, 1, 0, 3).reshape(D_, N))



def _bc(t, pitch, col, n):
    return bass.AP(t, col, [[pitch, 128], [0, n]])

def build_program(upto=None):
    _scope_state = []
    def _PH(name):
        if _scope_state:
            pn, sid = _scope_state.pop()
            nc.leave_named_scope(pn, sid, False)
        if name:
            sid, _ = nc.enter_named_scope(name, False)
            _scope_state.append((name, sid))

    if upto is None:
        upto = _UPTO
    nc = bass.Bass("TRN2", target_bir_lowering=False)

    def scratch(name, shape, dt):
        if name in _DEBUG_TAPS:
            return nc.dram_tensor(name, shape, dt, kind="ExternalOutput")
        return nc.dram_tensor(name, shape, dt)

    x_own = nc.dram_tensor("x_own", [TOWN, D], F32, kind="ExternalInput")
    x_prev = nc.dram_tensor("x_prev", [TOWN, D], F32, kind="ExternalInput")
    alpha_d = nc.dram_tensor("alpha", [128, 1], F32, kind="ExternalInput")
    wqT = nc.dram_tensor("wqT", [D, D], F32, kind="ExternalInput")
    wkT = nc.dram_tensor("wkT", [D, D], F32, kind="ExternalInput")
    wvT = nc.dram_tensor("wvT", [D, D], F32, kind="ExternalInput")
    woT = nc.dram_tensor("woT", [D, D], F32, kind="ExternalInput")
    chvec = nc.dram_tensor("chvec", [D, 8], F32, kind="ExternalInput")
    cwvec = nc.dram_tensor("cwvec", [D, 8], F32, kind="ExternalInput")
    ssmk = nc.dram_tensor("ssmk", [D, D], F32, kind="ExternalInput")
    wgT = nc.dram_tensor("wgT", [D, DI], F32, kind="ExternalInput")
    wuT = nc.dram_tensor("wuT", [D, DI], F32, kind="ExternalInput")
    wdT = nc.dram_tensor("wdT", [DI, D], F32, kind="ExternalInput")
    cst = {}
    for nm, shape, dt in [("h128", [128, 128], BF16), ("w2a", [128, 128], BF16),
                          ("w2b", [128, 128], BF16), ("identb", [128, 128], BF16),
                          ("identf", [128, 128], F32),
                          ("fwdr", [2048, NF], BF16), ("fwdi", [2048, NF], BF16),
                          ("invr", [NF, 2048], BF16), ("invi", [NF, 2048], BF16),
                          ("sgn", [128, 1], F32)]:
        cst[nm] = nc.dram_tensor(nm, shape, dt, kind="ExternalInput")
    out_d = nc.dram_tensor("out", [TOWN, D], F32, kind="ExternalOutput")

    xn_h = scratch("xn_h", [PAD + L, D], BF16)
    k_h = scratch("k_h", [D, L], BF16)
    v_h = scratch("v_h", [D, L], BF16)
    q_h = scratch("q_h", [D, 2560], BF16)
    kv_h = scratch("kv_h", [D, L], BF16)
    qc_h = scratch("qc_h", [D, TOWN], BF16)
    ssmb_h = scratch("ssmb_h", [D, D], BF16)
    y_h = scratch("y_h", [TOWN, D], BF16)
    attn_h = scratch("attn_h", [D, TOWN], BF16)
    s1a_h = scratch("s1a_h", [D, TOWN], BF16)
    xq1t_h = scratch("xq1t_h", [TOWN, D], BF16)
    z_h = scratch("z_h", [DI, TOWN], BF16)
    s2a_h = scratch("s2a_h", [DI, TOWN], BF16)
    xq2t_h = scratch("xq2t_h", [TOWN, DI], BF16)
    wdq_h = scratch("wdq_h", [DI, D], BF16)
    scr_h = scratch("scr_h", [1, 16], F32)

    es = contextlib.ExitStack()
    sb = lambda nm, sh, dt: es.enter_context(nc.sbuf_tensor(nm, sh, dt))
    psm = lambda nm, sh, dt: es.enter_context(nc.psum_tensor(nm, sh, dt))
    mksem = lambda nm: Ctr(es.enter_context(nc.semaphore(nm)))

    cA = sb("cA", [128, 8192], BF16)
    cB = sb("cB", [128, 8192], BF16)
    cE = sb("cE", [128, 8192], BF16)
    cF = sb("cF", [128, 8192], BF16)
    junk = sb("junkb", [128, 2048], BF16)
    zpad = sb("zpad", [128, 2048], BF16)
    hm = {nm: sb("s_" + nm, [128, 128], BF16)
          for nm in ("h128", "w2a", "w2b", "identb")}
    identf_s = sb("s_identf", [128, 128], F32)
    chv_s = sb("chv_s", [128, 128], F32)
    cw_s = sb("cw_s", [128, 128], F32)
    alpha_s = sb("alpha_s", [128, 1], F32)
    sgn_s = sb("sgn_s", [128, 1], F32)
    asg_s = sb("asg_s", [128, 1], F32)
    st_s = sb("st_s", [128, 128], F32)
    wsc = sb("wsc_s", [128, 16], F32)

    pf = [psm(f"pf{i}", [128, 512], F32) for i in range(6)]
    pb = psm("pb", [128, 1024], BF16)
    pb2 = psm("pb2", [128, 1024], BF16)

    s_ld = mksem("s_ld")
    s_l0 = mksem("s_l0")
    s_l1 = mksem("s_l1")
    s_lx = mksem("s_lx")
    s_gw0 = mksem("s_gw0")
    s_gw1 = mksem("s_gw1")
    s_gs = mksem("s_gs")
    s_gd = mksem("s_gd")
    s_mm = mksem("s_mm")
    s_ev = mksem("s_ev")
    s_ac = mksem("s_ac")
    s_fr = mksem("s_fr")
    s_vh = mksem("s_vh")
    s_pc = mksem("s_pc")

    def store(g, dst_ap, src_ap):
        s_gs.dma(g.dma_start(dst_ap, src_ap))
        g.wait_ge(s_gs.sem, s_gs.n)

    def vhs(v, inst):
        s_vh.inc(v.drain())
        v.wait_ge(s_vh.sem, s_vh.n)

    with nc.Block() as blk:
        @blk.sync
        def _(sync):
            for nm in ("h128", "w2a", "w2b", "identb"):
                s_lx.dma(sync.dma_start(hm[nm][:], cst[nm].ap()))
            s_lx.dma(sync.dma_start(identf_s[:], cst["identf"].ap()))
            s_lx.dma(sync.dma_start(alpha_s[:], alpha_d.ap()))
            s_lx.dma(sync.dma_start(sgn_s[:], cst["sgn"].ap()))
            s_lx.dma(sync.dma_start(
                chv_s[:], bass.AP(chvec, 0, [[8, 128], [1024, 16], [1, 8]])))
            s_lx.dma(sync.dma_start(
                cw_s[:], bass.AP(cwvec, 0, [[8, 128], [1024, 16], [1, 8]])))

        @blk.vector
        def _(v):
            v.memset(st_s[:], 0.0)
            v.memset(wsc[:], 0.0)
            v.drain()
            v.wait_ge(s_lx.sem, s_lx.n)
            v.tensor_mul(asg_s[:], alpha_s[:], sgn_s[:])

        @blk.gpsimd
        def _(g):
            g.memset(junk[:], 0.0)
            g.memset(zpad[:], 0.0)

    esA = contextlib.ExitStack()
    sbA = lambda nm, sh, dt: esA.enter_context(nc.sbuf_tensor(nm, sh, dt))
    cC = sbA("cCa", [128, 8192], F32)
    cD = sbA("cDa", [128, 8192], F32)
    bigR = sbA("bigRa", [128, 32768], BF16)

    _PH("P1")
    if upto >= 1:
        with nc.Block() as blk:
            ev0, ac0, fr0 = s_ev.n, s_ac.n, s_fr.n
            l_base = [s_l0.n, s_l1.n]
            gs0 = s_gs.n

            @blk.sync
            def _(sync):
                for r in range(PAD // 128):
                    s_lx.dma(sync.dma_start(
                        bass.AP(xn_h, r * 128 * D, [[D, 128], [1, D]]), zpad[:]))
                for i in range(32):
                    src = x_prev if i < 16 else x_own
                    if i >= 2:
                        sync.wait_ge(s_fr.sem, fr0 + i - 1)
                    sem = s_l0 if i % 2 == 0 else s_l1
                    sem.dma(sync.dma_start(
                        cC[:, (i % 2) * 2048:(i % 2) * 2048 + 2048],
                        bass.AP(src, (i % 16) * 128 * D, [[D, 128], [1, D]])))
                sync.wait_ge(s_lx.sem, s_lx.n)

            @blk.vector
            def _(v):
                for i in range(32):
                    xt = cC[:, (i % 2) * 2048:(i % 2) * 2048 + 2048]
                    dt_ = cD[:, (i % 2) * 2048:(i % 2) * 2048 + 2048]
                    sem = s_l0 if i % 2 == 0 else s_l1
                    v.wait_ge(sem.sem, l_base[i % 2] + 16 * (i // 2 + 1))
                    vhs(v, v.reduce_sum(st_s[:, 0:1], xt, axis=AX.X))
                    vhs(v, v.tensor_single_scalar(st_s[:, 1:2], st_s[:, 0:1],
                                                  1.0 / 2048.0, OP.mult))
                    v.tensor_tensor(dt_, xt, _bc(st_s, 128, 1, 2048), OP.subtract)
                    v.drain()
                    v.tensor_mul(xt, dt_, dt_)
                    v.drain()
                    vhs(v, v.reduce_sum(st_s[:, 2:3], xt, axis=AX.X))
                    vhs(v, v.tensor_scalar(st_s[:, 3:4], st_s[:, 2:3], 1.0 / 2048.0,
                                           1e-5, OP.mult, OP.add))
                    vhs(v, v.reciprocal(st_s[:, 5:6], st_s[:, 3:4]))
                    for _ in range(4):
                        vhs(v, v.tensor_mul(st_s[:, 6:7], st_s[:, 5:6], st_s[:, 5:6]))
                        vhs(v, v.tensor_mul(st_s[:, 7:8], st_s[:, 3:4], st_s[:, 6:7]))
                        vhs(v, v.tensor_scalar(st_s[:, 8:9], st_s[:, 7:8], -0.5, 1.5,
                                               OP.mult, OP.add))
                        vhs(v, v.tensor_mul(st_s[:, 5:6], st_s[:, 5:6], st_s[:, 8:9]))
                    xo = cE[:, (i % 2) * 2048:(i % 2) * 2048 + 2048]
                    if i >= 2:
                        v.wait_ge(s_gs.sem, gs0 + 16 * (i - 1))
                    v.tensor_tensor(xo, dt_, _bc(st_s, 128, 5, 2048), OP.mult)
                    s_ev.inc(v.drain())
                    s_fr.inc(v.drain())

            @blk.gpsimd
            def _(g):
                for i in range(32):
                    g.wait_ge(s_ev.sem, ev0 + i + 1)
                    store(g, bass.AP(xn_h, (PAD + i * 128) * D, [[D, 128], [1, D]]),
                          cE[:, (i % 2) * 2048:(i % 2) * 2048 + 2048])

    _PH("P1b")
    if upto >= 2:
        with nc.Block() as blk:
            ac0 = s_ac.n
            l_base = [s_l0.n, s_l1.n]
            chunks = [(mi, m, ci) for mi, m in enumerate((wgT, wuT, wdT))
                      for ci in range(64)]

            @blk.sync
            def _(sync):
                for idx, (mi, m, ci) in enumerate(chunks):
                    if idx >= 2:
                        sync.wait_ge(s_ac.sem, ac0 + idx - 1)
                    sem = s_l0 if idx % 2 == 0 else s_l1
                    sem.dma(sync.dma_start(
                        cD[:, (idx % 2) * 2048:(idx % 2) * 2048 + 2048],
                        bass.AP(m, ci * 128 * 2048, [[2048, 128], [1, 2048]])))

            @blk.scalar
            def _(s):
                for idx, (mi, m, ci) in enumerate(chunks):
                    sem = s_l0 if idx % 2 == 0 else s_l1
                    s.wait_ge(sem.sem, l_base[idx % 2] + 16 * (idx // 2 + 1))
                    s.activation(junk[:, 0:2048],
                                 cD[:, (idx % 2) * 2048:(idx % 2) * 2048 + 2048],
                                 AF.Abs, accum_out=cC[:, idx:idx + 1])
                    s_ac.inc(s.drain())

            @blk.vector
            def _(v):
                v.wait_ge(s_ac.sem, ac0 + len(chunks))
                for mi in range(3):
                    v.reduce_sum(st_s[:, mi * 32:mi * 32 + 1], cC[:, mi * 64:mi * 64 + 64],
                                 axis=AX.X)

        with nc.Block() as blk:
            @blk.tensor
            def _(te):
                te.transpose(bass.AP(pf[0], 0, [[512, 128], [1, 128]]),
                             st_s[:, 0:128], identf_s[:])

        with nc.Block() as blk:
            ev0 = s_ev.n

            @blk.vector
            def _(v):
                v.tensor_copy(cC[:, 0:128], pf[0][:, 0:128])
                v.drain()
                vhs(v, v.reduce_sum(st_s[:, 13:14], cC[:, 0:128], axis=AX.X))
                s_ev.inc(v.drain())

            @blk.gpsimd
            def _(g):
                g.wait_ge(s_ev.sem, ev0 + 1)
                for mi in range(3):
                    store(g, bass.AP(scr_h, mi, [[1, 1], [1, 1]]),
                          st_s[mi * 32:mi * 32 + 1, 13:14])

        with nc.Block() as blk:
            @blk.sync
            def _(sync):
                for mi in range(3):
                    s_lx.dma(sync.dma_start(wsc[:, mi:mi + 1],
                                            bass.AP(scr_h, mi, [[0, 128], [1, 1]])))

            @blk.vector
            def _(v):
                v.wait_ge(s_lx.sem, s_lx.n)
                for mi in range(3):
                    vhs(v, v.tensor_scalar(wsc[:, 3 + mi:4 + mi], wsc[:, mi:mi + 1],
                                           1.0 / (D * DI), 1e-5, OP.mult, OP.max))
                    vhs(v, v.tensor_single_scalar(wsc[:, 6 + mi:7 + mi],
                                                  wsc[:, 3 + mi:4 + mi], 0.5, OP.mult))
                    vhs(v, v.tensor_single_scalar(wsc[:, 9 + mi:10 + mi],
                                                  wsc[:, 6 + mi:7 + mi], -1.0, OP.mult))

    def proj_pass(wT, dst_h, nchunk, row_base, bias_col):
        with nc.Block() as blk:
            mm0, ev0 = s_mm.n, s_ev.n
            l_base = [s_l0.n, s_l1.n]
            gw0 = s_gw0.n
            gs0 = s_gs.n

            @blk.gpsimd
            def _(g):
                for o in range(16):
                    s_gw0.dma(g.dma_start(
                        bass.AP(bigR, o * 2048, [[32768, 128], [1, 2048]]),
                        bass.AP(wT, o * 128 * 2048, [[2048, 128], [1, 2048]])))
                for ci in range(nchunk):
                    for o in range(16):
                        idx = ci * 16 + o
                        g.wait_ge(s_ev.sem, ev0 + idx + 1)
                        store(g, bass.AP(dst_h, o * 128 * (nchunk * 512) + ci * 512,
                                         [[nchunk * 512, 128], [1, 512]]),
                              cE[:, (idx % 8) * 512:(idx % 8) * 512 + 512])

            @blk.sync
            def _(sync):
                for ci in range(nchunk):
                    slot = cB if ci % 2 == 0 else cF
                    if ci >= 2:
                        sync.wait_ge(s_mm.sem, mm0 + (ci - 1) * 16)
                    sem = s_l0 if ci % 2 == 0 else s_l1
                    for ht in range(16):
                        sem.dma(sync.dma_start_transpose(
                            slot[:, ht * 512:ht * 512 + 512],
                            bass.AP(xn_h, (row_base + ci * 512) * D + ht * 128,
                                    [[D, 512], [1, 128]])))

            @blk.tensor
            def _(te):
                te.wait_ge(s_gw0.sem, s_gw0.n)
                for ci in range(nchunk):
                    slot = cB if ci % 2 == 0 else cF
                    sem = s_l0 if ci % 2 == 0 else s_l1
                    te.wait_ge(sem.sem, l_base[ci % 2] + 256 * (ci // 2 + 1))
                    for o in range(16):
                        idx = ci * 16 + o
                        if idx >= 4:
                            te.wait_ge(s_ev.sem, ev0 + idx - 3)
                        bank = pf[o % 4]
                        for ht in range(16):
                            te.matmul(bank[:, 0:512],
                                      bigR[:, (o * 16 + ht) * 128:(o * 16 + ht) * 128 + 128],
                                      slot[:, ht * 512:ht * 512 + 512],
                                      start=(ht == 0), stop=(ht == 15))
                        s_mm.inc(te.drain())

            @blk.vector
            def _(v):
                for ci in range(nchunk):
                    for o in range(16):
                        idx = ci * 16 + o
                        v.wait_ge(s_mm.sem, mm0 + idx + 1)
                        if idx >= 8:
                            v.wait_ge(s_gs.sem, gs0 + 16 * (idx - 7))
                        v.tensor_tensor(
                            cE[:, (idx % 8) * 512:(idx % 8) * 512 + 512],
                            pf[o % 4][:, 0:512],
                            _bc(chv_s, 128, o * 8 + bias_col, 512), OP.add)
                        s_ev.inc(v.drain())

    _PH("P2k")
    if upto >= 3:
        proj_pass(wkT, k_h, 8, PAD, 1)
    _PH("P2v")
    if upto >= 4:
        proj_pass(wvT, v_h, 8, PAD, 2)
    _PH("P2q")
    if upto >= 5:
        proj_pass(wqT, q_h, 5, PAD + 1536, 0)

    _PH("P2kv")
    if upto >= 6:
        with nc.Block() as blk:
            ev0 = s_ev.n
            l_base = [s_l0.n, s_l1.n]
            gs0 = s_gs.n

            @blk.sync
            def _(sync):
                for o in range(16):
                    if o >= 2:
                        sync.wait_ge(s_ev.sem, ev0 + o - 1)
                    sem = s_l0 if o % 2 == 0 else s_l1
                    sem.dma(sync.dma_start(
                        cA[:, (o % 2) * 4096:(o % 2) * 4096 + 4096],
                        bass.AP(k_h, o * 128 * L, [[L, 128], [1, L]])))
                    sem.dma(sync.dma_start(
                        cB[:, (o % 2) * 4096:(o % 2) * 4096 + 4096],
                        bass.AP(v_h, o * 128 * L, [[L, 128], [1, L]])))

            @blk.vector
            def _(v):
                for o in range(16):
                    sem = s_l0 if o % 2 == 0 else s_l1
                    v.wait_ge(sem.sem, l_base[o % 2] + 32 * (o // 2 + 1))
                    ks = cA[:, (o % 2) * 4096:(o % 2) * 4096 + 4096]
                    vs = cB[:, (o % 2) * 4096:(o % 2) * 4096 + 4096]
                    km = cA[:, (o % 2) * 4096 + 2046:(o % 2) * 4096 + 2048]
                    v.tensor_tensor(km, km, _bc(alpha_s, 1, 0, 2), OP.mult)
                    v.drain()
                    v.memset(cC[:, 0:2], 0.0)
                    v.tensor_tensor(cC[:, 2:4098], ks, _bc(cw_s, 128, o * 8 + 5, 4096), OP.mult)
                    v.drain()
                    v.tensor_tensor(cC[:, 2:4098], cC[:, 2:4098],
                                    _bc(chv_s, 128, o * 8 + 5, 4096), OP.add)
                    v.drain()
                    v.tensor_tensor(cD[:, 0:4095], ks[:, 0:4095],
                                    _bc(cw_s, 128, o * 8 + 4, 4095), OP.mult)
                    v.drain()
                    v.tensor_tensor(cC[:, 3:4098], cC[:, 3:4098], cD[:, 0:4095], OP.add)
                    v.drain()
                    v.tensor_tensor(cD[:, 0:4094], ks[:, 0:4094],
                                    _bc(cw_s, 128, o * 8 + 3, 4094), OP.mult)
                    v.drain()
                    v.tensor_tensor(cC[:, 4:4098], cC[:, 4:4098], cD[:, 0:4094], OP.add)
                    v.drain()
                    if o >= 2:
                        v.wait_ge(s_gs.sem, gs0 + 16 * (o - 1))
                    v.tensor_mul(cE[:, (o % 2) * 4096:(o % 2) * 4096 + 4096],
                                 cC[:, 2:4098], vs)
                    s_ev.inc(v.drain())

            @blk.gpsimd
            def _(g):
                for o in range(16):
                    g.wait_ge(s_ev.sem, ev0 + o + 1)
                    store(g, bass.AP(kv_h, o * 128 * L, [[L, 128], [1, L]]),
                          cE[:, (o % 2) * 4096:(o % 2) * 4096 + 4096])

    _PH("P2qc")
    if upto >= 7:
        with nc.Block() as blk:
            ev0 = s_ev.n
            l_base = [s_l0.n, s_l1.n]
            gs0 = s_gs.n

            @blk.sync
            def _(sync):
                for o in range(16):
                    if o >= 2:
                        sync.wait_ge(s_ev.sem, ev0 + o - 1)
                    sem = s_l0 if o % 2 == 0 else s_l1
                    sem.dma(sync.dma_start(
                        cA[:, (o % 2) * 2560:(o % 2) * 2560 + 2560],
                        bass.AP(q_h, o * 128 * 2560, [[2560, 128], [1, 2560]])))

            @blk.vector
            def _(v):
                for o in range(16):
                    sem = s_l0 if o % 2 == 0 else s_l1
                    v.wait_ge(sem.sem, l_base[o % 2] + 16 * (o // 2 + 1))
                    qs = cA[:, (o % 2) * 2560:(o % 2) * 2560 + 2560]
                    qm = cA[:, (o % 2) * 2560 + 510:(o % 2) * 2560 + 512]
                    v.tensor_tensor(qm, qm, _bc(alpha_s, 1, 0, 2), OP.mult)
                    v.drain()
                    v.tensor_tensor(cC[:, 0:2048], qs[:, 512:2560],
                                    _bc(cw_s, 128, o * 8 + 2, 2048), OP.mult)
                    v.drain()
                    v.tensor_tensor(cC[:, 0:2048], cC[:, 0:2048],
                                    _bc(chv_s, 128, o * 8 + 4, 2048), OP.add)
                    v.drain()
                    v.tensor_tensor(cD[:, 0:2048], qs[:, 511:2559],
                                    _bc(cw_s, 128, o * 8 + 1, 2048), OP.mult)
                    v.drain()
                    v.tensor_tensor(cC[:, 0:2048], cC[:, 0:2048], cD[:, 0:2048], OP.add)
                    v.drain()
                    v.tensor_tensor(cD[:, 0:2048], qs[:, 510:2558],
                                    _bc(cw_s, 128, o * 8 + 0, 2048), OP.mult)
                    v.drain()
                    v.tensor_tensor(cC[:, 0:2048], cC[:, 0:2048], cD[:, 0:2048], OP.add)
                    v.drain()
                    if o >= 2:
                        v.wait_ge(s_gs.sem, gs0 + 16 * (o - 1))
                    v.tensor_copy(cE[:, (o % 2) * 2048:(o % 2) * 2048 + 2048],
                                  cC[:, 0:2048])
                    s_ev.inc(v.drain())

            @blk.gpsimd
            def _(g):
                for o in range(16):
                    g.wait_ge(s_ev.sem, ev0 + o + 1)
                    store(g, bass.AP(qc_h, o * 128 * TOWN, [[TOWN, 128], [1, TOWN]]),
                          cE[:, (o % 2) * 2048:(o % 2) * 2048 + 2048])

    _PH("P2ssm")
    if upto >= 8:
        with nc.Block() as blk:
            @blk.gpsimd
            def _(g):
                for o in range(16):
                    s_gw0.dma(g.dma_start(
                        cF[:, (o % 2) * 2048:(o % 2) * 2048 + 2048],
                        bass.AP(ssmk, o * 128 * D, [[D, 128], [1, D]])))
                    g.wait_ge(s_gw0.sem, s_gw0.n)
                    store(g, bass.AP(ssmb_h, o * 128 * D, [[D, 128], [1, D]]),
                          cF[:, (o % 2) * 2048:(o % 2) * 2048 + 2048])

    esA.close()

    _PH("P3conv")
    if upto >= 9:
        esB = contextlib.ExitStack()
        sbB = lambda nm, sh, dt: esB.enter_context(nc.sbuf_tensor(nm, sh, dt))
        vcv = sbB("vcv", [128, 24576], BF16)
        ybf = sbB("ybf", [128, 17408], BF16)
        cmA = sbB("cmA", [128, 4352], BF16)
        cmB = sbB("cmB", [128, 4352], BF16)
        w32 = sbB("w32", [128, 2048], F32)

        for cc in range(4):
            with nc.Block() as blk:
                mm0, ev0 = s_mm.n, s_ev.n
                l_base = [s_l0.n, s_l1.n]
                lx_mark = s_lx.n + 48 * 16

                @blk.sync
                def _(sync):
                    for seg in range(3):
                        src = [kv_h, kv_h, ssmb_h][seg]
                        roww = L if seg < 2 else D
                        tbase = [2048, 0, 0][seg]
                        for tt in range(16):
                            s_lx.dma(sync.dma_start_transpose(
                                vcv[:, (seg * 16 + tt) * 512:(seg * 16 + tt) * 512 + 512],
                                bass.AP(src, cc * 512 * roww + tbase + tt * 128,
                                        [[roww, 512], [1, 128]])))
                    for ft in range(NFT):
                        cm = cmA if ft % 2 == 0 else cmB
                        sem = s_l0 if ft % 2 == 0 else s_l1
                        if ft >= 2:
                            sync.wait_ge(s_mm.sem, mm0 + ft - 1)
                        sem.dma(sync.dma_start(
                            cm[:, 0:2048],
                            bass.AP(cst["fwdr"], ft * 128,
                                    [[NF, 128], [128 * NF, 16], [1, 128]])))
                        sem.dma(sync.dma_start(
                            cm[:, 2048:4096],
                            bass.AP(cst["fwdi"], ft * 128,
                                    [[NF, 128], [128 * NF, 16], [1, 128]])))

                @blk.tensor
                def _(te):
                    te.wait_ge(s_lx.sem, lx_mark)
                    for ft in range(NFT):
                        cm = cmA if ft % 2 == 0 else cmB
                        sem = s_l0 if ft % 2 == 0 else s_l1
                        te.wait_ge(sem.sem, l_base[ft % 2] + 32 * (ft // 2 + 1))
                        if ft >= 1:
                            te.wait_ge(s_ev.sem, ev0 + ft)
                        for tt in range(16):
                            for mat in range(2):
                                for seg in range(3):
                                    te.matmul(
                                        pf[seg * 2 + mat][:, 0:512],
                                        cm[:, mat * 2048 + tt * 128:mat * 2048 + tt * 128 + 128],
                                        vcv[:, (seg * 16 + tt) * 512:(seg * 16 + tt) * 512 + 512],
                                        start=(tt == 0), stop=(tt == 15))
                        s_mm.inc(te.drain())

                @blk.vector
                def _(v):
                    for ft in range(NFT):
                        v.wait_ge(s_mm.sem, mm0 + ft + 1)
                        v.tensor_tensor(w32[:, 0:512], pf[2][:, 0:512],
                                        _bc(asg_s, 1, 0, 512), OP.mult)
                        v.drain()
                        v.tensor_tensor(w32[:, 0:512], w32[:, 0:512],
                                        pf[0][:, 0:512], OP.add)
                        v.tensor_tensor(w32[:, 512:1024], pf[3][:, 0:512],
                                        _bc(asg_s, 1, 0, 512), OP.mult)
                        v.drain()
                        v.tensor_tensor(w32[:, 512:1024], w32[:, 512:1024],
                                        pf[1][:, 0:512], OP.add)
                        v.drain()
                        v.tensor_mul(w32[:, 1024:1536], w32[:, 0:512], pf[4][:, 0:512])
                        v.tensor_mul(w32[:, 1536:2048], w32[:, 512:1024], pf[5][:, 0:512])
                        v.drain()
                        v.tensor_sub(ybf[:, ft * 1024:ft * 1024 + 512],
                                     w32[:, 1024:1536], w32[:, 1536:2048])
                        v.drain()
                        v.tensor_mul(w32[:, 1024:1536], w32[:, 0:512], pf[5][:, 0:512])
                        v.tensor_mul(w32[:, 1536:2048], w32[:, 512:1024], pf[4][:, 0:512])
                        v.drain()
                        v.tensor_add(ybf[:, ft * 1024 + 512:ft * 1024 + 1024],
                                     w32[:, 1024:1536], w32[:, 1536:2048])
                        s_ev.inc(v.drain())

            with nc.Block() as blk:
                mm0, ev0 = s_mm.n, s_ev.n
                l_base = [s_l0.n, s_l1.n]
                gs0 = s_gs.n

                @blk.sync
                def _(sync):
                    for tt in range(16):
                        cm = cmA if tt % 2 == 0 else cmB
                        sem = s_l0 if tt % 2 == 0 else s_l1
                        if tt >= 2:
                            sync.wait_ge(s_mm.sem, mm0 + tt - 1)
                        sem.dma(sync.dma_start(
                            cm[:, 0:2176],
                            bass.AP(cst["invr"], tt * 128,
                                    [[2048, 128], [128 * 2048, 17], [1, 128]])))
                        sem.dma(sync.dma_start(
                            cm[:, 2176:4352],
                            bass.AP(cst["invi"], tt * 128,
                                    [[2048, 128], [128 * 2048, 17], [1, 128]])))

                @blk.tensor
                def _(te):
                    for tt in range(16):
                        cm = cmA if tt % 2 == 0 else cmB
                        sem = s_l0 if tt % 2 == 0 else s_l1
                        te.wait_ge(sem.sem, l_base[tt % 2] + 32 * (tt // 2 + 1))
                        if tt >= 2:
                            te.wait_ge(s_ev.sem, ev0 + tt - 1)
                        bank = pf[tt % 2]
                        for ft in range(NFT):
                            te.matmul(bank[:, 0:512], cm[:, ft * 128:ft * 128 + 128],
                                      ybf[:, ft * 1024:ft * 1024 + 512],
                                      start=(ft == 0), stop=False)
                            te.matmul(bank[:, 0:512],
                                      cm[:, 2176 + ft * 128:2176 + ft * 128 + 128],
                                      ybf[:, ft * 1024 + 512:ft * 1024 + 1024],
                                      start=False, stop=(ft == NFT - 1))
                        s_mm.inc(te.drain())

                @blk.vector
                def _(v):
                    for tt in range(16):
                        v.wait_ge(s_mm.sem, mm0 + tt + 1)
                        if tt >= 2:
                            v.wait_ge(s_gs.sem, gs0 + 16 * (tt - 1))
                        v.tensor_copy(cE[:, (tt % 2) * 512:(tt % 2) * 512 + 512],
                                      pf[tt % 2][:, 0:512])
                        s_ev.inc(v.drain())

                @blk.gpsimd
                def _(g):
                    for tt in range(16):
                        g.wait_ge(s_ev.sem, ev0 + tt + 1)
                        store(g, bass.AP(y_h, tt * 128 * D + cc * 512,
                                         [[D, 128], [1, 512]]),
                              cE[:, (tt % 2) * 512:(tt % 2) * 512 + 512])

        esB.close()

    esC = contextlib.ExitStack()
    sbC = lambda nm, sh, dt: esC.enter_context(nc.sbuf_tensor(nm, sh, dt))
    cC = sbC("cCc", [128, 8192], F32)
    cD = sbC("cDc", [128, 8192], F32)
    bigR = sbC("bigRc", [128, 32768], BF16)

    _PH("P4")
    if upto >= 10:
        with nc.Block() as blk:
            ev0 = s_ev.n
            l_base = [s_l0.n, s_l1.n]

            @blk.sync
            def _(sync):
                for o in range(16):
                    if o >= 2:
                        sync.wait_ge(s_ev.sem, ev0 + o - 1)
                    sem = s_l0 if o % 2 == 0 else s_l1
                    sem.dma(sync.dma_start_transpose(
                        cA[:, (o % 2) * 2048:(o % 2) * 2048 + 2048],
                        bass.AP(y_h, o * 128, [[D, 2048], [1, 128]])))
                    sem.dma(sync.dma_start(
                        cB[:, (o % 2) * 2048:(o % 2) * 2048 + 2048],
                        bass.AP(kv_h, o * 128 * L + 2048, [[L, 128], [1, 2048]])))
                    sem.dma(sync.dma_start(
                        cF[:, (o % 2) * 2048:(o % 2) * 2048 + 2048],
                        bass.AP(qc_h, o * 128 * TOWN, [[TOWN, 128], [1, TOWN]])))

            @blk.vector
            def _(v):
                for o in range(16):
                    sem = s_l0 if o % 2 == 0 else s_l1
                    v.wait_ge(sem.sem, l_base[o % 2] + 48 * (o // 2 + 1))
                    ys = cA[:, (o % 2) * 2048:(o % 2) * 2048 + 2048]
                    kvs = cB[:, (o % 2) * 2048:(o % 2) * 2048 + 2048]
                    qcs = cF[:, (o % 2) * 2048:(o % 2) * 2048 + 2048]
                    v.tensor_tensor(cC[:, 0:2048], kvs,
                                    _bc(chv_s, 128, o * 8 + 6, 2048), OP.mult)
                    v.drain()
                    v.tensor_tensor(cC[:, 0:2048], cC[:, 0:2048], ys, OP.add)
                    v.drain()
                    v.tensor_mul(bigR[:, o * 2048:o * 2048 + 2048], cC[:, 0:2048], qcs)
                    s_ev.inc(v.drain())

        with nc.Block() as blk:
            mm0, ev0 = s_mm.n, s_ev.n
            gw_base = [s_gw0.n, s_gw1.n]
            gs0 = s_gs.n

            @blk.gpsimd
            def _(g):
                for o in range(16):
                    if o >= 2:
                        g.wait_ge(s_mm.sem, mm0 + (o - 1) * 4)
                    gsem = s_gw0 if o % 2 == 0 else s_gw1
                    gsem.dma(g.dma_start(
                        bass.AP(cA, (o % 2) * 2048, [[8192, 128], [1, 2048]]),
                        bass.AP(woT, o * 128 * 2048, [[2048, 128], [1, 2048]])))
                    if o >= 1:
                        for ci in range(4):
                            idx = (o - 1) * 4 + ci
                            g.wait_ge(s_ev.sem, ev0 + idx + 1)
                            store(g, bass.AP(attn_h, (o - 1) * 128 * TOWN + ci * 512,
                                             [[TOWN, 128], [1, 512]]),
                                  cE[:, (idx % 8) * 512:(idx % 8) * 512 + 512])
                for ci in range(4):
                    idx = 15 * 4 + ci
                    g.wait_ge(s_ev.sem, ev0 + idx + 1)
                    store(g, bass.AP(attn_h, 15 * 128 * TOWN + ci * 512,
                                     [[TOWN, 128], [1, 512]]),
                          cE[:, (idx % 8) * 512:(idx % 8) * 512 + 512])

            @blk.tensor
            def _(te):
                for o in range(16):
                    gsem = s_gw0 if o % 2 == 0 else s_gw1
                    te.wait_ge(gsem.sem, gw_base[o % 2] + 16 * (o // 2 + 1))
                    for ci in range(4):
                        idx = o * 4 + ci
                        if idx >= 4:
                            te.wait_ge(s_ev.sem, ev0 + idx - 3)
                        bank = pf[ci % 4]
                        for ht in range(16):
                            te.matmul(
                                bank[:, 0:512],
                                cA[:, (o % 2) * 2048 + ht * 128:(o % 2) * 2048 + ht * 128 + 128],
                                bigR[:, ht * 2048 + ci * 512:ht * 2048 + ci * 512 + 512],
                                start=(ht == 0), stop=(ht == 15))
                        s_mm.inc(te.drain())

            @blk.vector
            def _(v):
                for o in range(16):
                    for ci in range(4):
                        idx = o * 4 + ci
                        v.wait_ge(s_mm.sem, mm0 + idx + 1)
                        if idx >= 8:
                            v.wait_ge(s_gs.sem, gs0 + 16 * (idx - 7))
                        v.tensor_tensor(
                            cE[:, (idx % 8) * 512:(idx % 8) * 512 + 512],
                            pf[ci % 4][:, 0:512],
                            _bc(chv_s, 128, o * 8 + 3, 512), OP.add)
                        s_ev.inc(v.drain())

    def stage2_quant(src_h, dst_h, ncols, n_outer, offcnt, w2_name, norm):
        gcnt = ncols // 128
        post = norm / 127.0
        for ti in range(16):
            with nc.Block() as blk:
                mm0, ev0 = s_mm.n, s_ev.n

                @blk.sync
                def _(sync):
                    s_lx.dma(sync.dma_start_transpose(
                        cB[:, 0:ncols],
                        bass.AP(src_h, ti * 128, [[TOWN, ncols], [1, 128]])))

                @blk.vector
                def _(v):
                    v.wait_ge(s_lx.sem, s_lx.n)
                    for g in range(gcnt):
                        v.tensor_copy(
                            bass.AP(cF, g * 128, [[8192, 128], [offcnt, n_outer], [1, offcnt]]),
                            bass.AP(cB, g * offcnt, [[8192, 128], [128, n_outer], [1, offcnt]]))
                        s_ev.inc(v.drain())
                    for g in range(gcnt):
                        v.wait_ge(s_mm.sem, mm0 + 2 * g + 1)
                        pbank = pb if g % 2 == 0 else pb2
                        v.tensor_copy(cA[:, g * 128:g * 128 + 128], pbank[:, 0:128])
                        s_ev.inc(v.drain())
                        v.wait_ge(s_mm.sem, mm0 + 2 * g + 2)
                        dst = cC if g < 32 else cD
                        gg = g if g < 32 else g - 32
                        v.tensor_copy(dst[:, gg * 128:gg * 128 + 128], pf[g % 3][:, 0:128])
                        s_ev.inc(v.drain())

                @blk.tensor
                def _(te):
                    for g in range(gcnt):
                        te.wait_ge(s_ev.sem, ev0 + g + 1)
                        if g >= 2:
                            te.wait_ge(s_ev.sem, ev0 + gcnt + 2 * (g - 2) + 1)
                        pbank = pb if g % 2 == 0 else pb2
                        te.transpose(pbank[:, 0:128],
                                     cF[:, g * 128:g * 128 + 128], hm["identb"][:])
                        s_mm.inc(te.drain())
                        te.wait_ge(s_ev.sem, ev0 + gcnt + 2 * g + 1)
                        if g >= 3:
                            te.wait_ge(s_ev.sem, ev0 + gcnt + 2 * (g - 3) + 2)
                        te.matmul(pf[g % 3][:, 0:128], cA[:, g * 128:g * 128 + 128],
                                  hm[w2_name][:], start=True, stop=True)
                        s_mm.inc(te.drain())

            with nc.Block() as blk:
                ev0, ac0 = s_ev.n, s_ac.n
                half = ncols > 4096
                w1 = min(ncols, 4096)

                @blk.vector
                def _(v):
                    vhs(v, v.tensor_reduce(st_s[:, 20:21], cC[:, 0:w1], AX.X, OP.max,
                                           apply_absolute_value=True))
                    if half:
                        vhs(v, v.tensor_reduce(st_s[:, 25:26], cD[:, 0:ncols - 4096],
                                               AX.X, OP.max, apply_absolute_value=True))
                        vhs(v, v.tensor_max(st_s[:, 20:21], st_s[:, 20:21],
                                            st_s[:, 25:26]))
                    vhs(v, v.tensor_single_scalar(st_s[:, 21:22], st_s[:, 20:21],
                                                  1e-5, OP.max))
                    vhs(v, v.reciprocal(st_s[:, 22:23], st_s[:, 21:22]))
                    vhs(v, v.tensor_single_scalar(st_s[:, 23:24], st_s[:, 22:23],
                                                  127.0, OP.mult))
                    vhs(v, v.tensor_single_scalar(st_s[:, 24:25], st_s[:, 21:22],
                                                  post, OP.mult))
                    v.tensor_tensor(cC[:, 0:w1], cC[:, 0:w1],
                                    _bc(st_s, 128, 23, w1), OP.mult)
                    if half:
                        v.tensor_tensor(cD[:, 0:4096], cD[:, 0:4096],
                                        _bc(st_s, 128, 23, 4096), OP.mult)
                    s_ev.inc(v.drain())
                    v.wait_ge(s_ac.sem, ac0 + 1)
                    v.tensor_mul(cC[:, 4096:4096 + w1], cC[:, 0:w1], cF[:, 0:w1])
                    if half:
                        v.tensor_mul(cD[:, 4096:8192], cD[:, 0:4096], cF[:, 4096:8192])
                    v.drain()
                    v.tensor_copy(cC[:, 0:w1], cC[:, 4096:4096 + w1])
                    if half:
                        v.tensor_copy(cD[:, 0:4096], cD[:, 4096:8192])
                    v.drain()
                    parts = [(cC, 0, w1, cE[:, 4096:4096 + w1] if not half else cE[:, 4096:8192],
                              cF[:, 0:w1])]
                    if half:
                        parts = [(cC, 0, 4096, cE[:, 4096:8192], cF[:, 0:4096]),
                                 (cD, 0, 4096, cE[:, 4096:8192], cF[:, 4096:8192])]
                    for pi, (buf, base, w, btmp, sgn_sl) in enumerate(parts):
                        for kv_ in (64.0, 32.0, 16.0, 8.0, 4.0, 2.0, 1.0):
                            v.tensor_scalar(btmp, buf[:, 0:w], kv_ - 0.5, kv_,
                                            OP.is_ge, OP.mult)
                            v.drain()
                            v.tensor_sub(buf[:, 0:w], buf[:, 0:w], btmp)
                            v.drain()
                        v.tensor_sub(buf[:, 0:w], buf[:, 4096:4096 + w], buf[:, 0:w])
                        v.drain()
                        v.tensor_tensor(btmp, buf[:, 0:w], sgn_sl, OP.mult)
                        v.drain()
                        v.tensor_tensor(cE[:, pi * 4096:pi * 4096 + w], btmp,
                                        _bc(st_s, 128, 24, w), OP.mult)
                        v.drain()
                    s_ev.inc(v.drain())

                @blk.scalar
                def _(s):
                    s.wait_ge(s_ev.sem, ev0 + 1)
                    s.activation(cF[:, 0:w1], cC[:, 0:w1], AF.Sign)
                    if half:
                        s.activation(cF[:, 4096:8192], cD[:, 0:4096], AF.Sign)
                    s_ac.inc(s.drain())

                @blk.gpsimd
                def _(g):
                    g.wait_ge(s_ev.sem, ev0 + 2)
                    store(g, bass.AP(dst_h, ti * 128 * ncols, [[ncols, 128], [1, ncols]]),
                          cE[:, 0:ncols])

    _PH("P5a")
    if upto >= 11:
        with nc.Block() as blk:
            mm0, ev0 = s_mm.n, s_ev.n
            l_base = [s_l0.n, s_l1.n]
            gs0 = s_gs.n

            @blk.sync
            def _(sync):
                for ho in range(16):
                    for ci in range(4):
                        idx = ho * 4 + ci
                        if idx >= 2:
                            sync.wait_ge(s_mm.sem, mm0 + idx - 1)
                        sem = s_l0 if idx % 2 == 0 else s_l1
                        sem.dma(sync.dma_start_transpose(
                            cB[:, (idx % 8) * 512:(idx % 8) * 512 + 512],
                            bass.AP(xn_h, (PAD + 2048 + ci * 512) * D + ho * 128,
                                    [[D, 512], [1, 128]])))

            @blk.tensor
            def _(te):
                for ho in range(16):
                    for ci in range(4):
                        idx = ho * 4 + ci
                        sem = s_l0 if idx % 2 == 0 else s_l1
                        te.wait_ge(sem.sem, l_base[idx % 2] + 16 * (idx // 2 + 1))
                        if idx >= 4:
                            te.wait_ge(s_ev.sem, ev0 + idx - 3)
                        te.matmul(pf[ci % 4][:, 0:512], hm["h128"][:],
                                  cB[:, (idx % 8) * 512:(idx % 8) * 512 + 512],
                                  start=True, stop=True)
                        s_mm.inc(te.drain())

            @blk.vector
            def _(v):
                for ho in range(16):
                    for ci in range(4):
                        idx = ho * 4 + ci
                        v.wait_ge(s_mm.sem, mm0 + idx + 1)
                        if ho >= 2:
                            v.wait_ge(s_gs.sem, gs0 + 16 * (ho - 1))
                        v.tensor_copy(cE[:, (idx % 8) * 512:(idx % 8) * 512 + 512],
                                      pf[ci % 4][:, 0:512])
                        s_ev.inc(v.drain())

            @blk.gpsimd
            def _(g):
                for ho in range(16):
                    g.wait_ge(s_ev.sem, ev0 + ho * 4 + 4)
                    store(g, bass.AP(s1a_h, ho * 128 * TOWN, [[TOWN, 128], [1, TOWN]]),
                          cE[:, (ho % 2) * 2048:(ho % 2) * 2048 + 2048])

        stage2_quant(s1a_h, xq1t_h, 2048, 16, 8, "w2a", 1.0 / math.sqrt(2048.0))

    _PH("P5b")
    if upto >= 12:
        with nc.Block() as blk:
            @blk.sync
            def _(sync):
                for ht in range(16):
                    for ci in range(4):
                        s_lx.dma(sync.dma_start_transpose(
                            bigR[:, (ht * 4 + ci) * 512:(ht * 4 + ci) * 512 + 512],
                            bass.AP(xq1t_h, ci * 512 * D + ht * 128, [[D, 512], [1, 128]])))
                sync.wait_ge(s_lx.sem, s_lx.n)

        with nc.Block() as blk:
            mm0, ev0, ac0 = s_mm.n, s_ev.n, s_ac.n
            l_base = [s_l0.n, s_l1.n]
            gs0 = s_gs.n

            def _wsp(col):
                return bass.AP(wsc, col, [[16, 128], [0, 1]])

            def ternarize(e, o, half, mi, t0, pool=False):
                wsrc = cD[:, half + (o % 2) * 2048:half + (o % 2) * 2048 + 2048]
                wdst = cF[:, half + (o % 2) * 2048:half + (o % 2) * 2048 + 2048]
                if pool:
                    e.tensor_single_scalar(cC[:, t0:t0 + 2048], wsrc,
                                           _wsp(6 + mi), OP.is_gt)
                    e.tensor_single_scalar(cC[:, t0 + 2048:t0 + 4096], wsrc,
                                           _wsp(9 + mi), OP.is_lt)
                else:
                    e.tensor_tensor(cC[:, t0:t0 + 2048], wsrc,
                                    _bc(wsc, 16, 6 + mi, 2048), OP.is_gt)
                    e.tensor_tensor(cC[:, t0 + 2048:t0 + 4096], wsrc,
                                    _bc(wsc, 16, 9 + mi, 2048), OP.is_lt)
                e.drain()
                e.tensor_sub(cC[:, t0:t0 + 2048], cC[:, t0:t0 + 2048],
                             cC[:, t0 + 2048:t0 + 4096])
                e.drain()
                e.tensor_tensor(wdst, cC[:, t0:t0 + 2048],
                                _bc(wsc, 16, 3 + mi, 2048), OP.mult)
                e.drain()

            @blk.sync
            def _(sync):
                for o in range(64):
                    if o >= 2:
                        sync.wait_ge(s_ev.sem, ev0 + (o - 2) * 3 + 1)
                    sem = s_l0 if o % 2 == 0 else s_l1
                    sem.dma(sync.dma_start(
                        bass.AP(cD, (o % 2) * 2048, [[8192, 128], [1, 2048]]),
                        bass.AP(wgT, o * 128 * 2048, [[2048, 128], [1, 2048]])))
                    sem.dma(sync.dma_start(
                        bass.AP(cD, 4096 + (o % 2) * 2048, [[8192, 128], [1, 2048]]),
                        bass.AP(wuT, o * 128 * 2048, [[2048, 128], [1, 2048]])))

            @blk.gpsimd
            def _(g):
                for o in range(64):
                    g.wait_ge(s_ev.sem, ev0 + o * 3 + 3)
                    store(g, bass.AP(z_h, o * 128 * TOWN, [[TOWN, 128], [1, TOWN]]),
                          cE[:, 2048 + (o % 2) * 2048:2048 + (o % 2) * 2048 + 2048])

            @blk.vector
            def _(v):
                for o in range(64):
                    sem = s_l0 if o % 2 == 0 else s_l1
                    v.wait_ge(sem.sem, l_base[o % 2] + 32 * (o // 2 + 1))
                    if o >= 2:
                        v.wait_ge(s_mm.sem, mm0 + (o - 1) * 2)
                    ternarize(v, o, 0, 0, 0)
                    ternarize(v, o, 4096, 1, 4096)
                    s_ev.inc(v.drain())
                    v.wait_ge(s_mm.sem, mm0 + o * 2 + 2)
                    v.wait_ge(s_ac.sem, ac0 + (o + 1) * 4)
                    if o >= 2:
                        v.wait_ge(s_gs.sem, gs0 + 16 * (o - 1))
                    for ci in range(4):
                        v.tensor_mul(cE[:, 2048 + (o % 2) * 2048 + ci * 512:
                                        2048 + (o % 2) * 2048 + ci * 512 + 512],
                                     cE[:, ci * 512:ci * 512 + 512], pf[ci][:, 0:512])
                    s_ev.inc(v.drain())
                    s_ev.inc(v.drain())

            @blk.tensor
            def _(te):
                for o in range(64):
                    te.wait_ge(s_ev.sem, ev0 + o * 3 + 1)
                    if o >= 1:
                        te.wait_ge(s_ev.sem, ev0 + (o - 1) * 3 + 2)
                    for ht in range(16):
                        lhs = cF[:, (o % 2) * 2048 + ht * 128:(o % 2) * 2048 + ht * 128 + 128]
                        for ci in range(4):
                            te.matmul(pf[ci][:, 0:512], lhs,
                                      bigR[:, (ht * 4 + ci) * 512:(ht * 4 + ci) * 512 + 512],
                                      start=(ht == 0), stop=(ht == 15))
                    s_mm.inc(te.drain())
                    te.wait_ge(s_ac.sem, ac0 + (o + 1) * 4)
                    for ht in range(16):
                        lhs = cF[:, 4096 + (o % 2) * 2048 + ht * 128:
                                 4096 + (o % 2) * 2048 + ht * 128 + 128]
                        for ci in range(4):
                            te.matmul(pf[ci][:, 0:512], lhs,
                                      bigR[:, (ht * 4 + ci) * 512:(ht * 4 + ci) * 512 + 512],
                                      start=(ht == 0), stop=(ht == 15))
                    s_mm.inc(te.drain())

            @blk.scalar
            def _(s):
                for o in range(64):
                    s.wait_ge(s_mm.sem, mm0 + o * 2 + 1)
                    if o >= 1:
                        s.wait_ge(s_ev.sem, ev0 + (o - 1) * 3 + 2)
                    for ci in range(4):
                        s.activation(cE[:, ci * 512:ci * 512 + 512], pf[ci][:, 0:512],
                                     AF.Silu)
                    s_ac.inc(s.drain(), 4)

    _PH("P5c")
    if upto >= 13:
        with nc.Block() as blk:
            mm0, ev0 = s_mm.n, s_ev.n
            l_base = [s_l0.n, s_l1.n]
            gs0 = s_gs.n

            @blk.sync
            def _(sync):
                for ho in range(64):
                    if ho >= 2:
                        sync.wait_ge(s_mm.sem, mm0 + (ho - 1) * 4)
                    sem = s_l0 if ho % 2 == 0 else s_l1
                    sem.dma(sync.dma_start(
                        cB[:, (ho % 2) * 2048:(ho % 2) * 2048 + 2048],
                        bass.AP(z_h, ho * 128 * TOWN, [[TOWN, 128], [1, TOWN]])))

            @blk.tensor
            def _(te):
                for ho in range(64):
                    sem = s_l0 if ho % 2 == 0 else s_l1
                    te.wait_ge(sem.sem, l_base[ho % 2] + 16 * (ho // 2 + 1))
                    for ci in range(4):
                        idx = ho * 4 + ci
                        if idx >= 4:
                            te.wait_ge(s_ev.sem, ev0 + idx - 3)
                        te.matmul(pf[ci % 4][:, 0:512], hm["h128"][:],
                                  cB[:, (ho % 2) * 2048 + ci * 512:(ho % 2) * 2048 + ci * 512 + 512],
                                  start=True, stop=True)
                        s_mm.inc(te.drain())

            @blk.vector
            def _(v):
                for ho in range(64):
                    for ci in range(4):
                        idx = ho * 4 + ci
                        v.wait_ge(s_mm.sem, mm0 + idx + 1)
                        if ho >= 2:
                            v.wait_ge(s_gs.sem, gs0 + 16 * (ho - 1))
                        v.tensor_copy(cE[:, (ho % 2) * 2048 + ci * 512:
                                         (ho % 2) * 2048 + ci * 512 + 512],
                                      pf[ci % 4][:, 0:512])
                        s_ev.inc(v.drain())

            @blk.gpsimd
            def _(g):
                for ho in range(64):
                    g.wait_ge(s_ev.sem, ev0 + ho * 4 + 4)
                    store(g, bass.AP(s2a_h, ho * 128 * TOWN, [[TOWN, 128], [1, TOWN]]),
                          cE[:, (ho % 2) * 2048:(ho % 2) * 2048 + 2048])

        stage2_quant(s2a_h, xq2t_h, 8192, 64, 2, "w2b", 1.0 / math.sqrt(8192.0))

    _PH("P5e")
    if upto >= 14:
        with nc.Block() as blk:
            ev0 = s_ev.n
            l_base = [s_l0.n, s_l1.n]
            gs0 = s_gs.n

            @blk.sync
            def _(sync):
                for j in range(64):
                    if j >= 2:
                        sync.wait_ge(s_ev.sem, ev0 + j - 1)
                    sem = s_l0 if j % 2 == 0 else s_l1
                    sem.dma(sync.dma_start(
                        cC[:, (j % 2) * 2048:(j % 2) * 2048 + 2048],
                        bass.AP(wdT, j * 128 * D, [[D, 128], [1, D]])))

            @blk.vector
            def _(v):
                for j in range(64):
                    sem = s_l0 if j % 2 == 0 else s_l1
                    v.wait_ge(sem.sem, l_base[j % 2] + 16 * (j // 2 + 1))
                    wsrc = cC[:, (j % 2) * 2048:(j % 2) * 2048 + 2048]
                    v.tensor_tensor(cD[:, 0:2048], wsrc, _bc(wsc, 16, 8, 2048), OP.is_gt)
                    v.tensor_tensor(cD[:, 2048:4096], wsrc, _bc(wsc, 16, 11, 2048), OP.is_lt)
                    v.drain()
                    v.tensor_sub(cD[:, 0:2048], cD[:, 0:2048], cD[:, 2048:4096])
                    v.drain()
                    if j >= 2:
                        v.wait_ge(s_gs.sem, gs0 + 16 * (j - 1))
                    v.tensor_tensor(cE[:, (j % 2) * 2048:(j % 2) * 2048 + 2048],
                                    cD[:, 0:2048], _bc(wsc, 16, 5, 2048), OP.mult)
                    s_ev.inc(v.drain())

            @blk.gpsimd
            def _(g):
                for j in range(64):
                    g.wait_ge(s_ev.sem, ev0 + j + 1)
                    store(g, bass.AP(wdq_h, j * 128,
                                     [[8192, 128], [1048576, 16], [1, 128]]),
                          cE[:, (j % 2) * 2048:(j % 2) * 2048 + 2048])

    _PH("P6")
    if upto >= 15:
        for c in range(4):
            with nc.Block() as blk:
                mm0, ev0 = s_mm.n, s_ev.n
                l_base = [s_l0.n, s_l1.n]
                gw_base = [s_gw0.n, s_gw1.n]
                gs0 = s_gs.n
                lx_mark = s_lx.n + 64 * 16

                @blk.sync
                def _(sync):
                    for g in range(64):
                        s_lx.dma(sync.dma_start_transpose(
                            bigR[:, g * 512:g * 512 + 512],
                            bass.AP(xq2t_h, c * 512 * DI + g * 128, [[DI, 512], [1, 128]])))
                    for d in range(16):
                        if d >= 2:
                            sync.wait_ge(s_ev.sem, ev0 + (d - 1) * 6)
                        sem = s_l0 if d % 2 == 0 else s_l1
                        sem.dma(sync.dma_start(
                            cB[:, (d % 2) * 512:(d % 2) * 512 + 512],
                            bass.AP(attn_h, d * 128 * TOWN + c * 512,
                                    [[TOWN, 128], [1, 512]])))
                        for tt in range(4):
                            sem.dma(sync.dma_start(
                                cD[:, 4096 + (d % 2) * 1024 + tt * 128:
                                     4096 + (d % 2) * 1024 + tt * 128 + 128],
                                bass.AP(x_own, (c * 512 + tt * 128) * D + d * 128,
                                        [[D, 128], [1, 128]])))

                @blk.gpsimd
                def _(g):
                    for d in range(16):
                        if d >= 2:
                            g.wait_ge(s_mm.sem, mm0 + (d - 1) * 5)
                        dst = cA if d % 2 == 0 else cE
                        gsem = s_gw0 if d % 2 == 0 else s_gw1
                        gsem.dma(g.dma_start(
                            bass.AP(dst, 0, [[8192, 128], [1, 8192]]),
                            bass.AP(wdq_h, d * 1048576, [[8192, 128], [1, 8192]])))
                        if d >= 1:
                            for tt in range(4):
                                g.wait_ge(s_ev.sem, ev0 + (d - 1) * 6 + 2 + tt + 1)
                                store(g, bass.AP(out_d,
                                                 (c * 512 + tt * 128) * D + (d - 1) * 128,
                                                 [[D, 128], [1, 128]]),
                                      cC[:, (((d - 1) * 4 + tt) % 8) * 128:
                                           (((d - 1) * 4 + tt) % 8) * 128 + 128])
                    for tt in range(4):
                        g.wait_ge(s_ev.sem, ev0 + 15 * 6 + 2 + tt + 1)
                        store(g, bass.AP(out_d, (c * 512 + tt * 128) * D + 15 * 128,
                                         [[D, 128], [1, 128]]),
                              cC[:, ((15 * 4 + tt) % 8) * 128:
                                   ((15 * 4 + tt) % 8) * 128 + 128])

                @blk.tensor
                def _(te):
                    te.wait_ge(s_lx.sem, lx_mark)
                    for d in range(16):
                        gsem = s_gw0 if d % 2 == 0 else s_gw1
                        te.wait_ge(gsem.sem, gw_base[d % 2] + 16 * (d // 2 + 1))
                        bank = pf[d % 2]
                        if d >= 2:
                            te.wait_ge(s_ev.sem, ev0 + (d - 1) * 6 + 2)
                        src = cA if d % 2 == 0 else cE
                        for g in range(64):
                            te.matmul(bank[:, 0:512], src[:, g * 128:g * 128 + 128],
                                      bigR[:, g * 512:g * 512 + 512],
                                      start=(g == 0), stop=(g == 63))
                        s_mm.inc(te.drain())
                        te.wait_ge(s_ev.sem, ev0 + d * 6 + 2)
                        for tt in range(4):
                            if d * 6 + 2 + tt - 2 > 0 and tt >= 2:
                                te.wait_ge(s_ev.sem, ev0 + d * 6 + 2 + tt - 1)
                            te.transpose(
                                bass.AP(pf[2 + tt % 2], 0, [[512, 128], [1, 128]]),
                                cF[:, (d % 2) * 512 + tt * 128:(d % 2) * 512 + tt * 128 + 128]
                                if False else
                                cC[:, 4096 + (d % 2) * 512 + tt * 128:
                                     4096 + (d % 2) * 512 + tt * 128 + 128],
                                identf_s[:])
                            s_mm.inc(te.drain())

                @blk.vector
                def _(v):
                    for d in range(16):
                        v.wait_ge(s_mm.sem, mm0 + d * 5 + 1)
                        sem = s_l0 if d % 2 == 0 else s_l1
                        v.wait_ge(sem.sem, l_base[d % 2] + 80 * (d // 2 + 1))
                        v.tensor_add(cC[:, 4096 + (d % 2) * 512:4096 + (d % 2) * 512 + 512],
                                     pf[d % 2][:, 0:512],
                                     cB[:, (d % 2) * 512:(d % 2) * 512 + 512])
                        s_ev.inc(v.drain())
                        s_ev.inc(v.drain())
                        for tt in range(4):
                            v.wait_ge(s_mm.sem, mm0 + d * 5 + 1 + tt + 1)
                            slot = (d * 4 + tt) % 8
                            if d * 4 + tt >= 8:
                                v.wait_ge(s_gs.sem, gs0 + 16 * (d * 4 + tt - 7))
                            v.tensor_add(
                                cC[:, slot * 128:slot * 128 + 128],
                                pf[2 + tt % 2][:, 0:128],
                                cD[:, 4096 + (d % 2) * 1024 + tt * 128:
                                     4096 + (d % 2) * 1024 + tt * 128 + 128])
                            s_ev.inc(v.drain())

        with nc.Block() as blk:
            @blk.gpsimd
            def _(g):
                g.wait_ge(s_gs.sem, s_gs.n)
                g.wait_ge(s_gw0.sem, s_gw0.n)
                g.wait_ge(s_gw1.sem, s_gw1.n)

            @blk.sync
            def _(sync):
                sync.wait_ge(s_lx.sem, s_lx.n)
                sync.wait_ge(s_l0.sem, s_l0.n)
                sync.wait_ge(s_l1.sem, s_l1.n)

    _PH(None)
    esC.close()
    es.close()
    return nc


def kernel(L0_attn_u, L0_attn_Wq, L0_attn_bq, L0_attn_Wk, L0_attn_bk,
           L0_attn_Wv, L0_attn_bv, L0_attn_convq_w, L0_attn_convq_b,
           L0_attn_convk_w, L0_attn_convk_b, L0_attn_ssm_kernel,
           L0_attn_D_skip, L0_attn_Wo, L0_attn_bo,
           L0_ffn_w_gate, L0_ffn_w_up, L0_ffn_w_down):
    x = np.ascontiguousarray(np.asarray(L0_attn_u, dtype=np.float32))
    f32 = lambda a: np.ascontiguousarray(np.asarray(a, dtype=np.float32))
    C = host_constants()

    chvec = np.zeros((D, 8), dtype=np.float32)
    chvec[:, 0] = f32(L0_attn_bq)
    chvec[:, 1] = f32(L0_attn_bk)
    chvec[:, 2] = f32(L0_attn_bv)
    chvec[:, 3] = f32(L0_attn_bo)
    chvec[:, 4] = f32(L0_attn_convq_b)
    chvec[:, 5] = f32(L0_attn_convk_b)
    chvec[:, 6] = f32(L0_attn_D_skip)
    cwvec = np.zeros((D, 8), dtype=np.float32)
    cwvec[:, 0:3] = f32(L0_attn_convq_w)[:, 0, :]
    cwvec[:, 3:6] = f32(L0_attn_convk_w)[:, 0, :]

    shared = {
        "wqT": _tile16(f32(L0_attn_Wq).T),
        "wkT": _tile16(f32(L0_attn_Wk).T),
        "wvT": _tile16(f32(L0_attn_Wv).T),
        "woT": _tile16(f32(L0_attn_Wo).T),
        "chvec": chvec, "cwvec": cwvec,
        "ssmk": f32(L0_attn_ssm_kernel),
        "wdT": np.ascontiguousarray(f32(L0_ffn_w_down).T[PERM2, :]),
    }
    shared["wgT"] = _tile16(np.ascontiguousarray(f32(L0_ffn_w_gate).T[PERM1, :]))
    shared["wuT"] = _tile16(np.ascontiguousarray(f32(L0_ffn_w_up).T[PERM1, :]))
    shared.update(C)

    nc = build_program()
    in_maps = []
    for core in range(8):
        b, h = core // 2, core % 2
        m = dict(shared)
        m["x_own"] = np.ascontiguousarray(x[b, h * 2048:(h + 1) * 2048])
        m["x_prev"] = np.ascontiguousarray(x[b, (1 - h) * 2048:(2 - h) * 2048])
        m["alpha"] = np.full((128, 1), float(h), dtype=np.float32)
        in_maps.append(m)

    res = run_bass_kernel_spmd(nc, in_maps, core_ids=list(range(8)))
    out = np.empty((4, L, D), dtype=np.float32)
    for core in range(8):
        b, h = core // 2, core % 2
        out[b, h * 2048:(h + 1) * 2048] = res.results[core]["out"]
    return out


# version 07
# speedup vs baseline: 1.7422x; 1.0699x over previous
import contextlib
import math
import numpy as np
import ml_dtypes

import concourse.bass as bass
import concourse.mybir as mybir
from concourse.bass_utils import run_bass_kernel_spmd

BFD = ml_dtypes.bfloat16
F32 = mybir.dt.float32
BF16 = mybir.dt.bfloat16
AF = mybir.ActivationFunctionType
OP = mybir.AluOpType
AX = mybir.AxisListType

D = 2048
L = 4096
TOWN = 2048
NF = 2176
NFT = 17
PAD = 512
DI = 8192

_DEBUG_TAPS = ()
_UPTO = 99


class Ctr:
    def __init__(self, sem):
        self.sem = sem
        self.n = 0

    def inc(self, inst, k=1):
        inst.then_inc(self.sem, k)
        self.n += k
        return inst

    def dma(self, inst):
        return self.inc(inst, 16)


def _hadamard(n):
    H = np.array([[1.0]])
    while H.shape[0] < n:
        H = np.block([[H, H], [H, -H]])
    return H


def _bf(x):
    return np.ascontiguousarray(np.asarray(x, dtype=np.float32).astype(BFD))


def host_constants():
    C = {}
    C["h128"] = _bf(_hadamard(128))
    C["w2a"] = _bf(np.kron(_hadamard(16), np.eye(8)))
    C["w2b"] = _bf(np.kron(_hadamard(64), np.eye(2)))
    C["identb"] = _bf(np.eye(128))
    C["identf"] = np.ascontiguousarray(np.eye(128, dtype=np.float32))
    t = np.arange(2048, dtype=np.float64)[:, None]
    f = np.arange(NF, dtype=np.float64)[None, :]
    ang = 2.0 * np.pi * t * f / 4096.0
    C["fwdr"] = _bf(np.cos(ang))
    C["fwdi"] = _bf(-np.sin(ang))
    k = np.arange(NF, dtype=np.float64)[:, None]
    tt = np.arange(2048, dtype=np.float64)[None, :]
    wk = np.where(k == 0, 1.0, 2.0)
    wk = np.where(k == 2048, 1.0, wk)
    wk = np.where(k > 2048, 0.0, wk) / 4096.0
    angi = 2.0 * np.pi * tt * k / 4096.0
    C["invr"] = _bf(wk * np.cos(angi))
    C["invi"] = _bf(-wk * np.sin(angi))
    C["sgn"] = np.ascontiguousarray(((-1.0) ** np.arange(128)).astype(np.float32)[:, None])
    return C


def _perm_fwht(n_outer, n_inner, offcnt):
    gcnt = n_inner // offcnt
    perm = np.empty(n_outer * n_inner, dtype=np.int64)
    for g in range(gcnt):
        for ho in range(n_outer):
            for off in range(offcnt):
                perm[g * 128 + ho * offcnt + off] = ho * n_inner + g * offcnt + off
    return perm


PERM1 = _perm_fwht(16, 128, 8)
PERM2 = _perm_fwht(64, 128, 2)


def _tile16(wT):
    D_, N = wT.shape
    A = wT.reshape(16, 128, N // 128, 128)
    return np.ascontiguousarray(A.transpose(2, 1, 0, 3).reshape(D_, N))



def _bc(t, pitch, col, n):
    return bass.AP(t, col, [[pitch, 128], [0, n]])

def build_program(upto=None):
    _scope_state = []
    def _PH(name):
        if _scope_state:
            pn, sid = _scope_state.pop()
            nc.leave_named_scope(pn, sid, False)
        if name:
            sid, _ = nc.enter_named_scope(name, False)
            _scope_state.append((name, sid))

    if upto is None:
        upto = _UPTO
    nc = bass.Bass("TRN2", target_bir_lowering=False)

    def scratch(name, shape, dt):
        if name in _DEBUG_TAPS:
            return nc.dram_tensor(name, shape, dt, kind="ExternalOutput")
        return nc.dram_tensor(name, shape, dt)

    x_own = nc.dram_tensor("x_own", [TOWN, D], F32, kind="ExternalInput")
    x_prev = nc.dram_tensor("x_prev", [TOWN, D], F32, kind="ExternalInput")
    alpha_d = nc.dram_tensor("alpha", [128, 1], F32, kind="ExternalInput")
    wqT = nc.dram_tensor("wqT", [D, D], F32, kind="ExternalInput")
    wkT = nc.dram_tensor("wkT", [D, D], F32, kind="ExternalInput")
    wvT = nc.dram_tensor("wvT", [D, D], F32, kind="ExternalInput")
    woT = nc.dram_tensor("woT", [D, D], F32, kind="ExternalInput")
    chvec = nc.dram_tensor("chvec", [D, 8], F32, kind="ExternalInput")
    cwvec = nc.dram_tensor("cwvec", [D, 8], F32, kind="ExternalInput")
    ssmk = nc.dram_tensor("ssmk", [D, D], F32, kind="ExternalInput")
    wgT = nc.dram_tensor("wgT", [D, DI], F32, kind="ExternalInput")
    wuT = nc.dram_tensor("wuT", [D, DI], F32, kind="ExternalInput")
    wdT = nc.dram_tensor("wdT", [DI, D], F32, kind="ExternalInput")
    cst = {}
    for nm, shape, dt in [("h128", [128, 128], BF16), ("w2a", [128, 128], BF16),
                          ("w2b", [128, 128], BF16), ("identb", [128, 128], BF16),
                          ("identf", [128, 128], F32),
                          ("fwdr", [2048, NF], BF16), ("fwdi", [2048, NF], BF16),
                          ("invr", [NF, 2048], BF16), ("invi", [NF, 2048], BF16),
                          ("sgn", [128, 1], F32)]:
        cst[nm] = nc.dram_tensor(nm, shape, dt, kind="ExternalInput")
    out_d = nc.dram_tensor("out", [TOWN, D], F32, kind="ExternalOutput")

    xn_h = scratch("xn_h", [PAD + L, D], BF16)
    k_h = scratch("k_h", [D, L], BF16)
    v_h = scratch("v_h", [D, L], BF16)
    q_h = scratch("q_h", [D, 2560], BF16)
    kv_h = scratch("kv_h", [D, L], BF16)
    qc_h = scratch("qc_h", [D, TOWN], BF16)
    ssmb_h = scratch("ssmb_h", [D, D], BF16)
    y_h = scratch("y_h", [TOWN, D], BF16)
    attn_h = scratch("attn_h", [D, TOWN], BF16)
    s1a_h = scratch("s1a_h", [D, TOWN], BF16)
    xq1t_h = scratch("xq1t_h", [TOWN, D], BF16)
    z_h = scratch("z_h", [DI, TOWN], BF16)
    s2a_h = scratch("s2a_h", [DI, TOWN], BF16)
    xq2t_h = scratch("xq2t_h", [TOWN, DI], BF16)
    wdq_h = scratch("wdq_h", [DI, D], BF16)
    scr_h = scratch("scr_h", [1, 16], F32)

    es = contextlib.ExitStack()
    sb = lambda nm, sh, dt: es.enter_context(nc.sbuf_tensor(nm, sh, dt))
    psm = lambda nm, sh, dt: es.enter_context(nc.psum_tensor(nm, sh, dt))
    mksem = lambda nm: Ctr(es.enter_context(nc.semaphore(nm)))

    cA = sb("cA", [128, 8192], BF16)
    cB = sb("cB", [128, 8192], BF16)
    cE = sb("cE", [128, 8192], BF16)
    cF = sb("cF", [128, 8192], BF16)
    junk = sb("junkb", [128, 2048], BF16)
    zpad = sb("zpad", [128, 2048], BF16)
    hm = {nm: sb("s_" + nm, [128, 128], BF16)
          for nm in ("h128", "w2a", "w2b", "identb")}
    identf_s = sb("s_identf", [128, 128], F32)
    chv_s = sb("chv_s", [128, 128], F32)
    cw_s = sb("cw_s", [128, 128], F32)
    alpha_s = sb("alpha_s", [128, 1], F32)
    sgn_s = sb("sgn_s", [128, 1], F32)
    asg_s = sb("asg_s", [128, 1], F32)
    st_s = sb("st_s", [128, 128], F32)
    wsc = sb("wsc_s", [128, 16], F32)

    pf = [psm(f"pf{i}", [128, 512], F32) for i in range(6)]
    pb = psm("pb", [128, 1024], BF16)
    pb2 = psm("pb2", [128, 1024], BF16)

    s_ld = mksem("s_ld")
    s_l0 = mksem("s_l0")
    s_l1 = mksem("s_l1")
    s_lx = mksem("s_lx")
    s_gw0 = mksem("s_gw0")
    s_gw1 = mksem("s_gw1")
    s_gs = mksem("s_gs")
    s_gd = mksem("s_gd")
    s_mm = mksem("s_mm")
    s_ev = mksem("s_ev")
    s_ac = mksem("s_ac")
    s_fr = mksem("s_fr")
    s_vh = mksem("s_vh")
    s_pc = mksem("s_pc")

    def store(g, dst_ap, src_ap):
        s_gs.dma(g.dma_start(dst_ap, src_ap))
        g.wait_ge(s_gs.sem, s_gs.n)

    def vhs(v, inst):
        s_vh.inc(v.drain())
        v.wait_ge(s_vh.sem, s_vh.n)

    with nc.Block() as blk:
        @blk.sync
        def _(sync):
            for nm in ("h128", "w2a", "w2b", "identb"):
                s_lx.dma(sync.dma_start(hm[nm][:], cst[nm].ap()))
            s_lx.dma(sync.dma_start(identf_s[:], cst["identf"].ap()))
            s_lx.dma(sync.dma_start(alpha_s[:], alpha_d.ap()))
            s_lx.dma(sync.dma_start(sgn_s[:], cst["sgn"].ap()))
            s_lx.dma(sync.dma_start(
                chv_s[:], bass.AP(chvec, 0, [[8, 128], [1024, 16], [1, 8]])))
            s_lx.dma(sync.dma_start(
                cw_s[:], bass.AP(cwvec, 0, [[8, 128], [1024, 16], [1, 8]])))

        @blk.vector
        def _(v):
            v.memset(st_s[:], 0.0)
            v.memset(wsc[:], 0.0)
            v.drain()
            v.wait_ge(s_lx.sem, s_lx.n)
            v.tensor_mul(asg_s[:], alpha_s[:], sgn_s[:])

        @blk.gpsimd
        def _(g):
            g.memset(junk[:], 0.0)
            g.memset(zpad[:], 0.0)

    esA = contextlib.ExitStack()
    sbA = lambda nm, sh, dt: esA.enter_context(nc.sbuf_tensor(nm, sh, dt))
    cC = sbA("cCa", [128, 8192], F32)
    cD = sbA("cDa", [128, 8192], F32)
    bigR = sbA("bigRa", [128, 32768], BF16)

    _PH("P1")
    if upto >= 1:
        with nc.Block() as blk:
            ev0, ac0, fr0 = s_ev.n, s_ac.n, s_fr.n
            l_base = [s_l0.n, s_l1.n]
            gs0 = s_gs.n

            @blk.sync
            def _(sync):
                for r in range(PAD // 128):
                    s_lx.dma(sync.dma_start(
                        bass.AP(xn_h, r * 128 * D, [[D, 128], [1, D]]), zpad[:]))
                for i in range(32):
                    src = x_prev if i < 16 else x_own
                    if i >= 2:
                        sync.wait_ge(s_fr.sem, fr0 + i - 1)
                    sem = s_l0 if i % 2 == 0 else s_l1
                    sem.dma(sync.dma_start(
                        cC[:, (i % 2) * 2048:(i % 2) * 2048 + 2048],
                        bass.AP(src, (i % 16) * 128 * D, [[D, 128], [1, D]])))
                sync.wait_ge(s_lx.sem, s_lx.n)

            @blk.vector
            def _(v):
                for i in range(32):
                    xt = cC[:, (i % 2) * 2048:(i % 2) * 2048 + 2048]
                    dt_ = cD[:, (i % 2) * 2048:(i % 2) * 2048 + 2048]
                    sem = s_l0 if i % 2 == 0 else s_l1
                    v.wait_ge(sem.sem, l_base[i % 2] + 16 * (i // 2 + 1))
                    vhs(v, v.reduce_sum(st_s[:, 0:1], xt, axis=AX.X))
                    vhs(v, v.tensor_single_scalar(st_s[:, 1:2], st_s[:, 0:1],
                                                  1.0 / 2048.0, OP.mult))
                    v.tensor_tensor(dt_, xt, _bc(st_s, 128, 1, 2048), OP.subtract)
                    v.drain()
                    v.tensor_mul(xt, dt_, dt_)
                    v.drain()
                    vhs(v, v.reduce_sum(st_s[:, 2:3], xt, axis=AX.X))
                    vhs(v, v.tensor_scalar(st_s[:, 3:4], st_s[:, 2:3], 1.0 / 2048.0,
                                           1e-5, OP.mult, OP.add))
                    vhs(v, v.reciprocal(st_s[:, 5:6], st_s[:, 3:4]))
                    for _ in range(4):
                        vhs(v, v.tensor_mul(st_s[:, 6:7], st_s[:, 5:6], st_s[:, 5:6]))
                        vhs(v, v.tensor_mul(st_s[:, 7:8], st_s[:, 3:4], st_s[:, 6:7]))
                        vhs(v, v.tensor_scalar(st_s[:, 8:9], st_s[:, 7:8], -0.5, 1.5,
                                               OP.mult, OP.add))
                        vhs(v, v.tensor_mul(st_s[:, 5:6], st_s[:, 5:6], st_s[:, 8:9]))
                    xo = cE[:, (i % 2) * 2048:(i % 2) * 2048 + 2048]
                    if i >= 2:
                        v.wait_ge(s_gs.sem, gs0 + 16 * (i - 1))
                    v.tensor_tensor(xo, dt_, _bc(st_s, 128, 5, 2048), OP.mult)
                    s_ev.inc(v.drain())
                    s_fr.inc(v.drain())

            @blk.gpsimd
            def _(g):
                for i in range(32):
                    g.wait_ge(s_ev.sem, ev0 + i + 1)
                    store(g, bass.AP(xn_h, (PAD + i * 128) * D, [[D, 128], [1, D]]),
                          cE[:, (i % 2) * 2048:(i % 2) * 2048 + 2048])

    _PH("P1b")
    if upto >= 2:
        with nc.Block() as blk:
            ac0 = s_ac.n
            l_base = [s_l0.n, s_l1.n]
            chunks = [(mi, m, ci) for mi, m in enumerate((wgT, wuT, wdT))
                      for ci in range(64)]

            @blk.sync
            def _(sync):
                for idx, (mi, m, ci) in enumerate(chunks):
                    if idx >= 2:
                        sync.wait_ge(s_ac.sem, ac0 + idx - 1)
                    sem = s_l0 if idx % 2 == 0 else s_l1
                    sem.dma(sync.dma_start(
                        cD[:, (idx % 2) * 2048:(idx % 2) * 2048 + 2048],
                        bass.AP(m, ci * 128 * 2048, [[2048, 128], [1, 2048]])))

            @blk.scalar
            def _(s):
                for idx, (mi, m, ci) in enumerate(chunks):
                    sem = s_l0 if idx % 2 == 0 else s_l1
                    s.wait_ge(sem.sem, l_base[idx % 2] + 16 * (idx // 2 + 1))
                    s.activation(junk[:, 0:2048],
                                 cD[:, (idx % 2) * 2048:(idx % 2) * 2048 + 2048],
                                 AF.Abs, accum_out=cC[:, idx:idx + 1])
                    s_ac.inc(s.drain())

            @blk.vector
            def _(v):
                v.wait_ge(s_ac.sem, ac0 + len(chunks))
                for mi in range(3):
                    v.reduce_sum(st_s[:, mi * 32:mi * 32 + 1], cC[:, mi * 64:mi * 64 + 64],
                                 axis=AX.X)

        with nc.Block() as blk:
            @blk.tensor
            def _(te):
                te.transpose(bass.AP(pf[0], 0, [[512, 128], [1, 128]]),
                             st_s[:, 0:128], identf_s[:])

        with nc.Block() as blk:
            ev0 = s_ev.n

            @blk.vector
            def _(v):
                v.tensor_copy(cC[:, 0:128], pf[0][:, 0:128])
                v.drain()
                vhs(v, v.reduce_sum(st_s[:, 13:14], cC[:, 0:128], axis=AX.X))
                s_ev.inc(v.drain())

            @blk.gpsimd
            def _(g):
                g.wait_ge(s_ev.sem, ev0 + 1)
                for mi in range(3):
                    store(g, bass.AP(scr_h, mi, [[1, 1], [1, 1]]),
                          st_s[mi * 32:mi * 32 + 1, 13:14])

        with nc.Block() as blk:
            @blk.sync
            def _(sync):
                for mi in range(3):
                    s_lx.dma(sync.dma_start(wsc[:, mi:mi + 1],
                                            bass.AP(scr_h, mi, [[0, 128], [1, 1]])))

            @blk.vector
            def _(v):
                v.wait_ge(s_lx.sem, s_lx.n)
                for mi in range(3):
                    vhs(v, v.tensor_scalar(wsc[:, 3 + mi:4 + mi], wsc[:, mi:mi + 1],
                                           1.0 / (D * DI), 1e-5, OP.mult, OP.max))
                    vhs(v, v.tensor_single_scalar(wsc[:, 6 + mi:7 + mi],
                                                  wsc[:, 3 + mi:4 + mi], 0.5, OP.mult))
                    vhs(v, v.tensor_single_scalar(wsc[:, 9 + mi:10 + mi],
                                                  wsc[:, 6 + mi:7 + mi], -1.0, OP.mult))

    def proj_pass(wT, dst_h, nchunk, row_base, bias_col):
        with nc.Block() as blk:
            mm0, ev0 = s_mm.n, s_ev.n
            l_base = [s_l0.n, s_l1.n]
            gw0 = s_gw0.n
            gs0 = s_gs.n

            @blk.gpsimd
            def _(g):
                for o in range(16):
                    s_gw0.dma(g.dma_start(
                        bass.AP(bigR, o * 2048, [[32768, 128], [1, 2048]]),
                        bass.AP(wT, o * 128 * 2048, [[2048, 128], [1, 2048]])))
                for ci in range(nchunk):
                    for o in range(16):
                        idx = ci * 16 + o
                        g.wait_ge(s_ev.sem, ev0 + idx + 1)
                        store(g, bass.AP(dst_h, o * 128 * (nchunk * 512) + ci * 512,
                                         [[nchunk * 512, 128], [1, 512]]),
                              cE[:, (idx % 8) * 512:(idx % 8) * 512 + 512])

            @blk.sync
            def _(sync):
                for ci in range(nchunk):
                    slot = cB if ci % 2 == 0 else cF
                    if ci >= 2:
                        sync.wait_ge(s_mm.sem, mm0 + (ci - 1) * 16)
                    sem = s_l0 if ci % 2 == 0 else s_l1
                    for ht in range(16):
                        sem.dma(sync.dma_start_transpose(
                            slot[:, ht * 512:ht * 512 + 512],
                            bass.AP(xn_h, (row_base + ci * 512) * D + ht * 128,
                                    [[D, 512], [1, 128]])))

            @blk.tensor
            def _(te):
                te.wait_ge(s_gw0.sem, s_gw0.n)
                for ci in range(nchunk):
                    slot = cB if ci % 2 == 0 else cF
                    sem = s_l0 if ci % 2 == 0 else s_l1
                    te.wait_ge(sem.sem, l_base[ci % 2] + 256 * (ci // 2 + 1))
                    for o in range(16):
                        idx = ci * 16 + o
                        if idx >= 4:
                            te.wait_ge(s_ev.sem, ev0 + idx - 3)
                        bank = pf[o % 4]
                        for ht in range(16):
                            te.matmul(bank[:, 0:512],
                                      bigR[:, (o * 16 + ht) * 128:(o * 16 + ht) * 128 + 128],
                                      slot[:, ht * 512:ht * 512 + 512],
                                      start=(ht == 0), stop=(ht == 15))
                        s_mm.inc(te.drain())

            @blk.vector
            def _(v):
                for ci in range(nchunk):
                    for o in range(16):
                        idx = ci * 16 + o
                        v.wait_ge(s_mm.sem, mm0 + idx + 1)
                        if idx >= 8:
                            v.wait_ge(s_gs.sem, gs0 + 16 * (idx - 7))
                        v.tensor_tensor(
                            cE[:, (idx % 8) * 512:(idx % 8) * 512 + 512],
                            pf[o % 4][:, 0:512],
                            _bc(chv_s, 128, o * 8 + bias_col, 512), OP.add)
                        s_ev.inc(v.drain())

    _PH("P2k")
    if upto >= 3:
        proj_pass(wkT, k_h, 8, PAD, 1)
    _PH("P2v")
    if upto >= 4:
        proj_pass(wvT, v_h, 8, PAD, 2)
    _PH("P2q")
    if upto >= 5:
        proj_pass(wqT, q_h, 5, PAD + 1536, 0)

    _PH("P2kv")
    if upto >= 6:
        with nc.Block() as blk:
            ev0 = s_ev.n
            l_base = [s_l0.n, s_l1.n]
            gs0 = s_gs.n

            @blk.sync
            def _(sync):
                for o in range(16):
                    if o >= 2:
                        sync.wait_ge(s_ev.sem, ev0 + o - 1)
                    sem = s_l0 if o % 2 == 0 else s_l1
                    sem.dma(sync.dma_start(
                        cA[:, (o % 2) * 4096:(o % 2) * 4096 + 4096],
                        bass.AP(k_h, o * 128 * L, [[L, 128], [1, L]])))
                    sem.dma(sync.dma_start(
                        cB[:, (o % 2) * 4096:(o % 2) * 4096 + 4096],
                        bass.AP(v_h, o * 128 * L, [[L, 128], [1, L]])))

            @blk.vector
            def _(v):
                for o in range(16):
                    sem = s_l0 if o % 2 == 0 else s_l1
                    v.wait_ge(sem.sem, l_base[o % 2] + 32 * (o // 2 + 1))
                    ks = cA[:, (o % 2) * 4096:(o % 2) * 4096 + 4096]
                    vs = cB[:, (o % 2) * 4096:(o % 2) * 4096 + 4096]
                    km = cA[:, (o % 2) * 4096 + 2046:(o % 2) * 4096 + 2048]
                    v.tensor_tensor(km, km, _bc(alpha_s, 1, 0, 2), OP.mult)
                    v.drain()
                    v.memset(cC[:, 0:2], 0.0)
                    v.tensor_tensor(cC[:, 2:4098], ks, _bc(cw_s, 128, o * 8 + 5, 4096), OP.mult)
                    v.drain()
                    v.tensor_tensor(cC[:, 2:4098], cC[:, 2:4098],
                                    _bc(chv_s, 128, o * 8 + 5, 4096), OP.add)
                    v.drain()
                    v.tensor_tensor(cD[:, 0:4095], ks[:, 0:4095],
                                    _bc(cw_s, 128, o * 8 + 4, 4095), OP.mult)
                    v.drain()
                    v.tensor_tensor(cC[:, 3:4098], cC[:, 3:4098], cD[:, 0:4095], OP.add)
                    v.drain()
                    v.tensor_tensor(cD[:, 0:4094], ks[:, 0:4094],
                                    _bc(cw_s, 128, o * 8 + 3, 4094), OP.mult)
                    v.drain()
                    v.tensor_tensor(cC[:, 4:4098], cC[:, 4:4098], cD[:, 0:4094], OP.add)
                    v.drain()
                    if o >= 2:
                        v.wait_ge(s_gs.sem, gs0 + 16 * (o - 1))
                    v.tensor_mul(cE[:, (o % 2) * 4096:(o % 2) * 4096 + 4096],
                                 cC[:, 2:4098], vs)
                    s_ev.inc(v.drain())

            @blk.gpsimd
            def _(g):
                for o in range(16):
                    g.wait_ge(s_ev.sem, ev0 + o + 1)
                    store(g, bass.AP(kv_h, o * 128 * L, [[L, 128], [1, L]]),
                          cE[:, (o % 2) * 4096:(o % 2) * 4096 + 4096])

    _PH("P2qc")
    if upto >= 7:
        with nc.Block() as blk:
            ev0 = s_ev.n
            l_base = [s_l0.n, s_l1.n]
            gs0 = s_gs.n

            @blk.sync
            def _(sync):
                for o in range(16):
                    if o >= 2:
                        sync.wait_ge(s_ev.sem, ev0 + o - 1)
                    sem = s_l0 if o % 2 == 0 else s_l1
                    sem.dma(sync.dma_start(
                        cA[:, (o % 2) * 2560:(o % 2) * 2560 + 2560],
                        bass.AP(q_h, o * 128 * 2560, [[2560, 128], [1, 2560]])))

            @blk.vector
            def _(v):
                for o in range(16):
                    sem = s_l0 if o % 2 == 0 else s_l1
                    v.wait_ge(sem.sem, l_base[o % 2] + 16 * (o // 2 + 1))
                    qs = cA[:, (o % 2) * 2560:(o % 2) * 2560 + 2560]
                    qm = cA[:, (o % 2) * 2560 + 510:(o % 2) * 2560 + 512]
                    v.tensor_tensor(qm, qm, _bc(alpha_s, 1, 0, 2), OP.mult)
                    v.drain()
                    v.tensor_tensor(cC[:, 0:2048], qs[:, 512:2560],
                                    _bc(cw_s, 128, o * 8 + 2, 2048), OP.mult)
                    v.drain()
                    v.tensor_tensor(cC[:, 0:2048], cC[:, 0:2048],
                                    _bc(chv_s, 128, o * 8 + 4, 2048), OP.add)
                    v.drain()
                    v.tensor_tensor(cD[:, 0:2048], qs[:, 511:2559],
                                    _bc(cw_s, 128, o * 8 + 1, 2048), OP.mult)
                    v.drain()
                    v.tensor_tensor(cC[:, 0:2048], cC[:, 0:2048], cD[:, 0:2048], OP.add)
                    v.drain()
                    v.tensor_tensor(cD[:, 0:2048], qs[:, 510:2558],
                                    _bc(cw_s, 128, o * 8 + 0, 2048), OP.mult)
                    v.drain()
                    v.tensor_tensor(cC[:, 0:2048], cC[:, 0:2048], cD[:, 0:2048], OP.add)
                    v.drain()
                    if o >= 2:
                        v.wait_ge(s_gs.sem, gs0 + 16 * (o - 1))
                    v.tensor_copy(cE[:, (o % 2) * 2048:(o % 2) * 2048 + 2048],
                                  cC[:, 0:2048])
                    s_ev.inc(v.drain())

            @blk.gpsimd
            def _(g):
                for o in range(16):
                    g.wait_ge(s_ev.sem, ev0 + o + 1)
                    store(g, bass.AP(qc_h, o * 128 * TOWN, [[TOWN, 128], [1, TOWN]]),
                          cE[:, (o % 2) * 2048:(o % 2) * 2048 + 2048])

    _PH("P2ssm")
    if upto >= 8:
        with nc.Block() as blk:
            @blk.gpsimd
            def _(g):
                for o in range(16):
                    s_gw0.dma(g.dma_start(
                        cF[:, (o % 2) * 2048:(o % 2) * 2048 + 2048],
                        bass.AP(ssmk, o * 128 * D, [[D, 128], [1, D]])))
                    g.wait_ge(s_gw0.sem, s_gw0.n)
                    store(g, bass.AP(ssmb_h, o * 128 * D, [[D, 128], [1, D]]),
                          cF[:, (o % 2) * 2048:(o % 2) * 2048 + 2048])

    esA.close()

    _PH("P3conv")
    if upto >= 9:
        esB = contextlib.ExitStack()
        sbB = lambda nm, sh, dt: esB.enter_context(nc.sbuf_tensor(nm, sh, dt))
        vcv = sbB("vcv", [128, 24576], BF16)
        ybf = sbB("ybf", [128, 17408], BF16)
        cmA = sbB("cmA", [128, 4352], BF16)
        cmB = sbB("cmB", [128, 4352], BF16)
        w32 = sbB("w32", [128, 2048], F32)

        for cc in range(4):
            with nc.Block() as blk:
                mm0, ev0 = s_mm.n, s_ev.n
                l_base = [s_l0.n, s_l1.n]
                lx_mark = s_lx.n + 48 * 16

                @blk.sync
                def _(sync):
                    for seg in range(3):
                        src = [kv_h, kv_h, ssmb_h][seg]
                        roww = L if seg < 2 else D
                        tbase = [2048, 0, 0][seg]
                        for tt in range(16):
                            s_lx.dma(sync.dma_start_transpose(
                                vcv[:, (seg * 16 + tt) * 512:(seg * 16 + tt) * 512 + 512],
                                bass.AP(src, cc * 512 * roww + tbase + tt * 128,
                                        [[roww, 512], [1, 128]])))
                    for ft in range(NFT):
                        cm = cmA if ft % 2 == 0 else cmB
                        sem = s_l0 if ft % 2 == 0 else s_l1
                        if ft >= 2:
                            sync.wait_ge(s_mm.sem, mm0 + ft - 1)
                        sem.dma(sync.dma_start(
                            cm[:, 0:2048],
                            bass.AP(cst["fwdr"], ft * 128,
                                    [[NF, 128], [128 * NF, 16], [1, 128]])))
                        sem.dma(sync.dma_start(
                            cm[:, 2048:4096],
                            bass.AP(cst["fwdi"], ft * 128,
                                    [[NF, 128], [128 * NF, 16], [1, 128]])))

                @blk.tensor
                def _(te):
                    te.wait_ge(s_lx.sem, lx_mark)
                    for ft in range(NFT):
                        cm = cmA if ft % 2 == 0 else cmB
                        sem = s_l0 if ft % 2 == 0 else s_l1
                        te.wait_ge(sem.sem, l_base[ft % 2] + 32 * (ft // 2 + 1))
                        if ft >= 1:
                            te.wait_ge(s_ev.sem, ev0 + ft)
                        for tt in range(16):
                            for mat in range(2):
                                for seg in range(3):
                                    te.matmul(
                                        pf[seg * 2 + mat][:, 0:512],
                                        cm[:, mat * 2048 + tt * 128:mat * 2048 + tt * 128 + 128],
                                        vcv[:, (seg * 16 + tt) * 512:(seg * 16 + tt) * 512 + 512],
                                        start=(tt == 0), stop=(tt == 15))
                        s_mm.inc(te.drain())

                @blk.vector
                def _(v):
                    for ft in range(NFT):
                        v.wait_ge(s_mm.sem, mm0 + ft + 1)
                        v.tensor_tensor(w32[:, 0:512], pf[2][:, 0:512],
                                        _bc(asg_s, 1, 0, 512), OP.mult)
                        v.drain()
                        v.tensor_tensor(w32[:, 0:512], w32[:, 0:512],
                                        pf[0][:, 0:512], OP.add)
                        v.tensor_tensor(w32[:, 512:1024], pf[3][:, 0:512],
                                        _bc(asg_s, 1, 0, 512), OP.mult)
                        v.drain()
                        v.tensor_tensor(w32[:, 512:1024], w32[:, 512:1024],
                                        pf[1][:, 0:512], OP.add)
                        v.drain()
                        v.tensor_mul(w32[:, 1024:1536], w32[:, 0:512], pf[4][:, 0:512])
                        v.tensor_mul(w32[:, 1536:2048], w32[:, 512:1024], pf[5][:, 0:512])
                        v.drain()
                        v.tensor_sub(ybf[:, ft * 1024:ft * 1024 + 512],
                                     w32[:, 1024:1536], w32[:, 1536:2048])
                        v.drain()
                        v.tensor_mul(w32[:, 1024:1536], w32[:, 0:512], pf[5][:, 0:512])
                        v.tensor_mul(w32[:, 1536:2048], w32[:, 512:1024], pf[4][:, 0:512])
                        v.drain()
                        v.tensor_add(ybf[:, ft * 1024 + 512:ft * 1024 + 1024],
                                     w32[:, 1024:1536], w32[:, 1536:2048])
                        s_ev.inc(v.drain())

            with nc.Block() as blk:
                mm0, ev0 = s_mm.n, s_ev.n
                l_base = [s_l0.n, s_l1.n]
                gs0 = s_gs.n

                @blk.sync
                def _(sync):
                    for tt in range(16):
                        cm = cmA if tt % 2 == 0 else cmB
                        sem = s_l0 if tt % 2 == 0 else s_l1
                        if tt >= 2:
                            sync.wait_ge(s_mm.sem, mm0 + tt - 1)
                        sem.dma(sync.dma_start(
                            cm[:, 0:2176],
                            bass.AP(cst["invr"], tt * 128,
                                    [[2048, 128], [128 * 2048, 17], [1, 128]])))
                        sem.dma(sync.dma_start(
                            cm[:, 2176:4352],
                            bass.AP(cst["invi"], tt * 128,
                                    [[2048, 128], [128 * 2048, 17], [1, 128]])))

                @blk.tensor
                def _(te):
                    for tt in range(16):
                        cm = cmA if tt % 2 == 0 else cmB
                        sem = s_l0 if tt % 2 == 0 else s_l1
                        te.wait_ge(sem.sem, l_base[tt % 2] + 32 * (tt // 2 + 1))
                        if tt >= 2:
                            te.wait_ge(s_ev.sem, ev0 + tt - 1)
                        bank = pf[tt % 2]
                        for ft in range(NFT):
                            te.matmul(bank[:, 0:512], cm[:, ft * 128:ft * 128 + 128],
                                      ybf[:, ft * 1024:ft * 1024 + 512],
                                      start=(ft == 0), stop=False)
                            te.matmul(bank[:, 0:512],
                                      cm[:, 2176 + ft * 128:2176 + ft * 128 + 128],
                                      ybf[:, ft * 1024 + 512:ft * 1024 + 1024],
                                      start=False, stop=(ft == NFT - 1))
                        s_mm.inc(te.drain())

                @blk.vector
                def _(v):
                    for tt in range(16):
                        v.wait_ge(s_mm.sem, mm0 + tt + 1)
                        if tt >= 2:
                            v.wait_ge(s_gs.sem, gs0 + 16 * (tt - 1))
                        v.tensor_copy(cE[:, (tt % 2) * 512:(tt % 2) * 512 + 512],
                                      pf[tt % 2][:, 0:512])
                        s_ev.inc(v.drain())

                @blk.gpsimd
                def _(g):
                    for tt in range(16):
                        g.wait_ge(s_ev.sem, ev0 + tt + 1)
                        store(g, bass.AP(y_h, tt * 128 * D + cc * 512,
                                         [[D, 128], [1, 512]]),
                              cE[:, (tt % 2) * 512:(tt % 2) * 512 + 512])

        esB.close()

    esC = contextlib.ExitStack()
    sbC = lambda nm, sh, dt: esC.enter_context(nc.sbuf_tensor(nm, sh, dt))
    cC = sbC("cCc", [128, 8192], F32)
    cD = sbC("cDc", [128, 8192], F32)
    bigR = sbC("bigRc", [128, 32768], BF16)

    _PH("P4")
    if upto >= 10:
        with nc.Block() as blk:
            ev0 = s_ev.n
            l_base = [s_l0.n, s_l1.n]

            @blk.sync
            def _(sync):
                for o in range(16):
                    if o >= 2:
                        sync.wait_ge(s_ev.sem, ev0 + o - 1)
                    sem = s_l0 if o % 2 == 0 else s_l1
                    sem.dma(sync.dma_start_transpose(
                        cA[:, (o % 2) * 2048:(o % 2) * 2048 + 2048],
                        bass.AP(y_h, o * 128, [[D, 2048], [1, 128]])))
                    sem.dma(sync.dma_start(
                        cB[:, (o % 2) * 2048:(o % 2) * 2048 + 2048],
                        bass.AP(kv_h, o * 128 * L + 2048, [[L, 128], [1, 2048]])))
                    sem.dma(sync.dma_start(
                        cF[:, (o % 2) * 2048:(o % 2) * 2048 + 2048],
                        bass.AP(qc_h, o * 128 * TOWN, [[TOWN, 128], [1, TOWN]])))

            @blk.vector
            def _(v):
                for o in range(16):
                    sem = s_l0 if o % 2 == 0 else s_l1
                    v.wait_ge(sem.sem, l_base[o % 2] + 48 * (o // 2 + 1))
                    ys = cA[:, (o % 2) * 2048:(o % 2) * 2048 + 2048]
                    kvs = cB[:, (o % 2) * 2048:(o % 2) * 2048 + 2048]
                    qcs = cF[:, (o % 2) * 2048:(o % 2) * 2048 + 2048]
                    v.tensor_tensor(cC[:, 0:2048], kvs,
                                    _bc(chv_s, 128, o * 8 + 6, 2048), OP.mult)
                    v.drain()
                    v.tensor_tensor(cC[:, 0:2048], cC[:, 0:2048], ys, OP.add)
                    v.drain()
                    v.tensor_mul(bigR[:, o * 2048:o * 2048 + 2048], cC[:, 0:2048], qcs)
                    s_ev.inc(v.drain())

        with nc.Block() as blk:
            mm0, ev0 = s_mm.n, s_ev.n
            gw_base = [s_gw0.n, s_gw1.n]
            gs0 = s_gs.n

            @blk.gpsimd
            def _(g):
                for o in range(16):
                    if o >= 2:
                        g.wait_ge(s_mm.sem, mm0 + (o - 1) * 4)
                    gsem = s_gw0 if o % 2 == 0 else s_gw1
                    gsem.dma(g.dma_start(
                        bass.AP(cA, (o % 2) * 2048, [[8192, 128], [1, 2048]]),
                        bass.AP(woT, o * 128 * 2048, [[2048, 128], [1, 2048]])))
                    if o >= 1:
                        for ci in range(4):
                            idx = (o - 1) * 4 + ci
                            g.wait_ge(s_ev.sem, ev0 + idx + 1)
                            store(g, bass.AP(attn_h, (o - 1) * 128 * TOWN + ci * 512,
                                             [[TOWN, 128], [1, 512]]),
                                  cE[:, (idx % 8) * 512:(idx % 8) * 512 + 512])
                for ci in range(4):
                    idx = 15 * 4 + ci
                    g.wait_ge(s_ev.sem, ev0 + idx + 1)
                    store(g, bass.AP(attn_h, 15 * 128 * TOWN + ci * 512,
                                     [[TOWN, 128], [1, 512]]),
                          cE[:, (idx % 8) * 512:(idx % 8) * 512 + 512])

            @blk.tensor
            def _(te):
                for o in range(16):
                    gsem = s_gw0 if o % 2 == 0 else s_gw1
                    te.wait_ge(gsem.sem, gw_base[o % 2] + 16 * (o // 2 + 1))
                    for ci in range(4):
                        idx = o * 4 + ci
                        if idx >= 4:
                            te.wait_ge(s_ev.sem, ev0 + idx - 3)
                        bank = pf[ci % 4]
                        for ht in range(16):
                            te.matmul(
                                bank[:, 0:512],
                                cA[:, (o % 2) * 2048 + ht * 128:(o % 2) * 2048 + ht * 128 + 128],
                                bigR[:, ht * 2048 + ci * 512:ht * 2048 + ci * 512 + 512],
                                start=(ht == 0), stop=(ht == 15))
                        s_mm.inc(te.drain())

            @blk.vector
            def _(v):
                for o in range(16):
                    for ci in range(4):
                        idx = o * 4 + ci
                        v.wait_ge(s_mm.sem, mm0 + idx + 1)
                        if idx >= 8:
                            v.wait_ge(s_gs.sem, gs0 + 16 * (idx - 7))
                        v.tensor_tensor(
                            cE[:, (idx % 8) * 512:(idx % 8) * 512 + 512],
                            pf[ci % 4][:, 0:512],
                            _bc(chv_s, 128, o * 8 + 3, 512), OP.add)
                        s_ev.inc(v.drain())

    def stage2_quant(src_h, dst_h, ncols, n_outer, offcnt, w2_name, norm):
        gcnt = ncols // 128
        post = norm / 127.0
        half = ncols > 4096
        w1 = min(ncols, 4096)
        SL = [16384, 24576]

        def dma_src(sync, k):
            sem = s_l0 if k % 2 == 0 else s_l1
            sem.dma(sync.dma_start_transpose(
                bigR[:, SL[k % 2]:SL[k % 2] + ncols],
                bass.AP(src_h, k * 128, [[TOWN, ncols], [1, 128]])))
            return sem.n

        def regroup(v, k):
            for g in range(gcnt):
                v.tensor_copy(
                    bass.AP(cF, g * 128, [[8192, 128], [offcnt, n_outer], [1, offcnt]]),
                    bass.AP(bigR, SL[k % 2] + g * offcnt,
                            [[32768, 128], [128, n_outer], [1, offcnt]]))
                s_ev.inc(v.drain())

        lx_after = {}

        with nc.Block() as blk:
            mm0, ev0, ac0 = s_mm.n, s_ev.n, s_ac.n
            l0_base = s_l0.n

            @blk.sync
            def _(sync):
                lx_after[0] = dma_src(sync, 0)
                lx_after[1] = dma_src(sync, 1)

            @blk.vector
            def _(v):
                v.wait_ge(s_l0.sem, lx_after[0])
                regroup(v, 0)

            @blk.tensor
            def _(te):
                for g in range(gcnt):
                    te.wait_ge(s_ev.sem, ev0 + g + 1)
                    if g >= 2:
                        te.wait_ge(s_ac.sem, ac0 + g - 1)
                    pbank = pb if g % 2 == 0 else pb2
                    te.transpose(pbank[:, 0:128],
                                 cF[:, g * 128:g * 128 + 128], hm["identb"][:])
                    s_mm.inc(te.drain())

            @blk.scalar
            def _(s):
                for g in range(gcnt):
                    s.wait_ge(s_mm.sem, mm0 + g + 1)
                    pbank = pb if g % 2 == 0 else pb2
                    s.activation(cA[:, g * 128:g * 128 + 128], pbank[:, 0:128],
                                 AF.Copy)
                    s_ac.inc(s.drain())

        for k in range(16):
            with nc.Block() as blk:
                mm0, ev0, ac0 = s_mm.n, s_ev.n, s_ac.n
                more = k + 1 <= 15
                evR = gcnt if more else 0
                evE = evR + gcnt
                acS = ac0 + (gcnt if more else 0) + 1

                @blk.sync
                def _(sync):
                    if k + 2 <= 15:
                        lx_after[k + 2] = dma_src(sync, k + 2)

                @blk.tensor
                def _(te):
                    for g in range(gcnt):
                        if g >= 3:
                            te.wait_ge(s_ev.sem, ev0 + evR + g - 2)
                        te.matmul(pf[g % 3][:, 0:128], cA[:, g * 128:g * 128 + 128],
                                  hm[w2_name][:], start=True, stop=True)
                        s_mm.inc(te.drain())
                    if more:
                        for g in range(gcnt):
                            te.wait_ge(s_ev.sem, ev0 + g + 1)
                            if g >= 2:
                                te.wait_ge(s_ac.sem, ac0 + g - 1)
                            pbank = pb if g % 2 == 0 else pb2
                            te.transpose(pbank[:, 0:128],
                                         cF[:, g * 128:g * 128 + 128], hm["identb"][:])
                            s_mm.inc(te.drain())

                @blk.scalar
                def _(s):
                    if more:
                        for g in range(gcnt):
                            s.wait_ge(s_mm.sem, mm0 + gcnt + g + 1)
                            pbank = pb if g % 2 == 0 else pb2
                            s.activation(cA[:, g * 128:g * 128 + 128],
                                         pbank[:, 0:128], AF.Copy)
                            s_ac.inc(s.drain())
                    s.wait_ge(s_ev.sem, ev0 + evE + 1)
                    s.activation(cB[:, 0:w1], cC[:, 0:w1], AF.Sign)
                    if half:
                        s.activation(cB[:, 4096:8192], cD[:, 0:4096], AF.Sign)
                    s_ac.inc(s.drain())

                @blk.vector
                def _(v):
                    if more:
                        psem = s_l0 if (k + 1) % 2 == 0 else s_l1
                        v.wait_ge(psem.sem, lx_after[k + 1])
                        regroup(v, k + 1)
                    for g in range(gcnt):
                        v.wait_ge(s_mm.sem, mm0 + g + 1)
                        dst = cC if g < 32 else cD
                        gg = g if g < 32 else g - 32
                        v.tensor_copy(dst[:, gg * 128:gg * 128 + 128],
                                      pf[g % 3][:, 0:128])
                        s_ev.inc(v.drain())
                    vhs(v, v.tensor_reduce(st_s[:, 20:21], cC[:, 0:w1], AX.X, OP.max,
                                           apply_absolute_value=True))
                    if half:
                        vhs(v, v.tensor_reduce(st_s[:, 25:26], cD[:, 0:ncols - 4096],
                                               AX.X, OP.max, apply_absolute_value=True))
                        vhs(v, v.tensor_max(st_s[:, 20:21], st_s[:, 20:21],
                                            st_s[:, 25:26]))
                    vhs(v, v.tensor_single_scalar(st_s[:, 21:22], st_s[:, 20:21],
                                                  1e-5, OP.max))
                    vhs(v, v.reciprocal(st_s[:, 22:23], st_s[:, 21:22]))
                    vhs(v, v.tensor_single_scalar(st_s[:, 23:24], st_s[:, 22:23],
                                                  127.0, OP.mult))
                    vhs(v, v.tensor_single_scalar(st_s[:, 24:25], st_s[:, 21:22],
                                                  post, OP.mult))
                    v.tensor_tensor(cC[:, 0:w1], cC[:, 0:w1],
                                    _bc(st_s, 128, 23, w1), OP.mult)
                    if half:
                        v.tensor_tensor(cD[:, 0:4096], cD[:, 0:4096],
                                        _bc(st_s, 128, 23, 4096), OP.mult)
                    s_ev.inc(v.drain())
                    v.wait_ge(s_ac.sem, acS)
                    v.tensor_mul(cC[:, 4096:4096 + w1], cC[:, 0:w1], cB[:, 0:w1])
                    if half:
                        v.tensor_mul(cD[:, 4096:8192], cD[:, 0:4096], cB[:, 4096:8192])
                    v.drain()
                    v.tensor_copy(cC[:, 0:w1], cC[:, 4096:4096 + w1])
                    if half:
                        v.tensor_copy(cD[:, 0:4096], cD[:, 4096:8192])
                    v.drain()
                    parts = [(cC, 0, w1, cE[:, 4096:4096 + w1] if not half else cE[:, 4096:8192],
                              cB[:, 0:w1])]
                    if half:
                        parts = [(cC, 0, 4096, cE[:, 4096:8192], cB[:, 0:4096]),
                                 (cD, 0, 4096, cE[:, 4096:8192], cB[:, 4096:8192])]
                    for pi, (buf, base, w, btmp, sgn_sl) in enumerate(parts):
                        for kv_ in (64.0, 32.0, 16.0, 8.0, 4.0, 2.0, 1.0):
                            v.tensor_scalar(btmp, buf[:, 0:w], kv_ - 0.5, kv_,
                                            OP.is_ge, OP.mult)
                            v.drain()
                            v.tensor_sub(buf[:, 0:w], buf[:, 0:w], btmp)
                            v.drain()
                        v.tensor_sub(buf[:, 0:w], buf[:, 4096:4096 + w], buf[:, 0:w])
                        v.drain()
                        v.tensor_tensor(btmp, buf[:, 0:w], sgn_sl, OP.mult)
                        v.drain()
                        v.tensor_tensor(cE[:, pi * 4096:pi * 4096 + w], btmp,
                                        _bc(st_s, 128, 24, w), OP.mult)
                        v.drain()
                    s_ev.inc(v.drain())

                @blk.gpsimd
                def _(g):
                    g.wait_ge(s_ev.sem, ev0 + evE + 2)
                    store(g, bass.AP(dst_h, k * 128 * ncols, [[ncols, 128], [1, ncols]]),
                          cE[:, 0:ncols])

    _PH("P5a")
    if upto >= 11:
        with nc.Block() as blk:
            mm0, ev0 = s_mm.n, s_ev.n
            l_base = [s_l0.n, s_l1.n]
            gs0 = s_gs.n

            @blk.sync
            def _(sync):
                for ho in range(16):
                    for ci in range(4):
                        idx = ho * 4 + ci
                        if idx >= 2:
                            sync.wait_ge(s_mm.sem, mm0 + idx - 1)
                        sem = s_l0 if idx % 2 == 0 else s_l1
                        sem.dma(sync.dma_start_transpose(
                            cB[:, (idx % 8) * 512:(idx % 8) * 512 + 512],
                            bass.AP(xn_h, (PAD + 2048 + ci * 512) * D + ho * 128,
                                    [[D, 512], [1, 128]])))

            @blk.tensor
            def _(te):
                for ho in range(16):
                    for ci in range(4):
                        idx = ho * 4 + ci
                        sem = s_l0 if idx % 2 == 0 else s_l1
                        te.wait_ge(sem.sem, l_base[idx % 2] + 16 * (idx // 2 + 1))
                        if idx >= 4:
                            te.wait_ge(s_ev.sem, ev0 + idx - 3)
                        te.matmul(pf[ci % 4][:, 0:512], hm["h128"][:],
                                  cB[:, (idx % 8) * 512:(idx % 8) * 512 + 512],
                                  start=True, stop=True)
                        s_mm.inc(te.drain())

            @blk.vector
            def _(v):
                for ho in range(16):
                    for ci in range(4):
                        idx = ho * 4 + ci
                        v.wait_ge(s_mm.sem, mm0 + idx + 1)
                        if ho >= 2:
                            v.wait_ge(s_gs.sem, gs0 + 16 * (ho - 1))
                        v.tensor_copy(cE[:, (idx % 8) * 512:(idx % 8) * 512 + 512],
                                      pf[ci % 4][:, 0:512])
                        s_ev.inc(v.drain())

            @blk.gpsimd
            def _(g):
                for ho in range(16):
                    g.wait_ge(s_ev.sem, ev0 + ho * 4 + 4)
                    store(g, bass.AP(s1a_h, ho * 128 * TOWN, [[TOWN, 128], [1, TOWN]]),
                          cE[:, (ho % 2) * 2048:(ho % 2) * 2048 + 2048])

        stage2_quant(s1a_h, xq1t_h, 2048, 16, 8, "w2a", 1.0 / math.sqrt(2048.0))

    _PH("P5b")
    if upto >= 12:
        with nc.Block() as blk:
            @blk.sync
            def _(sync):
                for ht in range(16):
                    for ci in range(4):
                        s_lx.dma(sync.dma_start_transpose(
                            bigR[:, (ht * 4 + ci) * 512:(ht * 4 + ci) * 512 + 512],
                            bass.AP(xq1t_h, ci * 512 * D + ht * 128, [[D, 512], [1, 128]])))
                sync.wait_ge(s_lx.sem, s_lx.n)

        with nc.Block() as blk:
            mm0, ev0, ac0 = s_mm.n, s_ev.n, s_ac.n
            l_base = [s_l0.n, s_l1.n]
            gs0 = s_gs.n

            def _wsp(col):
                return bass.AP(wsc, col, [[16, 128], [0, 1]])

            def ternarize(e, o, half, mi, t0, pool=False):
                wsrc = cD[:, half + (o % 2) * 2048:half + (o % 2) * 2048 + 2048]
                wdst = cF[:, half + (o % 2) * 2048:half + (o % 2) * 2048 + 2048]
                if pool:
                    e.tensor_single_scalar(cC[:, t0:t0 + 2048], wsrc,
                                           _wsp(6 + mi), OP.is_gt)
                    e.tensor_single_scalar(cC[:, t0 + 2048:t0 + 4096], wsrc,
                                           _wsp(9 + mi), OP.is_lt)
                else:
                    e.tensor_tensor(cC[:, t0:t0 + 2048], wsrc,
                                    _bc(wsc, 16, 6 + mi, 2048), OP.is_gt)
                    e.tensor_tensor(cC[:, t0 + 2048:t0 + 4096], wsrc,
                                    _bc(wsc, 16, 9 + mi, 2048), OP.is_lt)
                e.drain()
                e.tensor_sub(cC[:, t0:t0 + 2048], cC[:, t0:t0 + 2048],
                             cC[:, t0 + 2048:t0 + 4096])
                e.drain()
                e.tensor_tensor(wdst, cC[:, t0:t0 + 2048],
                                _bc(wsc, 16, 3 + mi, 2048), OP.mult)
                e.drain()

            @blk.sync
            def _(sync):
                for o in range(64):
                    if o >= 2:
                        sync.wait_ge(s_ev.sem, ev0 + (o - 2) * 3 + 1)
                    sem = s_l0 if o % 2 == 0 else s_l1
                    sem.dma(sync.dma_start(
                        bass.AP(cD, (o % 2) * 2048, [[8192, 128], [1, 2048]]),
                        bass.AP(wgT, o * 128 * 2048, [[2048, 128], [1, 2048]])))
                    sem.dma(sync.dma_start(
                        bass.AP(cD, 4096 + (o % 2) * 2048, [[8192, 128], [1, 2048]]),
                        bass.AP(wuT, o * 128 * 2048, [[2048, 128], [1, 2048]])))

            @blk.gpsimd
            def _(g):
                for o in range(64):
                    g.wait_ge(s_ev.sem, ev0 + o * 3 + 3)
                    store(g, bass.AP(z_h, o * 128 * TOWN, [[TOWN, 128], [1, TOWN]]),
                          cE[:, 2048 + (o % 2) * 2048:2048 + (o % 2) * 2048 + 2048])

            @blk.vector
            def _(v):
                for o in range(64):
                    sem = s_l0 if o % 2 == 0 else s_l1
                    v.wait_ge(sem.sem, l_base[o % 2] + 32 * (o // 2 + 1))
                    if o >= 2:
                        v.wait_ge(s_mm.sem, mm0 + (o - 1) * 2)
                    ternarize(v, o, 0, 0, 0)
                    ternarize(v, o, 4096, 1, 4096)
                    s_ev.inc(v.drain())
                    v.wait_ge(s_mm.sem, mm0 + o * 2 + 2)
                    v.wait_ge(s_ac.sem, ac0 + (o + 1) * 4)
                    if o >= 2:
                        v.wait_ge(s_gs.sem, gs0 + 16 * (o - 1))
                    for ci in range(4):
                        v.tensor_mul(cE[:, 2048 + (o % 2) * 2048 + ci * 512:
                                        2048 + (o % 2) * 2048 + ci * 512 + 512],
                                     cE[:, ci * 512:ci * 512 + 512], pf[ci][:, 0:512])
                    s_ev.inc(v.drain())
                    s_ev.inc(v.drain())

            @blk.tensor
            def _(te):
                for o in range(64):
                    te.wait_ge(s_ev.sem, ev0 + o * 3 + 1)
                    if o >= 1:
                        te.wait_ge(s_ev.sem, ev0 + (o - 1) * 3 + 2)
                    for ht in range(16):
                        lhs = cF[:, (o % 2) * 2048 + ht * 128:(o % 2) * 2048 + ht * 128 + 128]
                        for ci in range(4):
                            te.matmul(pf[ci][:, 0:512], lhs,
                                      bigR[:, (ht * 4 + ci) * 512:(ht * 4 + ci) * 512 + 512],
                                      start=(ht == 0), stop=(ht == 15))
                    s_mm.inc(te.drain())
                    te.wait_ge(s_ac.sem, ac0 + (o + 1) * 4)
                    for ht in range(16):
                        lhs = cF[:, 4096 + (o % 2) * 2048 + ht * 128:
                                 4096 + (o % 2) * 2048 + ht * 128 + 128]
                        for ci in range(4):
                            te.matmul(pf[ci][:, 0:512], lhs,
                                      bigR[:, (ht * 4 + ci) * 512:(ht * 4 + ci) * 512 + 512],
                                      start=(ht == 0), stop=(ht == 15))
                    s_mm.inc(te.drain())

            @blk.scalar
            def _(s):
                for o in range(64):
                    s.wait_ge(s_mm.sem, mm0 + o * 2 + 1)
                    if o >= 1:
                        s.wait_ge(s_ev.sem, ev0 + (o - 1) * 3 + 2)
                    for ci in range(4):
                        s.activation(cE[:, ci * 512:ci * 512 + 512], pf[ci][:, 0:512],
                                     AF.Silu)
                    s_ac.inc(s.drain(), 4)

    _PH("P5c")
    if upto >= 13:
        with nc.Block() as blk:
            mm0, ev0 = s_mm.n, s_ev.n
            l_base = [s_l0.n, s_l1.n]
            gs0 = s_gs.n

            @blk.sync
            def _(sync):
                for ho in range(64):
                    if ho >= 2:
                        sync.wait_ge(s_mm.sem, mm0 + (ho - 1) * 4)
                    sem = s_l0 if ho % 2 == 0 else s_l1
                    sem.dma(sync.dma_start(
                        cB[:, (ho % 2) * 2048:(ho % 2) * 2048 + 2048],
                        bass.AP(z_h, ho * 128 * TOWN, [[TOWN, 128], [1, TOWN]])))

            @blk.tensor
            def _(te):
                for ho in range(64):
                    sem = s_l0 if ho % 2 == 0 else s_l1
                    te.wait_ge(sem.sem, l_base[ho % 2] + 16 * (ho // 2 + 1))
                    for ci in range(4):
                        idx = ho * 4 + ci
                        if idx >= 4:
                            te.wait_ge(s_ev.sem, ev0 + idx - 3)
                        te.matmul(pf[ci % 4][:, 0:512], hm["h128"][:],
                                  cB[:, (ho % 2) * 2048 + ci * 512:(ho % 2) * 2048 + ci * 512 + 512],
                                  start=True, stop=True)
                        s_mm.inc(te.drain())

            @blk.vector
            def _(v):
                for ho in range(64):
                    for ci in range(4):
                        idx = ho * 4 + ci
                        v.wait_ge(s_mm.sem, mm0 + idx + 1)
                        if ho >= 2:
                            v.wait_ge(s_gs.sem, gs0 + 16 * (ho - 1))
                        v.tensor_copy(cE[:, (ho % 2) * 2048 + ci * 512:
                                         (ho % 2) * 2048 + ci * 512 + 512],
                                      pf[ci % 4][:, 0:512])
                        s_ev.inc(v.drain())

            @blk.gpsimd
            def _(g):
                for ho in range(64):
                    g.wait_ge(s_ev.sem, ev0 + ho * 4 + 4)
                    store(g, bass.AP(s2a_h, ho * 128 * TOWN, [[TOWN, 128], [1, TOWN]]),
                          cE[:, (ho % 2) * 2048:(ho % 2) * 2048 + 2048])

        stage2_quant(s2a_h, xq2t_h, 8192, 64, 2, "w2b", 1.0 / math.sqrt(8192.0))

    _PH("P5e")
    if upto >= 14:
        with nc.Block() as blk:
            ev0 = s_ev.n
            l_base = [s_l0.n, s_l1.n]
            gs0 = s_gs.n

            @blk.sync
            def _(sync):
                for j in range(64):
                    if j >= 2:
                        sync.wait_ge(s_ev.sem, ev0 + j - 1)
                    sem = s_l0 if j % 2 == 0 else s_l1
                    sem.dma(sync.dma_start(
                        cC[:, (j % 2) * 2048:(j % 2) * 2048 + 2048],
                        bass.AP(wdT, j * 128 * D, [[D, 128], [1, D]])))

            @blk.vector
            def _(v):
                for j in range(64):
                    sem = s_l0 if j % 2 == 0 else s_l1
                    v.wait_ge(sem.sem, l_base[j % 2] + 16 * (j // 2 + 1))
                    wsrc = cC[:, (j % 2) * 2048:(j % 2) * 2048 + 2048]
                    v.tensor_tensor(cD[:, 0:2048], wsrc, _bc(wsc, 16, 8, 2048), OP.is_gt)
                    v.tensor_tensor(cD[:, 2048:4096], wsrc, _bc(wsc, 16, 11, 2048), OP.is_lt)
                    v.drain()
                    v.tensor_sub(cD[:, 0:2048], cD[:, 0:2048], cD[:, 2048:4096])
                    v.drain()
                    if j >= 2:
                        v.wait_ge(s_gs.sem, gs0 + 16 * (j - 1))
                    v.tensor_tensor(cE[:, (j % 2) * 2048:(j % 2) * 2048 + 2048],
                                    cD[:, 0:2048], _bc(wsc, 16, 5, 2048), OP.mult)
                    s_ev.inc(v.drain())

            @blk.gpsimd
            def _(g):
                for j in range(64):
                    g.wait_ge(s_ev.sem, ev0 + j + 1)
                    store(g, bass.AP(wdq_h, j * 128,
                                     [[8192, 128], [1048576, 16], [1, 128]]),
                          cE[:, (j % 2) * 2048:(j % 2) * 2048 + 2048])

    _PH("P6")
    if upto >= 15:
        for c in range(4):
            with nc.Block() as blk:
                mm0, ev0 = s_mm.n, s_ev.n
                l_base = [s_l0.n, s_l1.n]
                gw_base = [s_gw0.n, s_gw1.n]
                gs0 = s_gs.n
                lx_mark = s_lx.n + 64 * 16

                @blk.sync
                def _(sync):
                    for g in range(64):
                        s_lx.dma(sync.dma_start_transpose(
                            bigR[:, g * 512:g * 512 + 512],
                            bass.AP(xq2t_h, c * 512 * DI + g * 128, [[DI, 512], [1, 128]])))
                    for d in range(16):
                        if d >= 2:
                            sync.wait_ge(s_ev.sem, ev0 + (d - 1) * 6)
                        sem = s_l0 if d % 2 == 0 else s_l1
                        sem.dma(sync.dma_start(
                            cB[:, (d % 2) * 512:(d % 2) * 512 + 512],
                            bass.AP(attn_h, d * 128 * TOWN + c * 512,
                                    [[TOWN, 128], [1, 512]])))
                        for tt in range(4):
                            sem.dma(sync.dma_start(
                                cD[:, 4096 + (d % 2) * 1024 + tt * 128:
                                     4096 + (d % 2) * 1024 + tt * 128 + 128],
                                bass.AP(x_own, (c * 512 + tt * 128) * D + d * 128,
                                        [[D, 128], [1, 128]])))

                @blk.gpsimd
                def _(g):
                    for d in range(16):
                        if d >= 2:
                            g.wait_ge(s_mm.sem, mm0 + (d - 1) * 5)
                        dst = cA if d % 2 == 0 else cE
                        gsem = s_gw0 if d % 2 == 0 else s_gw1
                        gsem.dma(g.dma_start(
                            bass.AP(dst, 0, [[8192, 128], [1, 8192]]),
                            bass.AP(wdq_h, d * 1048576, [[8192, 128], [1, 8192]])))
                        if d >= 1:
                            for tt in range(4):
                                g.wait_ge(s_ev.sem, ev0 + (d - 1) * 6 + 2 + tt + 1)
                                store(g, bass.AP(out_d,
                                                 (c * 512 + tt * 128) * D + (d - 1) * 128,
                                                 [[D, 128], [1, 128]]),
                                      cC[:, (((d - 1) * 4 + tt) % 8) * 128:
                                           (((d - 1) * 4 + tt) % 8) * 128 + 128])
                    for tt in range(4):
                        g.wait_ge(s_ev.sem, ev0 + 15 * 6 + 2 + tt + 1)
                        store(g, bass.AP(out_d, (c * 512 + tt * 128) * D + 15 * 128,
                                         [[D, 128], [1, 128]]),
                              cC[:, ((15 * 4 + tt) % 8) * 128:
                                   ((15 * 4 + tt) % 8) * 128 + 128])

                @blk.tensor
                def _(te):
                    te.wait_ge(s_lx.sem, lx_mark)
                    for d in range(16):
                        gsem = s_gw0 if d % 2 == 0 else s_gw1
                        te.wait_ge(gsem.sem, gw_base[d % 2] + 16 * (d // 2 + 1))
                        bank = pf[d % 2]
                        if d >= 2:
                            te.wait_ge(s_ev.sem, ev0 + (d - 1) * 6 + 2)
                        src = cA if d % 2 == 0 else cE
                        for g in range(64):
                            te.matmul(bank[:, 0:512], src[:, g * 128:g * 128 + 128],
                                      bigR[:, g * 512:g * 512 + 512],
                                      start=(g == 0), stop=(g == 63))
                        s_mm.inc(te.drain())
                        te.wait_ge(s_ev.sem, ev0 + d * 6 + 2)
                        for tt in range(4):
                            if d * 6 + 2 + tt - 2 > 0 and tt >= 2:
                                te.wait_ge(s_ev.sem, ev0 + d * 6 + 2 + tt - 1)
                            te.transpose(
                                bass.AP(pf[2 + tt % 2], 0, [[512, 128], [1, 128]]),
                                cF[:, (d % 2) * 512 + tt * 128:(d % 2) * 512 + tt * 128 + 128]
                                if False else
                                cC[:, 4096 + (d % 2) * 512 + tt * 128:
                                     4096 + (d % 2) * 512 + tt * 128 + 128],
                                identf_s[:])
                            s_mm.inc(te.drain())

                @blk.vector
                def _(v):
                    for d in range(16):
                        v.wait_ge(s_mm.sem, mm0 + d * 5 + 1)
                        sem = s_l0 if d % 2 == 0 else s_l1
                        v.wait_ge(sem.sem, l_base[d % 2] + 80 * (d // 2 + 1))
                        v.tensor_add(cC[:, 4096 + (d % 2) * 512:4096 + (d % 2) * 512 + 512],
                                     pf[d % 2][:, 0:512],
                                     cB[:, (d % 2) * 512:(d % 2) * 512 + 512])
                        s_ev.inc(v.drain())
                        s_ev.inc(v.drain())
                        for tt in range(4):
                            v.wait_ge(s_mm.sem, mm0 + d * 5 + 1 + tt + 1)
                            slot = (d * 4 + tt) % 8
                            if d * 4 + tt >= 8:
                                v.wait_ge(s_gs.sem, gs0 + 16 * (d * 4 + tt - 7))
                            v.tensor_add(
                                cC[:, slot * 128:slot * 128 + 128],
                                pf[2 + tt % 2][:, 0:128],
                                cD[:, 4096 + (d % 2) * 1024 + tt * 128:
                                     4096 + (d % 2) * 1024 + tt * 128 + 128])
                            s_ev.inc(v.drain())

        with nc.Block() as blk:
            @blk.gpsimd
            def _(g):
                g.wait_ge(s_gs.sem, s_gs.n)
                g.wait_ge(s_gw0.sem, s_gw0.n)
                g.wait_ge(s_gw1.sem, s_gw1.n)

            @blk.sync
            def _(sync):
                sync.wait_ge(s_lx.sem, s_lx.n)
                sync.wait_ge(s_l0.sem, s_l0.n)
                sync.wait_ge(s_l1.sem, s_l1.n)

    _PH(None)
    esC.close()
    es.close()
    return nc


def kernel(L0_attn_u, L0_attn_Wq, L0_attn_bq, L0_attn_Wk, L0_attn_bk,
           L0_attn_Wv, L0_attn_bv, L0_attn_convq_w, L0_attn_convq_b,
           L0_attn_convk_w, L0_attn_convk_b, L0_attn_ssm_kernel,
           L0_attn_D_skip, L0_attn_Wo, L0_attn_bo,
           L0_ffn_w_gate, L0_ffn_w_up, L0_ffn_w_down):
    x = np.ascontiguousarray(np.asarray(L0_attn_u, dtype=np.float32))
    f32 = lambda a: np.ascontiguousarray(np.asarray(a, dtype=np.float32))
    C = host_constants()

    chvec = np.zeros((D, 8), dtype=np.float32)
    chvec[:, 0] = f32(L0_attn_bq)
    chvec[:, 1] = f32(L0_attn_bk)
    chvec[:, 2] = f32(L0_attn_bv)
    chvec[:, 3] = f32(L0_attn_bo)
    chvec[:, 4] = f32(L0_attn_convq_b)
    chvec[:, 5] = f32(L0_attn_convk_b)
    chvec[:, 6] = f32(L0_attn_D_skip)
    cwvec = np.zeros((D, 8), dtype=np.float32)
    cwvec[:, 0:3] = f32(L0_attn_convq_w)[:, 0, :]
    cwvec[:, 3:6] = f32(L0_attn_convk_w)[:, 0, :]

    shared = {
        "wqT": _tile16(f32(L0_attn_Wq).T),
        "wkT": _tile16(f32(L0_attn_Wk).T),
        "wvT": _tile16(f32(L0_attn_Wv).T),
        "woT": _tile16(f32(L0_attn_Wo).T),
        "chvec": chvec, "cwvec": cwvec,
        "ssmk": f32(L0_attn_ssm_kernel),
        "wdT": np.ascontiguousarray(f32(L0_ffn_w_down).T[PERM2, :]),
    }
    shared["wgT"] = _tile16(np.ascontiguousarray(f32(L0_ffn_w_gate).T[PERM1, :]))
    shared["wuT"] = _tile16(np.ascontiguousarray(f32(L0_ffn_w_up).T[PERM1, :]))
    shared.update(C)

    nc = build_program()
    in_maps = []
    for core in range(8):
        b, h = core // 2, core % 2
        m = dict(shared)
        m["x_own"] = np.ascontiguousarray(x[b, h * 2048:(h + 1) * 2048])
        m["x_prev"] = np.ascontiguousarray(x[b, (1 - h) * 2048:(2 - h) * 2048])
        m["alpha"] = np.full((128, 1), float(h), dtype=np.float32)
        in_maps.append(m)

    res = run_bass_kernel_spmd(nc, in_maps, core_ids=list(range(8)))
    out = np.empty((4, L, D), dtype=np.float32)
    for core in range(8):
        b, h = core // 2, core % 2
        out[b, h * 2048:(h + 1) * 2048] = res.results[core]["out"]
    return out


# version 09 (final)
# speedup vs baseline: 1.7796x; 1.0215x over previous
import contextlib
import math
import numpy as np
import ml_dtypes

import concourse.bass as bass
import concourse.mybir as mybir
from concourse.bass_utils import run_bass_kernel_spmd

BFD = ml_dtypes.bfloat16
F32 = mybir.dt.float32
BF16 = mybir.dt.bfloat16
AF = mybir.ActivationFunctionType
OP = mybir.AluOpType
AX = mybir.AxisListType

D = 2048
L = 4096
TOWN = 2048
NF = 2176
NFT = 17
PAD = 512
DI = 8192

_DEBUG_TAPS = ()
_UPTO = 99


class Ctr:
    def __init__(self, sem):
        self.sem = sem
        self.n = 0

    def inc(self, inst, k=1):
        inst.then_inc(self.sem, k)
        self.n += k
        return inst

    def dma(self, inst):
        return self.inc(inst, 16)


def _hadamard(n):
    H = np.array([[1.0]])
    while H.shape[0] < n:
        H = np.block([[H, H], [H, -H]])
    return H


def _bf(x):
    return np.ascontiguousarray(np.asarray(x, dtype=np.float32).astype(BFD))


def host_constants():
    C = {}
    C["h128"] = _bf(_hadamard(128))
    C["w2a"] = _bf(np.kron(_hadamard(16), np.eye(8)))
    C["w2b"] = _bf(np.kron(_hadamard(64), np.eye(2)))
    C["identb"] = _bf(np.eye(128))
    C["identf"] = np.ascontiguousarray(np.eye(128, dtype=np.float32))
    t = np.arange(2048, dtype=np.float64)[:, None]
    f = np.arange(NF, dtype=np.float64)[None, :]
    ang = 2.0 * np.pi * t * f / 4096.0
    C["fwdr"] = _bf(np.cos(ang))
    C["fwdi"] = _bf(-np.sin(ang))
    k = np.arange(NF, dtype=np.float64)[:, None]
    tt = np.arange(2048, dtype=np.float64)[None, :]
    wk = np.where(k == 0, 1.0, 2.0)
    wk = np.where(k == 2048, 1.0, wk)
    wk = np.where(k > 2048, 0.0, wk) / 4096.0
    angi = 2.0 * np.pi * tt * k / 4096.0
    C["invr"] = _bf(wk * np.cos(angi))
    C["invi"] = _bf(-wk * np.sin(angi))
    C["sgn"] = np.ascontiguousarray(((-1.0) ** np.arange(128)).astype(np.float32)[:, None])
    return C


def _perm_fwht(n_outer, n_inner, offcnt):
    gcnt = n_inner // offcnt
    perm = np.empty(n_outer * n_inner, dtype=np.int64)
    for g in range(gcnt):
        for ho in range(n_outer):
            for off in range(offcnt):
                perm[g * 128 + ho * offcnt + off] = ho * n_inner + g * offcnt + off
    return perm


PERM1 = _perm_fwht(16, 128, 8)
PERM2 = _perm_fwht(64, 128, 2)


def _tile16(wT):
    D_, N = wT.shape
    A = wT.reshape(16, 128, N // 128, 128)
    return np.ascontiguousarray(A.transpose(2, 1, 0, 3).reshape(D_, N))



def _bc(t, pitch, col, n):
    return bass.AP(t, col, [[pitch, 128], [0, n]])

def build_program(upto=None):
    _scope_state = []
    def _PH(name):
        if _scope_state:
            pn, sid = _scope_state.pop()
            nc.leave_named_scope(pn, sid, False)
        if name:
            sid, _ = nc.enter_named_scope(name, False)
            _scope_state.append((name, sid))

    if upto is None:
        upto = _UPTO
    nc = bass.Bass("TRN2", target_bir_lowering=False)

    def scratch(name, shape, dt):
        if name in _DEBUG_TAPS:
            return nc.dram_tensor(name, shape, dt, kind="ExternalOutput")
        return nc.dram_tensor(name, shape, dt)

    x_own = nc.dram_tensor("x_own", [TOWN, D], F32, kind="ExternalInput")
    x_prev = nc.dram_tensor("x_prev", [TOWN, D], F32, kind="ExternalInput")
    alpha_d = nc.dram_tensor("alpha", [128, 1], F32, kind="ExternalInput")
    wqT = nc.dram_tensor("wqT", [D, D], F32, kind="ExternalInput")
    wkT = nc.dram_tensor("wkT", [D, D], F32, kind="ExternalInput")
    wvT = nc.dram_tensor("wvT", [D, D], F32, kind="ExternalInput")
    woT = nc.dram_tensor("woT", [D, D], F32, kind="ExternalInput")
    chvec = nc.dram_tensor("chvec", [D, 8], F32, kind="ExternalInput")
    cwvec = nc.dram_tensor("cwvec", [D, 8], F32, kind="ExternalInput")
    ssmk = nc.dram_tensor("ssmk", [D, D], F32, kind="ExternalInput")
    wgT = nc.dram_tensor("wgT", [D, DI], F32, kind="ExternalInput")
    wuT = nc.dram_tensor("wuT", [D, DI], F32, kind="ExternalInput")
    wdT = nc.dram_tensor("wdT", [DI, D], F32, kind="ExternalInput")
    cst = {}
    for nm, shape, dt in [("h128", [128, 128], BF16), ("w2a", [128, 128], BF16),
                          ("w2b", [128, 128], BF16), ("identb", [128, 128], BF16),
                          ("identf", [128, 128], F32),
                          ("fwdr", [2048, NF], BF16), ("fwdi", [2048, NF], BF16),
                          ("invr", [NF, 2048], BF16), ("invi", [NF, 2048], BF16),
                          ("sgn", [128, 1], F32)]:
        cst[nm] = nc.dram_tensor(nm, shape, dt, kind="ExternalInput")
    out_d = nc.dram_tensor("out", [TOWN, D], F32, kind="ExternalOutput")

    xn_h = scratch("xn_h", [PAD + L, D], BF16)
    k_h = scratch("k_h", [D, L], BF16)
    v_h = scratch("v_h", [D, L], BF16)
    q_h = scratch("q_h", [D, 2560], BF16)
    kv_h = scratch("kv_h", [D, L], BF16)
    qc_h = scratch("qc_h", [D, TOWN], BF16)
    ssmb_h = scratch("ssmb_h", [D, D], BF16)
    y_h = scratch("y_h", [TOWN, D], BF16)
    attn_h = scratch("attn_h", [D, TOWN], BF16)
    s1a_h = scratch("s1a_h", [D, TOWN], BF16)
    xq1t_h = scratch("xq1t_h", [TOWN, D], BF16)
    z_h = scratch("z_h", [DI, TOWN], BF16)
    s2a_h = scratch("s2a_h", [DI, TOWN], BF16)
    xq2t_h = scratch("xq2t_h", [TOWN, DI], BF16)
    wdq_h = scratch("wdq_h", [DI, D], BF16)
    scr_h = scratch("scr_h", [1, 16], F32)

    es = contextlib.ExitStack()
    sb = lambda nm, sh, dt: es.enter_context(nc.sbuf_tensor(nm, sh, dt))
    psm = lambda nm, sh, dt: es.enter_context(nc.psum_tensor(nm, sh, dt))
    mksem = lambda nm: Ctr(es.enter_context(nc.semaphore(nm)))

    cA = sb("cA", [128, 8192], BF16)
    cB = sb("cB", [128, 8192], BF16)
    cE = sb("cE", [128, 8192], BF16)
    cF = sb("cF", [128, 8192], BF16)
    junk = sb("junkb", [128, 2048], BF16)
    zpad = sb("zpad", [128, 2048], BF16)
    hm = {nm: sb("s_" + nm, [128, 128], BF16)
          for nm in ("h128", "w2a", "w2b", "identb")}
    identf_s = sb("s_identf", [128, 128], F32)
    chv_s = sb("chv_s", [128, 128], F32)
    cw_s = sb("cw_s", [128, 128], F32)
    alpha_s = sb("alpha_s", [128, 1], F32)
    sgn_s = sb("sgn_s", [128, 1], F32)
    asg_s = sb("asg_s", [128, 1], F32)
    st_s = sb("st_s", [128, 128], F32)
    wsc = sb("wsc_s", [128, 16], F32)

    pf = [psm(f"pf{i}", [128, 512], F32) for i in range(6)]
    pb = psm("pb", [128, 1024], BF16)
    pb2 = psm("pb2", [128, 1024], BF16)

    s_ld = mksem("s_ld")
    s_l0 = mksem("s_l0")
    s_l1 = mksem("s_l1")
    s_lx = mksem("s_lx")
    s_gw0 = mksem("s_gw0")
    s_gw1 = mksem("s_gw1")
    s_gs = mksem("s_gs")
    s_gd = mksem("s_gd")
    s_mm = mksem("s_mm")
    s_ev = mksem("s_ev")
    s_ac = mksem("s_ac")
    s_fr = mksem("s_fr")
    s_vh = mksem("s_vh")
    s_pc = mksem("s_pc")

    def store(g, dst_ap, src_ap):
        s_gs.dma(g.dma_start(dst_ap, src_ap))
        g.wait_ge(s_gs.sem, s_gs.n)

    def vhs(v, inst):
        s_vh.inc(v.drain())
        v.wait_ge(s_vh.sem, s_vh.n)

    with nc.Block() as blk:
        @blk.sync
        def _(sync):
            for nm in ("h128", "w2a", "w2b", "identb"):
                s_lx.dma(sync.dma_start(hm[nm][:], cst[nm].ap()))
            s_lx.dma(sync.dma_start(identf_s[:], cst["identf"].ap()))
            s_lx.dma(sync.dma_start(alpha_s[:], alpha_d.ap()))
            s_lx.dma(sync.dma_start(sgn_s[:], cst["sgn"].ap()))
            s_lx.dma(sync.dma_start(
                chv_s[:], bass.AP(chvec, 0, [[8, 128], [1024, 16], [1, 8]])))
            s_lx.dma(sync.dma_start(
                cw_s[:], bass.AP(cwvec, 0, [[8, 128], [1024, 16], [1, 8]])))

        @blk.vector
        def _(v):
            v.memset(st_s[:], 0.0)
            v.memset(wsc[:], 0.0)
            v.drain()
            v.wait_ge(s_lx.sem, s_lx.n)
            v.tensor_mul(asg_s[:], alpha_s[:], sgn_s[:])

        @blk.gpsimd
        def _(g):
            g.memset(junk[:], 0.0)
            g.memset(zpad[:], 0.0)

    esA = contextlib.ExitStack()
    sbA = lambda nm, sh, dt: esA.enter_context(nc.sbuf_tensor(nm, sh, dt))
    cC = sbA("cCa", [128, 8192], F32)
    cD = sbA("cDa", [128, 8192], F32)
    bigR = sbA("bigRa", [128, 32768], BF16)

    _PH("P1")
    if upto >= 1:
        with nc.Block() as blk:
            ev0, ac0, fr0 = s_ev.n, s_ac.n, s_fr.n
            l_base = [s_l0.n, s_l1.n]
            gs0 = s_gs.n

            @blk.sync
            def _(sync):
                for r in range(PAD // 128):
                    s_lx.dma(sync.dma_start(
                        bass.AP(xn_h, r * 128 * D, [[D, 128], [1, D]]), zpad[:]))
                for i in range(32):
                    src = x_prev if i < 16 else x_own
                    if i >= 2:
                        sync.wait_ge(s_fr.sem, fr0 + i - 1)
                    sem = s_l0 if i % 2 == 0 else s_l1
                    sem.dma(sync.dma_start(
                        cC[:, (i % 2) * 2048:(i % 2) * 2048 + 2048],
                        bass.AP(src, (i % 16) * 128 * D, [[D, 128], [1, D]])))
                sync.wait_ge(s_lx.sem, s_lx.n)

            @blk.vector
            def _(v):
                for i in range(32):
                    xt = cC[:, (i % 2) * 2048:(i % 2) * 2048 + 2048]
                    dt_ = cD[:, (i % 2) * 2048:(i % 2) * 2048 + 2048]
                    sem = s_l0 if i % 2 == 0 else s_l1
                    v.wait_ge(sem.sem, l_base[i % 2] + 16 * (i // 2 + 1))
                    vhs(v, v.reduce_sum(st_s[:, 0:1], xt, axis=AX.X))
                    vhs(v, v.tensor_single_scalar(st_s[:, 1:2], st_s[:, 0:1],
                                                  1.0 / 2048.0, OP.mult))
                    v.tensor_tensor(dt_, xt, _bc(st_s, 128, 1, 2048), OP.subtract)
                    v.drain()
                    v.tensor_mul(xt, dt_, dt_)
                    v.drain()
                    vhs(v, v.reduce_sum(st_s[:, 2:3], xt, axis=AX.X))
                    vhs(v, v.tensor_scalar(st_s[:, 3:4], st_s[:, 2:3], 1.0 / 2048.0,
                                           1e-5, OP.mult, OP.add))
                    vhs(v, v.reciprocal(st_s[:, 5:6], st_s[:, 3:4]))
                    for _ in range(4):
                        vhs(v, v.tensor_mul(st_s[:, 6:7], st_s[:, 5:6], st_s[:, 5:6]))
                        vhs(v, v.tensor_mul(st_s[:, 7:8], st_s[:, 3:4], st_s[:, 6:7]))
                        vhs(v, v.tensor_scalar(st_s[:, 8:9], st_s[:, 7:8], -0.5, 1.5,
                                               OP.mult, OP.add))
                        vhs(v, v.tensor_mul(st_s[:, 5:6], st_s[:, 5:6], st_s[:, 8:9]))
                    xo = cE[:, (i % 2) * 2048:(i % 2) * 2048 + 2048]
                    if i >= 2:
                        v.wait_ge(s_gs.sem, gs0 + 16 * (i - 1))
                    v.tensor_tensor(xo, dt_, _bc(st_s, 128, 5, 2048), OP.mult)
                    s_ev.inc(v.drain())
                    s_fr.inc(v.drain())

            @blk.gpsimd
            def _(g):
                for i in range(32):
                    g.wait_ge(s_ev.sem, ev0 + i + 1)
                    store(g, bass.AP(xn_h, (PAD + i * 128) * D, [[D, 128], [1, D]]),
                          cE[:, (i % 2) * 2048:(i % 2) * 2048 + 2048])

    _PH("P1b")
    if upto >= 2:
        with nc.Block() as blk:
            ac0 = s_ac.n
            sems = [s_l0, s_gw0, s_l1, s_gw1]
            s_base = [sm.n for sm in sems]
            chunks = [(mi, m, ci) for mi, m in enumerate((wgT, wuT, wdT))
                      for ci in range(64)]

            def issue(q, idx):
                mi, m, ci = chunks[idx]
                if idx >= 4:
                    q.wait_ge(s_ac.sem, ac0 + idx - 3)
                sems[idx % 4].dma(q.dma_start(
                    cD[:, (idx % 4) * 2048:(idx % 4) * 2048 + 2048],
                    bass.AP(m, ci * 128 * 2048, [[2048, 128], [1, 2048]])))

            @blk.sync
            def _(sync):
                for idx in range(len(chunks)):
                    if idx % 2 == 0:
                        issue(sync, idx)

            @blk.gpsimd
            def _(g):
                for idx in range(len(chunks)):
                    if idx % 2 == 1:
                        issue(g, idx)

            @blk.scalar
            def _(s):
                for idx in range(len(chunks)):
                    s.wait_ge(sems[idx % 4].sem,
                              s_base[idx % 4] + 16 * (idx // 4 + 1))
                    s.activation(junk[:, 0:2048],
                                 cD[:, (idx % 4) * 2048:(idx % 4) * 2048 + 2048],
                                 AF.Abs, accum_out=cC[:, idx:idx + 1])
                    s_ac.inc(s.drain())

            @blk.vector
            def _(v):
                v.wait_ge(s_ac.sem, ac0 + len(chunks))
                for mi in range(3):
                    v.reduce_sum(st_s[:, mi * 32:mi * 32 + 1], cC[:, mi * 64:mi * 64 + 64],
                                 axis=AX.X)

        with nc.Block() as blk:
            @blk.tensor
            def _(te):
                te.transpose(bass.AP(pf[0], 0, [[512, 128], [1, 128]]),
                             st_s[:, 0:128], identf_s[:])

        with nc.Block() as blk:
            ev0 = s_ev.n

            @blk.vector
            def _(v):
                v.tensor_copy(cC[:, 0:128], pf[0][:, 0:128])
                v.drain()
                vhs(v, v.reduce_sum(st_s[:, 13:14], cC[:, 0:128], axis=AX.X))
                s_ev.inc(v.drain())

            @blk.gpsimd
            def _(g):
                g.wait_ge(s_ev.sem, ev0 + 1)
                for mi in range(3):
                    store(g, bass.AP(scr_h, mi, [[1, 1], [1, 1]]),
                          st_s[mi * 32:mi * 32 + 1, 13:14])

        with nc.Block() as blk:
            @blk.sync
            def _(sync):
                for mi in range(3):
                    s_lx.dma(sync.dma_start(wsc[:, mi:mi + 1],
                                            bass.AP(scr_h, mi, [[0, 128], [1, 1]])))

            @blk.vector
            def _(v):
                v.wait_ge(s_lx.sem, s_lx.n)
                for mi in range(3):
                    vhs(v, v.tensor_scalar(wsc[:, 3 + mi:4 + mi], wsc[:, mi:mi + 1],
                                           1.0 / (D * DI), 1e-5, OP.mult, OP.max))
                    vhs(v, v.tensor_single_scalar(wsc[:, 6 + mi:7 + mi],
                                                  wsc[:, 3 + mi:4 + mi], 0.5, OP.mult))
                    vhs(v, v.tensor_single_scalar(wsc[:, 9 + mi:10 + mi],
                                                  wsc[:, 6 + mi:7 + mi], -1.0, OP.mult))

    def proj_pass(wT, dst_h, nchunk, row_base, bias_col):
        with nc.Block() as blk:
            mm0, ev0 = s_mm.n, s_ev.n
            l_base = [s_l0.n, s_l1.n]
            gw0 = s_gw0.n
            gs0 = s_gs.n

            @blk.gpsimd
            def _(g):
                for o in range(16):
                    s_gw0.dma(g.dma_start(
                        bass.AP(bigR, o * 2048, [[32768, 128], [1, 2048]]),
                        bass.AP(wT, o * 128 * 2048, [[2048, 128], [1, 2048]])))
                for ci in range(nchunk):
                    for o in range(16):
                        idx = ci * 16 + o
                        g.wait_ge(s_ev.sem, ev0 + idx + 1)
                        store(g, bass.AP(dst_h, o * 128 * (nchunk * 512) + ci * 512,
                                         [[nchunk * 512, 128], [1, 512]]),
                              cE[:, (idx % 8) * 512:(idx % 8) * 512 + 512])

            @blk.sync
            def _(sync):
                for ci in range(nchunk):
                    slot = cB if ci % 2 == 0 else cF
                    if ci >= 2:
                        sync.wait_ge(s_mm.sem, mm0 + (ci - 1) * 16)
                    sem = s_l0 if ci % 2 == 0 else s_l1
                    for ht in range(16):
                        sem.dma(sync.dma_start_transpose(
                            slot[:, ht * 512:ht * 512 + 512],
                            bass.AP(xn_h, (row_base + ci * 512) * D + ht * 128,
                                    [[D, 512], [1, 128]])))

            @blk.tensor
            def _(te):
                te.wait_ge(s_gw0.sem, s_gw0.n)
                for ci in range(nchunk):
                    slot = cB if ci % 2 == 0 else cF
                    sem = s_l0 if ci % 2 == 0 else s_l1
                    te.wait_ge(sem.sem, l_base[ci % 2] + 256 * (ci // 2 + 1))
                    for o in range(16):
                        idx = ci * 16 + o
                        if idx >= 4:
                            te.wait_ge(s_ev.sem, ev0 + idx - 3)
                        bank = pf[o % 4]
                        for ht in range(16):
                            te.matmul(bank[:, 0:512],
                                      bigR[:, (o * 16 + ht) * 128:(o * 16 + ht) * 128 + 128],
                                      slot[:, ht * 512:ht * 512 + 512],
                                      start=(ht == 0), stop=(ht == 15))
                        s_mm.inc(te.drain())

            @blk.vector
            def _(v):
                for ci in range(nchunk):
                    for o in range(16):
                        idx = ci * 16 + o
                        v.wait_ge(s_mm.sem, mm0 + idx + 1)
                        if idx >= 8:
                            v.wait_ge(s_gs.sem, gs0 + 16 * (idx - 7))
                        v.tensor_tensor(
                            cE[:, (idx % 8) * 512:(idx % 8) * 512 + 512],
                            pf[o % 4][:, 0:512],
                            _bc(chv_s, 128, o * 8 + bias_col, 512), OP.add)
                        s_ev.inc(v.drain())

    _PH("P2k")
    if upto >= 3:
        proj_pass(wkT, k_h, 8, PAD, 1)
    _PH("P2v")
    if upto >= 4:
        proj_pass(wvT, v_h, 8, PAD, 2)
    _PH("P2q")
    if upto >= 5:
        proj_pass(wqT, q_h, 5, PAD + 1536, 0)

    _PH("P2kv")
    if upto >= 6:
        with nc.Block() as blk:
            ev0 = s_ev.n
            l_base = [s_l0.n, s_l1.n]
            gs0 = s_gs.n

            @blk.sync
            def _(sync):
                for o in range(16):
                    if o >= 2:
                        sync.wait_ge(s_ev.sem, ev0 + o - 1)
                    sem = s_l0 if o % 2 == 0 else s_l1
                    sem.dma(sync.dma_start(
                        cA[:, (o % 2) * 4096:(o % 2) * 4096 + 4096],
                        bass.AP(k_h, o * 128 * L, [[L, 128], [1, L]])))
                    sem.dma(sync.dma_start(
                        cB[:, (o % 2) * 4096:(o % 2) * 4096 + 4096],
                        bass.AP(v_h, o * 128 * L, [[L, 128], [1, L]])))

            @blk.vector
            def _(v):
                for o in range(16):
                    sem = s_l0 if o % 2 == 0 else s_l1
                    v.wait_ge(sem.sem, l_base[o % 2] + 32 * (o // 2 + 1))
                    ks = cA[:, (o % 2) * 4096:(o % 2) * 4096 + 4096]
                    vs = cB[:, (o % 2) * 4096:(o % 2) * 4096 + 4096]
                    km = cA[:, (o % 2) * 4096 + 2046:(o % 2) * 4096 + 2048]
                    v.tensor_tensor(km, km, _bc(alpha_s, 1, 0, 2), OP.mult)
                    v.drain()
                    v.memset(cC[:, 0:2], 0.0)
                    v.tensor_tensor(cC[:, 2:4098], ks, _bc(cw_s, 128, o * 8 + 5, 4096), OP.mult)
                    v.drain()
                    v.tensor_tensor(cC[:, 2:4098], cC[:, 2:4098],
                                    _bc(chv_s, 128, o * 8 + 5, 4096), OP.add)
                    v.drain()
                    v.tensor_tensor(cD[:, 0:4095], ks[:, 0:4095],
                                    _bc(cw_s, 128, o * 8 + 4, 4095), OP.mult)
                    v.drain()
                    v.tensor_tensor(cC[:, 3:4098], cC[:, 3:4098], cD[:, 0:4095], OP.add)
                    v.drain()
                    v.tensor_tensor(cD[:, 0:4094], ks[:, 0:4094],
                                    _bc(cw_s, 128, o * 8 + 3, 4094), OP.mult)
                    v.drain()
                    v.tensor_tensor(cC[:, 4:4098], cC[:, 4:4098], cD[:, 0:4094], OP.add)
                    v.drain()
                    if o >= 2:
                        v.wait_ge(s_gs.sem, gs0 + 16 * (o - 1))
                    v.tensor_mul(cE[:, (o % 2) * 4096:(o % 2) * 4096 + 4096],
                                 cC[:, 2:4098], vs)
                    s_ev.inc(v.drain())

            @blk.gpsimd
            def _(g):
                for o in range(16):
                    g.wait_ge(s_ev.sem, ev0 + o + 1)
                    store(g, bass.AP(kv_h, o * 128 * L, [[L, 128], [1, L]]),
                          cE[:, (o % 2) * 4096:(o % 2) * 4096 + 4096])

    _PH("P2qc")
    if upto >= 7:
        with nc.Block() as blk:
            ev0 = s_ev.n
            l_base = [s_l0.n, s_l1.n]
            gs0 = s_gs.n

            @blk.sync
            def _(sync):
                for o in range(16):
                    if o >= 2:
                        sync.wait_ge(s_ev.sem, ev0 + o - 1)
                    sem = s_l0 if o % 2 == 0 else s_l1
                    sem.dma(sync.dma_start(
                        cA[:, (o % 2) * 2560:(o % 2) * 2560 + 2560],
                        bass.AP(q_h, o * 128 * 2560, [[2560, 128], [1, 2560]])))

            @blk.vector
            def _(v):
                for o in range(16):
                    sem = s_l0 if o % 2 == 0 else s_l1
                    v.wait_ge(sem.sem, l_base[o % 2] + 16 * (o // 2 + 1))
                    qs = cA[:, (o % 2) * 2560:(o % 2) * 2560 + 2560]
                    qm = cA[:, (o % 2) * 2560 + 510:(o % 2) * 2560 + 512]
                    v.tensor_tensor(qm, qm, _bc(alpha_s, 1, 0, 2), OP.mult)
                    v.drain()
                    v.tensor_tensor(cC[:, 0:2048], qs[:, 512:2560],
                                    _bc(cw_s, 128, o * 8 + 2, 2048), OP.mult)
                    v.drain()
                    v.tensor_tensor(cC[:, 0:2048], cC[:, 0:2048],
                                    _bc(chv_s, 128, o * 8 + 4, 2048), OP.add)
                    v.drain()
                    v.tensor_tensor(cD[:, 0:2048], qs[:, 511:2559],
                                    _bc(cw_s, 128, o * 8 + 1, 2048), OP.mult)
                    v.drain()
                    v.tensor_tensor(cC[:, 0:2048], cC[:, 0:2048], cD[:, 0:2048], OP.add)
                    v.drain()
                    v.tensor_tensor(cD[:, 0:2048], qs[:, 510:2558],
                                    _bc(cw_s, 128, o * 8 + 0, 2048), OP.mult)
                    v.drain()
                    v.tensor_tensor(cC[:, 0:2048], cC[:, 0:2048], cD[:, 0:2048], OP.add)
                    v.drain()
                    if o >= 2:
                        v.wait_ge(s_gs.sem, gs0 + 16 * (o - 1))
                    v.tensor_copy(cE[:, (o % 2) * 2048:(o % 2) * 2048 + 2048],
                                  cC[:, 0:2048])
                    s_ev.inc(v.drain())

            @blk.gpsimd
            def _(g):
                for o in range(16):
                    g.wait_ge(s_ev.sem, ev0 + o + 1)
                    store(g, bass.AP(qc_h, o * 128 * TOWN, [[TOWN, 128], [1, TOWN]]),
                          cE[:, (o % 2) * 2048:(o % 2) * 2048 + 2048])

    _PH("P2ssm")
    if upto >= 8:
        with nc.Block() as blk:
            @blk.gpsimd
            def _(g):
                for o in range(16):
                    s_gw0.dma(g.dma_start(
                        cF[:, (o % 2) * 2048:(o % 2) * 2048 + 2048],
                        bass.AP(ssmk, o * 128 * D, [[D, 128], [1, D]])))
                    g.wait_ge(s_gw0.sem, s_gw0.n)
                    store(g, bass.AP(ssmb_h, o * 128 * D, [[D, 128], [1, D]]),
                          cF[:, (o % 2) * 2048:(o % 2) * 2048 + 2048])

    esA.close()

    _PH("P3conv")
    if upto >= 9:
        esB = contextlib.ExitStack()
        sbB = lambda nm, sh, dt: esB.enter_context(nc.sbuf_tensor(nm, sh, dt))
        vcv = sbB("vcv", [128, 24576], BF16)
        ybf = sbB("ybf", [128, 17408], BF16)
        cmA = sbB("cmA", [128, 4352], BF16)
        cmB = sbB("cmB", [128, 4352], BF16)
        w32 = sbB("w32", [128, 2048], F32)

        for cc in range(4):
            with nc.Block() as blk:
                mm0, ev0 = s_mm.n, s_ev.n
                l_base = [s_l0.n, s_l1.n]
                lx_mark = s_lx.n + 48 * 16

                @blk.sync
                def _(sync):
                    for seg in range(3):
                        src = [kv_h, kv_h, ssmb_h][seg]
                        roww = L if seg < 2 else D
                        tbase = [2048, 0, 0][seg]
                        for tt in range(16):
                            s_lx.dma(sync.dma_start_transpose(
                                vcv[:, (seg * 16 + tt) * 512:(seg * 16 + tt) * 512 + 512],
                                bass.AP(src, cc * 512 * roww + tbase + tt * 128,
                                        [[roww, 512], [1, 128]])))
                    for ft in range(NFT):
                        cm = cmA if ft % 2 == 0 else cmB
                        sem = s_l0 if ft % 2 == 0 else s_l1
                        if ft >= 2:
                            sync.wait_ge(s_mm.sem, mm0 + ft - 1)
                        sem.dma(sync.dma_start(
                            cm[:, 0:2048],
                            bass.AP(cst["fwdr"], ft * 128,
                                    [[NF, 128], [128 * NF, 16], [1, 128]])))
                        sem.dma(sync.dma_start(
                            cm[:, 2048:4096],
                            bass.AP(cst["fwdi"], ft * 128,
                                    [[NF, 128], [128 * NF, 16], [1, 128]])))

                @blk.tensor
                def _(te):
                    te.wait_ge(s_lx.sem, lx_mark)
                    for ft in range(NFT):
                        cm = cmA if ft % 2 == 0 else cmB
                        sem = s_l0 if ft % 2 == 0 else s_l1
                        te.wait_ge(sem.sem, l_base[ft % 2] + 32 * (ft // 2 + 1))
                        if ft >= 1:
                            te.wait_ge(s_ev.sem, ev0 + ft)
                        for tt in range(16):
                            for mat in range(2):
                                for seg in range(3):
                                    te.matmul(
                                        pf[seg * 2 + mat][:, 0:512],
                                        cm[:, mat * 2048 + tt * 128:mat * 2048 + tt * 128 + 128],
                                        vcv[:, (seg * 16 + tt) * 512:(seg * 16 + tt) * 512 + 512],
                                        start=(tt == 0), stop=(tt == 15))
                        s_mm.inc(te.drain())

                @blk.vector
                def _(v):
                    for ft in range(NFT):
                        v.wait_ge(s_mm.sem, mm0 + ft + 1)
                        v.tensor_tensor(w32[:, 0:512], pf[2][:, 0:512],
                                        _bc(asg_s, 1, 0, 512), OP.mult)
                        v.drain()
                        v.tensor_tensor(w32[:, 0:512], w32[:, 0:512],
                                        pf[0][:, 0:512], OP.add)
                        v.tensor_tensor(w32[:, 512:1024], pf[3][:, 0:512],
                                        _bc(asg_s, 1, 0, 512), OP.mult)
                        v.drain()
                        v.tensor_tensor(w32[:, 512:1024], w32[:, 512:1024],
                                        pf[1][:, 0:512], OP.add)
                        v.drain()
                        v.tensor_mul(w32[:, 1024:1536], w32[:, 0:512], pf[4][:, 0:512])
                        v.tensor_mul(w32[:, 1536:2048], w32[:, 512:1024], pf[5][:, 0:512])
                        v.drain()
                        v.tensor_sub(ybf[:, ft * 1024:ft * 1024 + 512],
                                     w32[:, 1024:1536], w32[:, 1536:2048])
                        v.drain()
                        v.tensor_mul(w32[:, 1024:1536], w32[:, 0:512], pf[5][:, 0:512])
                        v.tensor_mul(w32[:, 1536:2048], w32[:, 512:1024], pf[4][:, 0:512])
                        v.drain()
                        v.tensor_add(ybf[:, ft * 1024 + 512:ft * 1024 + 1024],
                                     w32[:, 1024:1536], w32[:, 1536:2048])
                        s_ev.inc(v.drain())

            with nc.Block() as blk:
                mm0, ev0 = s_mm.n, s_ev.n
                l_base = [s_l0.n, s_l1.n]
                gs0 = s_gs.n

                @blk.sync
                def _(sync):
                    for tt in range(16):
                        cm = cmA if tt % 2 == 0 else cmB
                        sem = s_l0 if tt % 2 == 0 else s_l1
                        if tt >= 2:
                            sync.wait_ge(s_mm.sem, mm0 + tt - 1)
                        sem.dma(sync.dma_start(
                            cm[:, 0:2176],
                            bass.AP(cst["invr"], tt * 128,
                                    [[2048, 128], [128 * 2048, 17], [1, 128]])))
                        sem.dma(sync.dma_start(
                            cm[:, 2176:4352],
                            bass.AP(cst["invi"], tt * 128,
                                    [[2048, 128], [128 * 2048, 17], [1, 128]])))

                @blk.tensor
                def _(te):
                    for tt in range(16):
                        cm = cmA if tt % 2 == 0 else cmB
                        sem = s_l0 if tt % 2 == 0 else s_l1
                        te.wait_ge(sem.sem, l_base[tt % 2] + 32 * (tt // 2 + 1))
                        if tt >= 2:
                            te.wait_ge(s_ev.sem, ev0 + tt - 1)
                        bank = pf[tt % 2]
                        for ft in range(NFT):
                            te.matmul(bank[:, 0:512], cm[:, ft * 128:ft * 128 + 128],
                                      ybf[:, ft * 1024:ft * 1024 + 512],
                                      start=(ft == 0), stop=False)
                            te.matmul(bank[:, 0:512],
                                      cm[:, 2176 + ft * 128:2176 + ft * 128 + 128],
                                      ybf[:, ft * 1024 + 512:ft * 1024 + 1024],
                                      start=False, stop=(ft == NFT - 1))
                        s_mm.inc(te.drain())

                @blk.vector
                def _(v):
                    for tt in range(16):
                        v.wait_ge(s_mm.sem, mm0 + tt + 1)
                        if tt >= 2:
                            v.wait_ge(s_gs.sem, gs0 + 16 * (tt - 1))
                        v.tensor_copy(cE[:, (tt % 2) * 512:(tt % 2) * 512 + 512],
                                      pf[tt % 2][:, 0:512])
                        s_ev.inc(v.drain())

                @blk.gpsimd
                def _(g):
                    for tt in range(16):
                        g.wait_ge(s_ev.sem, ev0 + tt + 1)
                        store(g, bass.AP(y_h, tt * 128 * D + cc * 512,
                                         [[D, 128], [1, 512]]),
                              cE[:, (tt % 2) * 512:(tt % 2) * 512 + 512])

        esB.close()

    esC = contextlib.ExitStack()
    sbC = lambda nm, sh, dt: esC.enter_context(nc.sbuf_tensor(nm, sh, dt))
    cC = sbC("cCc", [128, 8192], F32)
    cD = sbC("cDc", [128, 8192], F32)
    bigR = sbC("bigRc", [128, 32768], BF16)

    _PH("P4")
    if upto >= 10:
        with nc.Block() as blk:
            ev0 = s_ev.n
            l_base = [s_l0.n, s_l1.n]

            @blk.sync
            def _(sync):
                for o in range(16):
                    if o >= 2:
                        sync.wait_ge(s_ev.sem, ev0 + o - 1)
                    sem = s_l0 if o % 2 == 0 else s_l1
                    sem.dma(sync.dma_start_transpose(
                        cA[:, (o % 2) * 2048:(o % 2) * 2048 + 2048],
                        bass.AP(y_h, o * 128, [[D, 2048], [1, 128]])))
                    sem.dma(sync.dma_start(
                        cB[:, (o % 2) * 2048:(o % 2) * 2048 + 2048],
                        bass.AP(kv_h, o * 128 * L + 2048, [[L, 128], [1, 2048]])))
                    sem.dma(sync.dma_start(
                        cF[:, (o % 2) * 2048:(o % 2) * 2048 + 2048],
                        bass.AP(qc_h, o * 128 * TOWN, [[TOWN, 128], [1, TOWN]])))

            @blk.vector
            def _(v):
                for o in range(16):
                    sem = s_l0 if o % 2 == 0 else s_l1
                    v.wait_ge(sem.sem, l_base[o % 2] + 48 * (o // 2 + 1))
                    ys = cA[:, (o % 2) * 2048:(o % 2) * 2048 + 2048]
                    kvs = cB[:, (o % 2) * 2048:(o % 2) * 2048 + 2048]
                    qcs = cF[:, (o % 2) * 2048:(o % 2) * 2048 + 2048]
                    v.tensor_tensor(cC[:, 0:2048], kvs,
                                    _bc(chv_s, 128, o * 8 + 6, 2048), OP.mult)
                    v.drain()
                    v.tensor_tensor(cC[:, 0:2048], cC[:, 0:2048], ys, OP.add)
                    v.drain()
                    v.tensor_mul(bigR[:, o * 2048:o * 2048 + 2048], cC[:, 0:2048], qcs)
                    s_ev.inc(v.drain())

        with nc.Block() as blk:
            mm0, ev0 = s_mm.n, s_ev.n
            gw_base = [s_gw0.n, s_gw1.n]
            gs0 = s_gs.n

            @blk.gpsimd
            def _(g):
                for o in range(16):
                    if o >= 2:
                        g.wait_ge(s_mm.sem, mm0 + (o - 1) * 4)
                    gsem = s_gw0 if o % 2 == 0 else s_gw1
                    gsem.dma(g.dma_start(
                        bass.AP(cA, (o % 2) * 2048, [[8192, 128], [1, 2048]]),
                        bass.AP(woT, o * 128 * 2048, [[2048, 128], [1, 2048]])))
                    if o >= 1:
                        for ci in range(4):
                            idx = (o - 1) * 4 + ci
                            g.wait_ge(s_ev.sem, ev0 + idx + 1)
                            store(g, bass.AP(attn_h, (o - 1) * 128 * TOWN + ci * 512,
                                             [[TOWN, 128], [1, 512]]),
                                  cE[:, (idx % 8) * 512:(idx % 8) * 512 + 512])
                for ci in range(4):
                    idx = 15 * 4 + ci
                    g.wait_ge(s_ev.sem, ev0 + idx + 1)
                    store(g, bass.AP(attn_h, 15 * 128 * TOWN + ci * 512,
                                     [[TOWN, 128], [1, 512]]),
                          cE[:, (idx % 8) * 512:(idx % 8) * 512 + 512])

            @blk.tensor
            def _(te):
                for o in range(16):
                    gsem = s_gw0 if o % 2 == 0 else s_gw1
                    te.wait_ge(gsem.sem, gw_base[o % 2] + 16 * (o // 2 + 1))
                    for ci in range(4):
                        idx = o * 4 + ci
                        if idx >= 4:
                            te.wait_ge(s_ev.sem, ev0 + idx - 3)
                        bank = pf[ci % 4]
                        for ht in range(16):
                            te.matmul(
                                bank[:, 0:512],
                                cA[:, (o % 2) * 2048 + ht * 128:(o % 2) * 2048 + ht * 128 + 128],
                                bigR[:, ht * 2048 + ci * 512:ht * 2048 + ci * 512 + 512],
                                start=(ht == 0), stop=(ht == 15))
                        s_mm.inc(te.drain())

            @blk.vector
            def _(v):
                for o in range(16):
                    for ci in range(4):
                        idx = o * 4 + ci
                        v.wait_ge(s_mm.sem, mm0 + idx + 1)
                        if idx >= 8:
                            v.wait_ge(s_gs.sem, gs0 + 16 * (idx - 7))
                        v.tensor_tensor(
                            cE[:, (idx % 8) * 512:(idx % 8) * 512 + 512],
                            pf[ci % 4][:, 0:512],
                            _bc(chv_s, 128, o * 8 + 3, 512), OP.add)
                        s_ev.inc(v.drain())

    def stage2_quant(src_h, dst_h, ncols, n_outer, offcnt, w2_name, norm):
        gcnt = ncols // 128
        post = norm / 127.0
        half = ncols > 4096
        w1 = min(ncols, 4096)
        SL = [16384, 24576]

        def dma_src(sync, k):
            sem = s_l0 if k % 2 == 0 else s_l1
            sem.dma(sync.dma_start_transpose(
                bigR[:, SL[k % 2]:SL[k % 2] + ncols],
                bass.AP(src_h, k * 128, [[TOWN, ncols], [1, 128]])))
            return sem.n

        def regroup(v, k):
            for g in range(gcnt):
                v.tensor_copy(
                    bass.AP(cF, g * 128, [[8192, 128], [offcnt, n_outer], [1, offcnt]]),
                    bass.AP(bigR, SL[k % 2] + g * offcnt,
                            [[32768, 128], [128, n_outer], [1, offcnt]]))
                s_ev.inc(v.drain())

        lx_after = {}

        with nc.Block() as blk:
            mm0, ev0, ac0 = s_mm.n, s_ev.n, s_ac.n
            l0_base = s_l0.n

            @blk.sync
            def _(sync):
                lx_after[0] = dma_src(sync, 0)
                lx_after[1] = dma_src(sync, 1)

            @blk.vector
            def _(v):
                v.wait_ge(s_l0.sem, lx_after[0])
                regroup(v, 0)

            @blk.tensor
            def _(te):
                for g in range(gcnt):
                    te.wait_ge(s_ev.sem, ev0 + g + 1)
                    if g >= 2:
                        te.wait_ge(s_ac.sem, ac0 + g - 1)
                    pbank = pb if g % 2 == 0 else pb2
                    te.transpose(pbank[:, 0:128],
                                 cF[:, g * 128:g * 128 + 128], hm["identb"][:])
                    s_mm.inc(te.drain())

            @blk.scalar
            def _(s):
                for g in range(gcnt):
                    s.wait_ge(s_mm.sem, mm0 + g + 1)
                    pbank = pb if g % 2 == 0 else pb2
                    s.activation(cA[:, g * 128:g * 128 + 128], pbank[:, 0:128],
                                 AF.Copy)
                    s_ac.inc(s.drain())

        for k in range(16):
            with nc.Block() as blk:
                mm0, ev0, ac0 = s_mm.n, s_ev.n, s_ac.n
                more = k + 1 <= 15
                evR = gcnt if more else 0
                evE = evR + gcnt
                acS = ac0 + (gcnt if more else 0) + 1

                @blk.sync
                def _(sync):
                    if k + 2 <= 15:
                        lx_after[k + 2] = dma_src(sync, k + 2)

                @blk.tensor
                def _(te):
                    for g in range(gcnt):
                        if g >= 3:
                            te.wait_ge(s_ev.sem, ev0 + evR + g - 2)
                        te.matmul(pf[g % 3][:, 0:128], cA[:, g * 128:g * 128 + 128],
                                  hm[w2_name][:], start=True, stop=True)
                        s_mm.inc(te.drain())
                    if more:
                        for g in range(gcnt):
                            te.wait_ge(s_ev.sem, ev0 + g + 1)
                            if g >= 2:
                                te.wait_ge(s_ac.sem, ac0 + g - 1)
                            pbank = pb if g % 2 == 0 else pb2
                            te.transpose(pbank[:, 0:128],
                                         cF[:, g * 128:g * 128 + 128], hm["identb"][:])
                            s_mm.inc(te.drain())

                @blk.scalar
                def _(s):
                    if more:
                        for g in range(gcnt):
                            s.wait_ge(s_mm.sem, mm0 + gcnt + g + 1)
                            pbank = pb if g % 2 == 0 else pb2
                            s.activation(cA[:, g * 128:g * 128 + 128],
                                         pbank[:, 0:128], AF.Copy)
                            s_ac.inc(s.drain())
                    s.wait_ge(s_ev.sem, ev0 + evE + 1)
                    s.activation(cB[:, 0:w1], cC[:, 0:w1], AF.Sign)
                    if half:
                        s.activation(cB[:, 4096:8192], cD[:, 0:4096], AF.Sign)
                    s_ac.inc(s.drain())

                @blk.vector
                def _(v):
                    if more:
                        psem = s_l0 if (k + 1) % 2 == 0 else s_l1
                        v.wait_ge(psem.sem, lx_after[k + 1])
                        regroup(v, k + 1)
                    for g in range(gcnt):
                        v.wait_ge(s_mm.sem, mm0 + g + 1)
                        dst = cC if g < 32 else cD
                        gg = g if g < 32 else g - 32
                        v.tensor_copy(dst[:, gg * 128:gg * 128 + 128],
                                      pf[g % 3][:, 0:128])
                        s_ev.inc(v.drain())
                    vhs(v, v.tensor_reduce(st_s[:, 20:21], cC[:, 0:w1], AX.X, OP.max,
                                           apply_absolute_value=True))
                    if half:
                        vhs(v, v.tensor_reduce(st_s[:, 25:26], cD[:, 0:ncols - 4096],
                                               AX.X, OP.max, apply_absolute_value=True))
                        vhs(v, v.tensor_max(st_s[:, 20:21], st_s[:, 20:21],
                                            st_s[:, 25:26]))
                    vhs(v, v.tensor_single_scalar(st_s[:, 21:22], st_s[:, 20:21],
                                                  1e-5, OP.max))
                    vhs(v, v.reciprocal(st_s[:, 22:23], st_s[:, 21:22]))
                    vhs(v, v.tensor_single_scalar(st_s[:, 23:24], st_s[:, 22:23],
                                                  127.0, OP.mult))
                    vhs(v, v.tensor_single_scalar(st_s[:, 24:25], st_s[:, 21:22],
                                                  post, OP.mult))
                    v.tensor_tensor(cC[:, 0:w1], cC[:, 0:w1],
                                    _bc(st_s, 128, 23, w1), OP.mult)
                    if half:
                        v.tensor_tensor(cD[:, 0:4096], cD[:, 0:4096],
                                        _bc(st_s, 128, 23, 4096), OP.mult)
                    s_ev.inc(v.drain())
                    v.wait_ge(s_ac.sem, acS)
                    v.tensor_mul(cC[:, 4096:4096 + w1], cC[:, 0:w1], cB[:, 0:w1])
                    if half:
                        v.tensor_mul(cD[:, 4096:8192], cD[:, 0:4096], cB[:, 4096:8192])
                    v.drain()
                    v.tensor_copy(cC[:, 0:w1], cC[:, 4096:4096 + w1])
                    if half:
                        v.tensor_copy(cD[:, 0:4096], cD[:, 4096:8192])
                    v.drain()
                    parts = [(cC, 0, w1, cE[:, 4096:4096 + w1] if not half else cE[:, 4096:8192],
                              cB[:, 0:w1])]
                    if half:
                        parts = [(cC, 0, 4096, cE[:, 4096:8192], cB[:, 0:4096]),
                                 (cD, 0, 4096, cE[:, 4096:8192], cB[:, 4096:8192])]
                    for pi, (buf, base, w, btmp, sgn_sl) in enumerate(parts):
                        for kv_ in (64.0, 32.0, 16.0, 8.0, 4.0, 2.0, 1.0):
                            v.tensor_scalar(btmp, buf[:, 0:w], kv_ - 0.5, kv_,
                                            OP.is_ge, OP.mult)
                            v.drain()
                            v.tensor_sub(buf[:, 0:w], buf[:, 0:w], btmp)
                            v.drain()
                        v.tensor_sub(buf[:, 0:w], buf[:, 4096:4096 + w], buf[:, 0:w])
                        v.drain()
                        v.tensor_tensor(btmp, buf[:, 0:w], sgn_sl, OP.mult)
                        v.drain()
                        v.tensor_tensor(cE[:, pi * 4096:pi * 4096 + w], btmp,
                                        _bc(st_s, 128, 24, w), OP.mult)
                        v.drain()
                    s_ev.inc(v.drain())

                @blk.gpsimd
                def _(g):
                    g.wait_ge(s_ev.sem, ev0 + evE + 2)
                    store(g, bass.AP(dst_h, k * 128 * ncols, [[ncols, 128], [1, ncols]]),
                          cE[:, 0:ncols])

    _PH("P5a")
    if upto >= 11:
        with nc.Block() as blk:
            mm0, ev0 = s_mm.n, s_ev.n
            l_base = [s_l0.n, s_l1.n]
            gs0 = s_gs.n

            @blk.sync
            def _(sync):
                for ho in range(16):
                    for ci in range(4):
                        idx = ho * 4 + ci
                        if idx >= 2:
                            sync.wait_ge(s_mm.sem, mm0 + idx - 1)
                        sem = s_l0 if idx % 2 == 0 else s_l1
                        sem.dma(sync.dma_start_transpose(
                            cB[:, (idx % 8) * 512:(idx % 8) * 512 + 512],
                            bass.AP(xn_h, (PAD + 2048 + ci * 512) * D + ho * 128,
                                    [[D, 512], [1, 128]])))

            @blk.tensor
            def _(te):
                for ho in range(16):
                    for ci in range(4):
                        idx = ho * 4 + ci
                        sem = s_l0 if idx % 2 == 0 else s_l1
                        te.wait_ge(sem.sem, l_base[idx % 2] + 16 * (idx // 2 + 1))
                        if idx >= 4:
                            te.wait_ge(s_ev.sem, ev0 + idx - 3)
                        te.matmul(pf[ci % 4][:, 0:512], hm["h128"][:],
                                  cB[:, (idx % 8) * 512:(idx % 8) * 512 + 512],
                                  start=True, stop=True)
                        s_mm.inc(te.drain())

            @blk.vector
            def _(v):
                for ho in range(16):
                    for ci in range(4):
                        idx = ho * 4 + ci
                        v.wait_ge(s_mm.sem, mm0 + idx + 1)
                        if ho >= 2:
                            v.wait_ge(s_gs.sem, gs0 + 16 * (ho - 1))
                        v.tensor_copy(cE[:, (idx % 8) * 512:(idx % 8) * 512 + 512],
                                      pf[ci % 4][:, 0:512])
                        s_ev.inc(v.drain())

            @blk.gpsimd
            def _(g):
                for ho in range(16):
                    g.wait_ge(s_ev.sem, ev0 + ho * 4 + 4)
                    store(g, bass.AP(s1a_h, ho * 128 * TOWN, [[TOWN, 128], [1, TOWN]]),
                          cE[:, (ho % 2) * 2048:(ho % 2) * 2048 + 2048])

        stage2_quant(s1a_h, xq1t_h, 2048, 16, 8, "w2a", 1.0 / math.sqrt(2048.0))

    _PH("P5b")
    if upto >= 12:
        with nc.Block() as blk:
            @blk.sync
            def _(sync):
                for ht in range(16):
                    for ci in range(4):
                        s_lx.dma(sync.dma_start_transpose(
                            bigR[:, (ht * 4 + ci) * 512:(ht * 4 + ci) * 512 + 512],
                            bass.AP(xq1t_h, ci * 512 * D + ht * 128, [[D, 512], [1, 128]])))
                sync.wait_ge(s_lx.sem, s_lx.n)

        with nc.Block() as blk:
            mm0, ev0, ac0 = s_mm.n, s_ev.n, s_ac.n
            l_base = [s_l0.n, s_l1.n]
            gs0 = s_gs.n

            def _wsp(col):
                return bass.AP(wsc, col, [[16, 128], [0, 1]])

            def ternarize(e, o, half, mi, t0, pool=False):
                wsrc = cD[:, half + (o % 2) * 2048:half + (o % 2) * 2048 + 2048]
                wdst = cF[:, half + (o % 2) * 2048:half + (o % 2) * 2048 + 2048]
                if pool:
                    e.tensor_single_scalar(cC[:, t0:t0 + 2048], wsrc,
                                           _wsp(6 + mi), OP.is_gt)
                    e.tensor_single_scalar(cC[:, t0 + 2048:t0 + 4096], wsrc,
                                           _wsp(9 + mi), OP.is_lt)
                else:
                    e.tensor_tensor(cC[:, t0:t0 + 2048], wsrc,
                                    _bc(wsc, 16, 6 + mi, 2048), OP.is_gt)
                    e.tensor_tensor(cC[:, t0 + 2048:t0 + 4096], wsrc,
                                    _bc(wsc, 16, 9 + mi, 2048), OP.is_lt)
                e.drain()
                e.tensor_sub(cC[:, t0:t0 + 2048], cC[:, t0:t0 + 2048],
                             cC[:, t0 + 2048:t0 + 4096])
                e.drain()
                e.tensor_tensor(wdst, cC[:, t0:t0 + 2048],
                                _bc(wsc, 16, 3 + mi, 2048), OP.mult)
                e.drain()

            @blk.sync
            def _(sync):
                for o in range(64):
                    if o >= 2:
                        sync.wait_ge(s_ev.sem, ev0 + (o - 2) * 3 + 1)
                    sem = s_l0 if o % 2 == 0 else s_l1
                    sem.dma(sync.dma_start(
                        bass.AP(cD, (o % 2) * 2048, [[8192, 128], [1, 2048]]),
                        bass.AP(wgT, o * 128 * 2048, [[2048, 128], [1, 2048]])))
                    sem.dma(sync.dma_start(
                        bass.AP(cD, 4096 + (o % 2) * 2048, [[8192, 128], [1, 2048]]),
                        bass.AP(wuT, o * 128 * 2048, [[2048, 128], [1, 2048]])))

            @blk.gpsimd
            def _(g):
                for o in range(64):
                    g.wait_ge(s_ev.sem, ev0 + o * 3 + 3)
                    store(g, bass.AP(z_h, o * 128 * TOWN, [[TOWN, 128], [1, TOWN]]),
                          cE[:, 2048 + (o % 2) * 2048:2048 + (o % 2) * 2048 + 2048])

            @blk.vector
            def _(v):
                for o in range(64):
                    sem = s_l0 if o % 2 == 0 else s_l1
                    v.wait_ge(sem.sem, l_base[o % 2] + 32 * (o // 2 + 1))
                    if o >= 2:
                        v.wait_ge(s_mm.sem, mm0 + (o - 1) * 2)
                    ternarize(v, o, 0, 0, 0)
                    ternarize(v, o, 4096, 1, 4096)
                    s_ev.inc(v.drain())
                    v.wait_ge(s_mm.sem, mm0 + o * 2 + 2)
                    v.wait_ge(s_ac.sem, ac0 + (o + 1) * 4)
                    if o >= 2:
                        v.wait_ge(s_gs.sem, gs0 + 16 * (o - 1))
                    for ci in range(4):
                        v.tensor_mul(cE[:, 2048 + (o % 2) * 2048 + ci * 512:
                                        2048 + (o % 2) * 2048 + ci * 512 + 512],
                                     cE[:, ci * 512:ci * 512 + 512], pf[ci][:, 0:512])
                    s_ev.inc(v.drain())
                    s_ev.inc(v.drain())

            @blk.tensor
            def _(te):
                for o in range(64):
                    te.wait_ge(s_ev.sem, ev0 + o * 3 + 1)
                    if o >= 1:
                        te.wait_ge(s_ev.sem, ev0 + (o - 1) * 3 + 2)
                    for ht in range(16):
                        lhs = cF[:, (o % 2) * 2048 + ht * 128:(o % 2) * 2048 + ht * 128 + 128]
                        for ci in range(4):
                            te.matmul(pf[ci][:, 0:512], lhs,
                                      bigR[:, (ht * 4 + ci) * 512:(ht * 4 + ci) * 512 + 512],
                                      start=(ht == 0), stop=(ht == 15))
                    s_mm.inc(te.drain())
                    te.wait_ge(s_ac.sem, ac0 + (o + 1) * 4)
                    for ht in range(16):
                        lhs = cF[:, 4096 + (o % 2) * 2048 + ht * 128:
                                 4096 + (o % 2) * 2048 + ht * 128 + 128]
                        for ci in range(4):
                            te.matmul(pf[ci][:, 0:512], lhs,
                                      bigR[:, (ht * 4 + ci) * 512:(ht * 4 + ci) * 512 + 512],
                                      start=(ht == 0), stop=(ht == 15))
                    s_mm.inc(te.drain())

            @blk.scalar
            def _(s):
                for o in range(64):
                    s.wait_ge(s_mm.sem, mm0 + o * 2 + 1)
                    if o >= 1:
                        s.wait_ge(s_ev.sem, ev0 + (o - 1) * 3 + 2)
                    for ci in range(4):
                        s.activation(cE[:, ci * 512:ci * 512 + 512], pf[ci][:, 0:512],
                                     AF.Silu)
                    s_ac.inc(s.drain(), 4)

    _PH("P5c")
    if upto >= 13:
        with nc.Block() as blk:
            mm0, ev0 = s_mm.n, s_ev.n
            l_base = [s_l0.n, s_l1.n]
            gs0 = s_gs.n

            @blk.sync
            def _(sync):
                for ho in range(64):
                    if ho >= 2:
                        sync.wait_ge(s_mm.sem, mm0 + (ho - 1) * 4)
                    sem = s_l0 if ho % 2 == 0 else s_l1
                    sem.dma(sync.dma_start(
                        cB[:, (ho % 2) * 2048:(ho % 2) * 2048 + 2048],
                        bass.AP(z_h, ho * 128 * TOWN, [[TOWN, 128], [1, TOWN]])))

            @blk.tensor
            def _(te):
                for ho in range(64):
                    sem = s_l0 if ho % 2 == 0 else s_l1
                    te.wait_ge(sem.sem, l_base[ho % 2] + 16 * (ho // 2 + 1))
                    for ci in range(4):
                        idx = ho * 4 + ci
                        if idx >= 4:
                            te.wait_ge(s_ev.sem, ev0 + idx - 3)
                        te.matmul(pf[ci % 4][:, 0:512], hm["h128"][:],
                                  cB[:, (ho % 2) * 2048 + ci * 512:(ho % 2) * 2048 + ci * 512 + 512],
                                  start=True, stop=True)
                        s_mm.inc(te.drain())

            @blk.vector
            def _(v):
                for ho in range(64):
                    for ci in range(4):
                        idx = ho * 4 + ci
                        v.wait_ge(s_mm.sem, mm0 + idx + 1)
                        if ho >= 2:
                            v.wait_ge(s_gs.sem, gs0 + 16 * (ho - 1))
                        v.tensor_copy(cE[:, (ho % 2) * 2048 + ci * 512:
                                         (ho % 2) * 2048 + ci * 512 + 512],
                                      pf[ci % 4][:, 0:512])
                        s_ev.inc(v.drain())

            @blk.gpsimd
            def _(g):
                for ho in range(64):
                    g.wait_ge(s_ev.sem, ev0 + ho * 4 + 4)
                    store(g, bass.AP(s2a_h, ho * 128 * TOWN, [[TOWN, 128], [1, TOWN]]),
                          cE[:, (ho % 2) * 2048:(ho % 2) * 2048 + 2048])

        stage2_quant(s2a_h, xq2t_h, 8192, 64, 2, "w2b", 1.0 / math.sqrt(8192.0))

    _PH("P5e")
    if upto >= 14:
        with nc.Block() as blk:
            ev0 = s_ev.n
            l_base = [s_l0.n, s_l1.n]
            gs0 = s_gs.n

            @blk.sync
            def _(sync):
                for j in range(64):
                    if j >= 2:
                        sync.wait_ge(s_ev.sem, ev0 + j - 1)
                    sem = s_l0 if j % 2 == 0 else s_l1
                    sem.dma(sync.dma_start(
                        cC[:, (j % 2) * 2048:(j % 2) * 2048 + 2048],
                        bass.AP(wdT, j * 128 * D, [[D, 128], [1, D]])))

            @blk.vector
            def _(v):
                for j in range(64):
                    sem = s_l0 if j % 2 == 0 else s_l1
                    v.wait_ge(sem.sem, l_base[j % 2] + 16 * (j // 2 + 1))
                    wsrc = cC[:, (j % 2) * 2048:(j % 2) * 2048 + 2048]
                    v.tensor_tensor(cD[:, 0:2048], wsrc, _bc(wsc, 16, 8, 2048), OP.is_gt)
                    v.tensor_tensor(cD[:, 2048:4096], wsrc, _bc(wsc, 16, 11, 2048), OP.is_lt)
                    v.drain()
                    v.tensor_sub(cD[:, 0:2048], cD[:, 0:2048], cD[:, 2048:4096])
                    v.drain()
                    if j >= 2:
                        v.wait_ge(s_gs.sem, gs0 + 16 * (j - 1))
                    v.tensor_tensor(cE[:, (j % 2) * 2048:(j % 2) * 2048 + 2048],
                                    cD[:, 0:2048], _bc(wsc, 16, 5, 2048), OP.mult)
                    s_ev.inc(v.drain())

            @blk.gpsimd
            def _(g):
                for j in range(64):
                    g.wait_ge(s_ev.sem, ev0 + j + 1)
                    store(g, bass.AP(wdq_h, j * 128,
                                     [[8192, 128], [1048576, 16], [1, 128]]),
                          cE[:, (j % 2) * 2048:(j % 2) * 2048 + 2048])

    _PH("P6")
    if upto >= 15:
        for c in range(4):
            with nc.Block() as blk:
                mm0, ev0 = s_mm.n, s_ev.n
                l_base = [s_l0.n, s_l1.n]
                gw_base = [s_gw0.n, s_gw1.n]
                gs0 = s_gs.n
                lx_mark = s_lx.n + 64 * 16

                @blk.sync
                def _(sync):
                    for g in range(64):
                        s_lx.dma(sync.dma_start_transpose(
                            bigR[:, g * 512:g * 512 + 512],
                            bass.AP(xq2t_h, c * 512 * DI + g * 128, [[DI, 512], [1, 128]])))
                    for d in range(16):
                        if d >= 2:
                            sync.wait_ge(s_ev.sem, ev0 + (d - 1) * 6)
                        sem = s_l0 if d % 2 == 0 else s_l1
                        sem.dma(sync.dma_start(
                            cB[:, (d % 2) * 512:(d % 2) * 512 + 512],
                            bass.AP(attn_h, d * 128 * TOWN + c * 512,
                                    [[TOWN, 128], [1, 512]])))
                        for tt in range(4):
                            sem.dma(sync.dma_start(
                                cD[:, 4096 + (d % 2) * 1024 + tt * 128:
                                     4096 + (d % 2) * 1024 + tt * 128 + 128],
                                bass.AP(x_own, (c * 512 + tt * 128) * D + d * 128,
                                        [[D, 128], [1, 128]])))

                @blk.gpsimd
                def _(g):
                    for d in range(16):
                        if d >= 2:
                            g.wait_ge(s_mm.sem, mm0 + (d - 1) * 5)
                        dst = cA if d % 2 == 0 else cE
                        gsem = s_gw0 if d % 2 == 0 else s_gw1
                        gsem.dma(g.dma_start(
                            bass.AP(dst, 0, [[8192, 128], [1, 8192]]),
                            bass.AP(wdq_h, d * 1048576, [[8192, 128], [1, 8192]])))
                    g.wait_ge(s_ev.sem, ev0 + 15 * 6 + 6)
                    for tt in range(4):
                        stage = cC if tt < 2 else cD
                        store(g, bass.AP(out_d, (c * 512 + tt * 128) * D,
                                         [[D, 128], [1, D]]),
                              stage[:, (tt % 2) * 2048:(tt % 2) * 2048 + 2048])

                @blk.tensor
                def _(te):
                    te.wait_ge(s_lx.sem, lx_mark)
                    for d in range(16):
                        gsem = s_gw0 if d % 2 == 0 else s_gw1
                        te.wait_ge(gsem.sem, gw_base[d % 2] + 16 * (d // 2 + 1))
                        bank = pf[d % 2]
                        if d >= 2:
                            te.wait_ge(s_ev.sem, ev0 + (d - 1) * 6 + 2)
                        src = cA if d % 2 == 0 else cE
                        for g in range(64):
                            te.matmul(bank[:, 0:512], src[:, g * 128:g * 128 + 128],
                                      bigR[:, g * 512:g * 512 + 512],
                                      start=(g == 0), stop=(g == 63))
                        s_mm.inc(te.drain())
                        te.wait_ge(s_ev.sem, ev0 + d * 6 + 2)
                        for tt in range(4):
                            if d * 6 + 2 + tt - 2 > 0 and tt >= 2:
                                te.wait_ge(s_ev.sem, ev0 + d * 6 + 2 + tt - 1)
                            te.transpose(
                                bass.AP(pf[2 + tt % 2], 0, [[512, 128], [1, 128]]),
                                cF[:, (d % 2) * 512 + tt * 128:(d % 2) * 512 + tt * 128 + 128]
                                if False else
                                cC[:, 4096 + (d % 2) * 512 + tt * 128:
                                     4096 + (d % 2) * 512 + tt * 128 + 128],
                                identf_s[:])
                            s_mm.inc(te.drain())

                @blk.vector
                def _(v):
                    for d in range(16):
                        v.wait_ge(s_mm.sem, mm0 + d * 5 + 1)
                        sem = s_l0 if d % 2 == 0 else s_l1
                        v.wait_ge(sem.sem, l_base[d % 2] + 80 * (d // 2 + 1))
                        v.tensor_add(cC[:, 4096 + (d % 2) * 512:4096 + (d % 2) * 512 + 512],
                                     pf[d % 2][:, 0:512],
                                     cB[:, (d % 2) * 512:(d % 2) * 512 + 512])
                        s_ev.inc(v.drain())
                        s_ev.inc(v.drain())
                        for tt in range(4):
                            v.wait_ge(s_mm.sem, mm0 + d * 5 + 1 + tt + 1)
                            stage = cC if tt < 2 else cD
                            scol = (tt % 2) * 2048 + d * 128
                            v.tensor_add(
                                stage[:, scol:scol + 128],
                                pf[2 + tt % 2][:, 0:128],
                                cD[:, 4096 + (d % 2) * 1024 + tt * 128:
                                     4096 + (d % 2) * 1024 + tt * 128 + 128])
                            s_ev.inc(v.drain())

        with nc.Block() as blk:
            @blk.gpsimd
            def _(g):
                g.wait_ge(s_gs.sem, s_gs.n)
                g.wait_ge(s_gw0.sem, s_gw0.n)
                g.wait_ge(s_gw1.sem, s_gw1.n)

            @blk.sync
            def _(sync):
                sync.wait_ge(s_lx.sem, s_lx.n)
                sync.wait_ge(s_l0.sem, s_l0.n)
                sync.wait_ge(s_l1.sem, s_l1.n)

    _PH(None)
    esC.close()
    es.close()
    return nc


def kernel(L0_attn_u, L0_attn_Wq, L0_attn_bq, L0_attn_Wk, L0_attn_bk,
           L0_attn_Wv, L0_attn_bv, L0_attn_convq_w, L0_attn_convq_b,
           L0_attn_convk_w, L0_attn_convk_b, L0_attn_ssm_kernel,
           L0_attn_D_skip, L0_attn_Wo, L0_attn_bo,
           L0_ffn_w_gate, L0_ffn_w_up, L0_ffn_w_down):
    x = np.ascontiguousarray(np.asarray(L0_attn_u, dtype=np.float32))
    f32 = lambda a: np.ascontiguousarray(np.asarray(a, dtype=np.float32))
    C = host_constants()

    chvec = np.zeros((D, 8), dtype=np.float32)
    chvec[:, 0] = f32(L0_attn_bq)
    chvec[:, 1] = f32(L0_attn_bk)
    chvec[:, 2] = f32(L0_attn_bv)
    chvec[:, 3] = f32(L0_attn_bo)
    chvec[:, 4] = f32(L0_attn_convq_b)
    chvec[:, 5] = f32(L0_attn_convk_b)
    chvec[:, 6] = f32(L0_attn_D_skip)
    cwvec = np.zeros((D, 8), dtype=np.float32)
    cwvec[:, 0:3] = f32(L0_attn_convq_w)[:, 0, :]
    cwvec[:, 3:6] = f32(L0_attn_convk_w)[:, 0, :]

    shared = {
        "wqT": _tile16(f32(L0_attn_Wq).T),
        "wkT": _tile16(f32(L0_attn_Wk).T),
        "wvT": _tile16(f32(L0_attn_Wv).T),
        "woT": _tile16(f32(L0_attn_Wo).T),
        "chvec": chvec, "cwvec": cwvec,
        "ssmk": f32(L0_attn_ssm_kernel),
        "wdT": np.ascontiguousarray(f32(L0_ffn_w_down).T[PERM2, :]),
    }
    shared["wgT"] = _tile16(np.ascontiguousarray(f32(L0_ffn_w_gate).T[PERM1, :]))
    shared["wuT"] = _tile16(np.ascontiguousarray(f32(L0_ffn_w_up).T[PERM1, :]))
    shared.update(C)

    nc = build_program()
    in_maps = []
    for core in range(8):
        b, h = core // 2, core % 2
        m = dict(shared)
        m["x_own"] = np.ascontiguousarray(x[b, h * 2048:(h + 1) * 2048])
        m["x_prev"] = np.ascontiguousarray(x[b, (1 - h) * 2048:(2 - h) * 2048])
        m["alpha"] = np.full((128, 1), float(h), dtype=np.float32)
        in_maps.append(m)

    res = run_bass_kernel_spmd(nc, in_maps, core_ids=list(range(8)))
    out = np.empty((4, L, D), dtype=np.float32)
    for core in range(8):
        b, h = core // 2, core % 2
        out[b, h * 2048:(h + 1) * 2048] = res.results[core]["out"]
    return out
